# version 21
# baseline (speedup 1.0000x reference)
import numpy as np
from contextlib import ExitStack

import concourse.bass as bass
import concourse.mybir as mybir
from concourse.bass_utils import run_bass_kernel_spmd

F32 = mybir.dt.float32
BF16 = mybir.dt.bfloat16
AF = mybir.ActivationFunctionType
ALU = mybir.AluOpType
AX = mybir.AxisListType

D = 1024
NCH = 8
H = 8
DH = 128
CH = 64
FF = 4096
NJ = 32
LN_EPS = 1e-5
A_EPS = 1e-5
RMS_EPS = 1e-5
LN09 = float(np.log(0.9))

ENGS = ("t", "v", "s", "p", "y")


class Prog:

    def __init__(self, nc, es):
        self.nc = nc
        self.es = es
        self.ops = {e: [] for e in ENGS}
        self.seq = {e: 0 for e in ENGS}
        self.waited = {e: set() for e in ENGS}
        self.dma_cnt = {}
        self.dma_h = {}
        self.eng_h = {}
        self.reg = {}
        for e in ENGS:
            self.eng_h[e] = es.enter_context(nc.semaphore(f"se_{e}"))

    def dma_sem(self, name):
        if name not in self.dma_h:
            self.dma_h[name] = self.es.enter_context(self.nc.semaphore(f"sd_{name}"))
            self.dma_cnt[name] = 0
        return self.dma_h[name]

    def _dep(self, name):
        if name not in self.reg:
            self.reg[name] = {"w": None, "r": []}
        return self.reg[name]

    def _collect(self, reads, writes):
        waits = []
        for r in reads:
            d = self._dep(r)
            if d["w"] is not None:
                waits.append(d["w"])
        for w in writes:
            d = self._dep(w)
            if d["w"] is not None:
                waits.append(d["w"])
            waits.extend(d["r"])
        best = {}
        for kind, key, val in waits:
            k = (kind, key)
            if k not in best or best[k] < val:
                best[k] = val
        return [(kind, key, val) for (kind, key), val in best.items()]

    def _record(self, tok, reads, writes):
        for r in reads:
            self._dep(r)["r"].append(tok)
        for w in writes:
            self.reg[w] = {"w": tok, "r": []}

    def op(self, eng, fn, reads=(), writes=(), extra=()):
        waits = self._collect(reads, writes) + list(extra)
        my = []
        for kind, key, val in waits:
            if kind == "eng":
                if key == eng:
                    self.waited[key].add(val)
                else:
                    self.waited[key].add(val)
            my.append((kind, key, val))
        self.seq[eng] += 1
        seq = self.seq[eng]
        self.ops[eng].append({"fn": fn, "waits": my, "seq": seq})
        self._record(("eng", eng, seq), reads, writes)
        return seq

    def dma(self, sem_name, fn, reads=(), writes=()):
        self.dma_sem(sem_name)
        prev = self.dma_cnt[sem_name]
        waits = self._collect(reads, writes)
        if prev > 0:
            waits.append(("dma", sem_name, prev))
        for kind, key, val in waits:
            if kind == "eng":
                self.waited[key].add(val)
        self.dma_cnt[sem_name] = prev + 16
        val = self.dma_cnt[sem_name]
        self.seq["y"] += 1
        self.ops["y"].append(
            {"fn": fn, "waits": waits, "seq": None, "dma": (sem_name, val)}
        )
        self._record(("dma", sem_name, val), reads, writes)
        return val

    def final_waits(self, eng="y"):
        def mk(name, val):
            def run(e, name=name, val=val):
                e.wait_ge(self.dma_h[name], val)
            return run
        for name, val in self.dma_cnt.items():
            if val > 0:
                self.ops[eng].append({"fn": mk(name, val), "waits": (), "seq": None,
                                      "raw": True})

    def finalize(self):
        remap = {}
        for e in ENGS:
            wl = sorted(self.waited[e])
            remap[e] = {v: i + 1 for i, v in enumerate(wl)}
        self.real_inc = {e: set(self.waited[e]) for e in ENGS}
        self.remap = remap

    def run_engine(self, e, handle):
        already = {}
        for rec in self.ops[e]:
            if rec.get("raw"):
                rec["fn"](handle)
                continue
            for kind, key, val in rec["waits"]:
                if kind == "eng":
                    rv = self.remap[key].get(val)
                    if rv is None:
                        continue
                    k = ("e", key)
                    if already.get(k, 0) >= rv:
                        continue
                    already[k] = rv
                    handle.wait_ge(self.eng_h[key], rv)
                else:
                    k = ("d", key)
                    if already.get(k, 0) >= val:
                        continue
                    already[k] = val
                    handle.wait_ge(self.dma_h[key], val)
            inst = rec["fn"](handle)
            if rec.get("dma"):
                name, val = rec["dma"]
                inst.then_inc(self.dma_h[name], 16)
            elif rec["seq"] in self.real_inc[e]:
                inst.then_inc(self.eng_h[e], 1)


def ceil_div(a, b):
    return -(-a // b)


def build_nc(S=4096, phases=("attn_a", "ffn1", "attn_b", "ffn2")):
    nc = bass.Bass("TRN2", target_bir_lowering=False)
    NS = S // 512
    NS2 = S // 256
    NCHK = S // CH
    NTC = S // 128

    dr = {}
    def din(name, shape):
        dr[name] = nc.dram_tensor(name, shape, F32, kind="ExternalInput").ap()
    def dinb(name, shape):
        dr[name] = nc.dram_tensor(name, shape, BF16, kind="ExternalInput").ap()

    din("xT", [D, S])
    for w in ("aq", "ak", "av", "ao"):
        din(w, [D, D])
    din("anw", [128, 1])
    din("w1", [D, FF]); din("w2", [FF, D])
    din("b1c", [128, NJ]); din("b2c", [128, NCH])
    for w in ("bq", "bk", "bv", "bo"):
        din(w, [D, D])
    din("bvecs", [128, 96])
    din("boc", [128, NCH])
    dinb("onesall", [128, 128])
    din("onesf32", [128, 128])
    dinb("maskt", [128, 512])
    din("cvals", [128, 8])
    din("onef", [1, 1])
    out = nc.dram_tensor("outT", [D, S], F32, kind="ExternalOutput").ap()
    import os as _os
    DBG = bool(_os.environ.get("KDBG"))
    if DBG:
        dbg1 = nc.dram_tensor("dbg1", [128, S], F32, kind="ExternalOutput").ap()
        dbg2 = nc.dram_tensor("dbg2", [128, NTC], F32, kind="ExternalOutput").ap()
        dbg3 = nc.dram_tensor("dbg3", [128, NTC * 1024], BF16, kind="ExternalOutput").ap()
    xB = nc.dram_tensor("xBbuf", [D, S], F32, kind="Internal").ap()
    wcA = nc.dram_tensor("wcA", [128, 32768], BF16, kind="Internal").ap()
    wcB = nc.dram_tensor("wcB", [128, 32768], BF16, kind="Internal").ap()
    wcC = nc.dram_tensor("wcC", [128, 32768], BF16, kind="Internal").ap()
    xC = nc.dram_tensor("xCbuf", [D, S], F32, kind="Internal").ap()

    es = ExitStack()
    AA = es.enter_context(nc.sbuf_tensor("AA", [128, 32768], BF16))
    AB = es.enter_context(nc.sbuf_tensor("AB", [128, 32768], BF16))
    XB0 = es.enter_context(nc.sbuf_tensor("XB0", [128, 4096], F32))
    XB1 = es.enter_context(nc.sbuf_tensor("XB1", [128, 4096], F32))
    LNB = es.enter_context(nc.sbuf_tensor("LNB", [128, 4096], BF16))
    BIG = es.enter_context(nc.sbuf_tensor("BIG", [128, 4096], F32))
    SQ = es.enter_context(nc.sbuf_tensor("SQ", [128, 4096], F32))
    STG = es.enter_context(nc.sbuf_tensor("STG", [128, 1024], F32))
    CON = es.enter_context(nc.sbuf_tensor("CON", [128, 128], BF16))
    CONF = es.enter_context(nc.sbuf_tensor("CONF", [128, 128], F32))
    MSK = es.enter_context(nc.sbuf_tensor("MSK", [128, 512], BF16))
    CV = es.enter_context(nc.sbuf_tensor("CV", [128, 8], F32))
    NW = es.enter_context(nc.sbuf_tensor("NW", [128, 1], F32))
    B1C = es.enter_context(nc.sbuf_tensor("B1C", [128, NJ], F32))
    B2C = es.enter_context(nc.sbuf_tensor("B2C", [128, NCH], F32))
    BVE = es.enter_context(nc.sbuf_tensor("BVE", [128, 96], F32))
    BOC = es.enter_context(nc.sbuf_tensor("BOC", [128, NCH], F32))
    ALF = es.enter_context(nc.sbuf_tensor("ALF", [128, 64], F32))
    CAR = es.enter_context(nc.sbuf_tensor("CAR", [128, 8], F32))
    CARB = es.enter_context(nc.sbuf_tensor("CARB", [128, 8], BF16))
    K8 = es.enter_context(nc.sbuf_tensor("K8", [128, 8], F32))
    SKC = es.enter_context(nc.sbuf_tensor("SKC", [128, 32], F32))
    CSB = es.enter_context(nc.sbuf_tensor("CSB", [128, NCH], F32))
    ONEF = es.enter_context(nc.sbuf_tensor("ONEF", [1, 1], F32))
    PS = [es.enter_context(nc.psum_tensor(f"PS{i}", [128, 512], F32))
          for i in range(8)]

    BIGB = BIG.ap().bitcast(BF16)
    SQB = SQ.ap().bitcast(BF16)

    P = Prog(nc, es)

    def wp(reg, col, ncol=1024, dcs=range(8)):
        cp = col // ncol
        return tuple(f"{reg}@{dc}:{cp}" for dc in dcs)

    def MMg(psum_ap, pairs, reads, writes, starts=None):
        def fn(t, psum_ap=psum_ap, pairs=pairs):
            inst = None
            for (lh, rh, st, sp, oap) in pairs:
                inst = t.matmul(oap if oap is not None else psum_ap, lh, rh,
                                start=st, stop=sp)
            return inst
        return P.op("t", fn, reads=reads, writes=writes)

    P.dma("dc", lambda y: y.dma_start(CON[:, :], dr["onesall"]), writes=("c_ones",))
    P.dma("dc", lambda y: y.dma_start(CONF[:, :], dr["onesf32"]), writes=("c_onesf",))
    P.dma("dc", lambda y: y.dma_start(MSK[:, :], dr["maskt"]), writes=("c_mask",))
    P.dma("dc", lambda y: y.dma_start(CV[:, :], dr["cvals"]), writes=("c_cv",))
    P.dma("dc", lambda y: y.dma_start(NW[:, :], dr["anw"]), writes=("c_nw",))
    P.dma("dc", lambda y: y.dma_start(B1C[:, :], dr["b1c"]), writes=("c_b1",))
    P.dma("dc", lambda y: y.dma_start(B2C[:, :], dr["b2c"]), writes=("c_b2",))
    P.dma("dc", lambda y: y.dma_start(BVE[:, :], dr["bvecs"]), writes=("c_bv",))
    P.dma("dc", lambda y: y.dma_start(BOC[:, :], dr["boc"]), writes=("c_boc",))
    P.dma("dc", lambda y: y.dma_start(ONEF[:, :], dr["onef"]), writes=("c_onef",))

    ARENA_REGS = (
        ["qfm", "kfm", "ktm", "vtm", "on", "stateB0", "stateB1", "VW", "VW#"]
        + [f"wAq@{r}:{c}" for r in range(8) for c in range(4)]
        + [f"wAk@{r}:0" for r in range(8)]
        + [f"wAv@{r}:0" for r in range(8)]
        + [f"wAo@{r}:0" for r in range(8)]
        + [f"wB@{r}:0" for r in range(32)]
        + ["wps0", "wps1", "wcb0", "wcb1"]
    )

    ABF_T = AB.ap().bitcast(F32)

    def wprep_piece(dname, rc, cache, coff, k):
        slot = k % 2
        stg = ABF_T[:, 12288 + slot * 1024: 13312 + slot * 1024]
        bnc = AB[:, 22528 + slot * 1024: 23552 + slot * 1024]
        sreg, breg = f"wps{slot}", f"wcb{slot}"
        src = dr[dname][rc * 128:(rc + 1) * 128, :]
        P.dma(f"dwp{slot}", (lambda y, d=stg, so=src: y.dma_start(d, so)),
              reads=(), writes=(sreg,))
        P.op("p", (lambda p, d=bnc, so=stg: p.tensor_copy(d, so)),
             reads=(sreg,), writes=(breg,))
        dst = cache[:, coff + rc * 1024: coff + (rc + 1) * 1024]
        P.dma(f"dwo{slot}", (lambda y, d=dst, so=bnc: y.dma_start(d, so)),
              reads=(breg,), writes=(f"wc:{dname}",))

    def load_weights(src_list, slot1=None, ceng="v"):
        P.op("v", (lambda v: v.memset(K8[0:1, 0:1], 0.0)),
             reads=(), writes=tuple(ARENA_REGS))
        si = 0
        AL1 = ("lnm2", "ln_mb", "ln_rs", "x2a", "x2b", "ln_tmp",
               "qsig0", "qsig1", "at0", "at1", "u0")
        if slot1 is None:
            s1_ap, s1_al, s1_tail = BIG, AL1, (lambda dn, rc, cp: False)
        else:
            s1_ap, s1_al, s1_tail = slot1
        for (dname, nrow, ncol, reg, dtile, coff) in src_list:
            npc = ceil_div(ncol, 1024)
            pc = ncol // npc
            for cp in range(npc):
                for rc in range(nrow):
                    slot = si % 2
                    si += 1
                    if s1_tail(dname, rc, cp):
                        slot = 0
                    stg_ap = (STG[:, 0:pc], s1_ap[:, 0:pc])[slot]
                    sreg = f"stg{slot}"
                    al = (() if slot == 0 else tuple(s1_al))
                    src = dr[dname][rc * 128:(rc + 1) * 128,
                                    cp * pc:(cp + 1) * pc]
                    P.dma(f"dw{slot}",
                          (lambda y, d=stg_ap, s=src: y.dma_start(d, s)),
                          reads=(), writes=(sreg,) + al)
                    off = coff + rc * ncol + cp * pc
                    dst = dtile[:, off: off + pc]
                    P.op(ceng, (lambda e, d=dst, s=stg_ap: e.tensor_copy(d, s)),
                         reads=(sreg,) + al, writes=(f"{reg}@{rc}:{cp}",))


    def ln_pass(T, xap, xregs, lnap, lnreg, ps_m, ps_s, sb, mr_store=None,
                extra_war=(), mid=None, x2_bb=None, split_apply=False,
                math_war=(), apply_war=(), rsqrt_dve=False, cast_ping=False):
        pm, psx = f"PS{ps_m}", f"PS{ps_s}"
        if sb == "ffnpre":
            LNBF2 = LNB.ap().bitcast(F32)
            lnm2 = LNBF2[:, 1152:1152 + T]
            x2a = LNB[:, 2816:2816 + T]
            x2b = LNB[:, 2816 + T:2816 + 2 * T]
            bb = None
        else:
            lnm2 = BIG[:, sb:sb + T]
            bb = 2 * (sb + 3 * T) if x2_bb is None else x2_bb
            x2a = BIGB[:, bb:bb + T]
            x2b = BIGB[:, bb + T:bb + 2 * T]
        if bb is None:
            tmp = tmp2 = None
        elif x2_bb is None:
            tmp = BIGB[:, bb + 2 * T:bb + 3 * T]
            tmp2 = tmp
        else:
            tmp = BIGB[:, 3072:3072 + T]
            tmp2 = BIGB[:, 3072 + T:3072 + 2 * T]
        if mr_store is None:
            mb = BIG[:, sb + T:sb + 2 * T]
            rs = BIG[:, sb + 2 * T:sb + 3 * T]
            mbr, rsr = "ln_mb", "ln_rs"
        else:
            mb, rs, mbr, rsr = mr_store
        for c in range(NCH):
            xa = xap[:, c * T:(c + 1) * T]
            if cast_ping:
                la = x2a
                wl = ("x2a",)
                lrd = ("c_ones", "x2a")
            elif mr_store is None:
                la = lnap[:, c * T:(c + 1) * T]
                wl = (lnreg,) if c == 0 else (lnreg + "#",)
                lrd = ("c_ones", lnreg, lnreg + "#")
            else:
                la = LNB[:, 3328 + (c % 2) * T: 3328 + (c % 2) * T + T]
                wl = ("pxa",) if c % 2 == 0 else ("pxb",)
                lrd = ("c_ones",) + wl
            if c == 0:
                wl = wl + tuple(extra_war)
            P.op("v", (lambda v, d=la, so=xa: v.tensor_copy(d, so)),
                 reads=xregs, writes=wl)
            if cast_ping:
                x2, x2r = x2b, "x2b"
            else:
                x2 = x2a if c % 2 == 0 else x2b
                x2r = "x2a" if c % 2 == 0 else "x2b"
            P.op("s", (lambda s_, d=x2, so=xa: s_.activation(d, so, AF.Square)),
                 reads=xregs, writes=(x2r,) + (tuple(extra_war) if c == 0 else ()))
            P.op("t", (lambda t, o=PS[ps_m][:, 0:T], l=la, st=(c == 0),
                       sp=(c == NCH - 1):
                       t.matmul(o, CON[:, :], l, start=st, stop=sp)),
                 reads=lrd, writes=(pm,))
            P.op("t", (lambda t, o=PS[ps_s][:, 0:T], l=x2, st=(c == 0),
                       sp=(c == NCH - 1):
                       t.matmul(o, CON[:, :], l, start=st, stop=sp)),
                 reads=("c_ones", x2r), writes=(psx,))
        if mid is not None:
            mid()
        P.op("v", (lambda v, d=mb: v.tensor_scalar(d, PS[ps_m][:, 0:T],
                                                   1.0 / D, None, ALU.mult)),
             reads=(pm,), writes=(mbr,) + tuple(math_war))
        P.op("v", (lambda v, d=lnm2, m=mb: v.tensor_tensor(d, m, m, ALU.mult)),
             reads=(mbr,), writes=("lnm2",))
        P.op("v", (lambda v, d=lnm2, m=lnm2: v.scalar_tensor_tensor(
            d, PS[ps_s][:, 0:T], 1.0 / D, m, ALU.mult, ALU.subtract)),
             reads=(psx, "lnm2"), writes=("lnm2",))
        if not rsqrt_dve:
            P.op("s", (lambda s_, d=lnm2: s_.activation(d, d, AF.Ln, bias=CV[:, 1:2])),
                 reads=("lnm2", "c_cv"), writes=("lnm2",))
            P.op("s", (lambda s_, d=rs, so=lnm2: s_.activation(d, so, AF.Exp,
                                                               scale=-0.5)),
                 reads=("lnm2",), writes=(rsr,))
        else:
            I32 = mybir.dt.int32
            if sb == "ffnpre":
                LNBF3 = LNB.ap().bitcast(F32)
                yv = LNBF3[:, 1408:1408 + T]
                t1v = LNBF3[:, 1664:1664 + T]
            else:
                yv = BIG[:, 3584:3584 + T]
                t1v = BIG[:, 1536:1536 + T]
            yregs = ("x2a", "x2b")
            tregs = ("pxa", "pxb") if sb == "ffnpre" else ("ln_tmp", "ln_tmp2")
            yi = yv.bitcast(I32)
            li = lnm2.bitcast(I32)
            P.op("v", (lambda v, d=lnm2: v.tensor_scalar(d, d, LN_EPS, None,
                                                         ALU.add)),
                 reads=("lnm2",), writes=("lnm2",))
            P.op("v", (lambda v, d=yi, so=li: v.tensor_scalar(
                d, so, 1, None, ALU.arith_shift_right)),
                 reads=("lnm2",), writes=yregs)
            P.op("v", (lambda v, d=yi: v.tensor_scalar(d, d, 0x5f3759df, None,
                                                       ALU.subtract)),
                 reads=yregs, writes=yregs)
            P.op("v", (lambda v, d=yi: v.tensor_scalar(d, d, -1, None,
                                                       ALU.bitwise_xor)),
                 reads=yregs, writes=yregs)
            P.op("v", (lambda v, d=yi: v.tensor_scalar(d, d, 1, None, ALU.add)),
                 reads=yregs, writes=yregs)
            for it in range(2):
                last = it == 1
                P.op("v", (lambda v, d=t1v, y2=yv: v.tensor_tensor(
                    d, y2, y2, ALU.mult)),
                     reads=yregs, writes=tregs)
                P.op("v", (lambda v, d=t1v, l2=lnm2: v.tensor_tensor(
                    d, d, l2, ALU.mult)),
                     reads=tregs + ("lnm2",), writes=tregs)
                P.op("v", (lambda v, d=t1v: v.tensor_scalar(
                    d, d, -0.5, 1.5, ALU.mult, ALU.add)),
                     reads=tregs, writes=tregs)
                if last:
                    P.op("v", (lambda v, d=rs, y2=yv, t2=t1v: v.tensor_tensor(
                        d, y2, t2, ALU.mult)),
                         reads=yregs + tregs, writes=(rsr,))
                else:
                    P.op("v", (lambda v, d=yv, t2=t1v: v.tensor_tensor(
                        d, d, t2, ALU.mult)),
                         reads=yregs + tregs, writes=yregs)

        if mr_store is None:
            for c in range(NCH):
                eng = "p" if (split_apply and c >= NCH // 2) else "v"
                tm_ = tmp2 if eng == "p" else tmp
                tr_ = "ln_tmp2" if eng == "p" else "ln_tmp"
                aw = tuple(apply_war) if c in (0, NCH // 2) else ()
                xa = xap[:, c * T:(c + 1) * T]
                la = lnap[:, c * T:(c + 1) * T]
                P.op(eng, (lambda e, d=tm_, so=xa, m=mb:
                           e.tensor_tensor(d, so, m, ALU.subtract)),
                     reads=xregs + (mbr, tr_), writes=(tr_,) + aw)
                P.op(eng, (lambda e, d=la, t=tm_, r=rs:
                           e.tensor_tensor(d, t, r, ALU.mult)),
                     reads=(tr_, rsr, lnreg, lnreg + "#"), writes=(lnreg,))

    def x_load(buf_ap, dram_ap, s, T, slot_reg, sem):
        src = dram_ap.rearrange("(c p) t -> p c t", p=128)[:, :, s * T:(s + 1) * T]
        dst = buf_ap.rearrange("p (c t) -> p c t", c=NCH)
        wr = (slot_reg,) if isinstance(slot_reg, str) else tuple(slot_reg)
        return P.dma(sem, (lambda y, d=dst, so=src: y.dma_start(d, so)),
                     reads=(), writes=wr)

    def x_store(buf_ap, dram_ap, s, T, slot_reg, sem, hreg):
        dst = dram_ap.rearrange("(c p) t -> p c t", p=128)[:, :, s * T:(s + 1) * T]
        src = buf_ap.rearrange("p (c t) -> p c t", c=NCH)
        return P.dma(sem, (lambda y, d=dst, so=src: y.dma_start(d, so)),
                     reads=(slot_reg,), writes=(hreg,))

    def phase_attn_a(src_dram, src_regs, dst_dram, dst_pref):
        T = 512
        QF, KF, KT, VT, ON, STB = 0, 4096, 8192, 12288, 16384, 20480
        P.op("v", lambda v: v.memset(AB[:, STB:STB + 2048], 0.0), writes=("stateB0", "stateB1"))
        P.op("v", lambda v: v.memset(SQ[:, 0:1024], 0.0), writes=("statef",))
        P.op("v", lambda v: v.memset(CAR[:, :], 0.0), writes=("car32",))
        P.op("v", lambda v: v.memset(CARB[:, :], 0.0), writes=("carb",))

        nsl = S // T

        def sctx(s):
            sl = s % 2
            xb = (XB0 if sl == 0 else XB1).ap()
            return sl, xb, (f"xs{2*sl}", f"xs{2*sl+1}")

        def load_ln(s):
            sl, xb, xregs = sctx(s)
            x_load(xb, src_dram, s, T, xregs, f"dx{sl}")
            ln_pass(T, xb, xregs, LNB.ap(), "lnb", 6, 7, 0)

        load_ln(0)
        load_weights([("aq", 8, 1024, "wAq", AA, 0),
                      ("ak", 8, 1024, "wAk", AA, 8192),
                      ("av", 8, 1024, "wAv", AA, 16384),
                      ("ao", 8, 1024, "wAo", AA, 24576)])
        wprep_jobs = []
        for k in range(32):
            rc, cp = k // 4, k % 4
            wprep_jobs.append(("w1pc", (rc, cp)))
        for rc in range(32):
            wprep_jobs.append(("w2", rc))
        for wi, nm in enumerate(("bq", "bk", "bv", "bo")):
            for rc in range(8):
                wprep_jobs.append((nm, (wi, rc)))
        wprep_k = [0]

        def wprep_emit(n):
            for _ in range(n):
                if wprep_k[0] >= len(wprep_jobs):
                    return
                job = wprep_jobs[wprep_k[0]]
                k = wprep_k[0]
                wprep_k[0] += 1
                if job[0] == "w1pc":
                    rc, cp = job[1]
                    slot = k % 2
                    stg = ABF_T[:, 12288 + slot * 1024: 13312 + slot * 1024]
                    bnc = AB[:, 22528 + slot * 1024: 23552 + slot * 1024]
                    sreg, breg = f"wps{slot}", f"wcb{slot}"
                    src = dr["w1"][rc * 128:(rc + 1) * 128,
                                   cp * 1024:(cp + 1) * 1024]
                    P.dma(f"dwp{slot}", (lambda y, d=stg, so=src:
                                         y.dma_start(d, so)),
                          reads=(), writes=(sreg,))
                    P.op("p", (lambda p, d=bnc, so=stg: p.tensor_copy(d, so)),
                         reads=(sreg,), writes=(breg,))
                    dst = wcA[:, rc * 4096 + cp * 1024: rc * 4096 + (cp + 1) * 1024]
                    P.dma(f"dwo{slot}", (lambda y, d=dst, so=bnc:
                                         y.dma_start(d, so)),
                          reads=(breg,), writes=("wc:w1",))
                elif job[0] == "w2":
                    wprep_piece("w2", job[1], wcB, 0, k)
                else:
                    wi, rc = job[1]
                    wprep_piece(job[0], rc, wcC, wi * 8192, k)
        P.op("v", (lambda v: v.tensor_scalar(AA[:, 24576:32768], AA[:, 24576:32768],
                                             NW[:, 0:1], None, ALU.mult)),
             reads=wp("wAo", 0) + ("c_nw",), writes=wp("wAo", 0))
        for s in range(nsl):
            sl, xb, xregs = sctx(s)
            def proj(dst_off, woff, dreg, kind):
                if kind == "fm":
                    for ec in range(NCH):
                        pb = ec % 2
                        pairs = []
                        for dc in range(NCH):
                            lh = AA[:, woff + dc * 1024 + ec * 128:
                                    woff + dc * 1024 + ec * 128 + 128]
                            rh = LNB[:, dc * T:(dc + 1) * T]
                            pairs.append((lh, rh, dc == 0, dc == 7, PS[pb][:, 0:T]))
                        MMg(None, pairs, reads=wp(dreg, 0) + ("lnb", f"PS{pb}"),
                            writes=(f"PS{pb}",))
                        dst = AB[:, dst_off + ec * T: dst_off + (ec + 1) * T]
                        P.op("v", (lambda v, d=dst, so=PS[pb][:, 0:T]:
                                   v.tensor_copy(d, so)),
                             reads=(f"PS{pb}",),
                             writes=(("qfm",) if dst_off == QF else ("kfm",)))
                else:
                    for t4 in range(4):
                        for eh in range(2):
                            pb = (t4 * 2 + eh) % 2
                            pairs = []
                            for dc in range(NCH):
                                lh = LNB[:, dc * T + t4 * 128: dc * T + t4 * 128 + 128]
                                rh = AA[:, woff + dc * 1024 + eh * 512:
                                        woff + dc * 1024 + (eh + 1) * 512]
                                pairs.append((lh, rh, dc == 0, dc == 7,
                                              PS[pb][:, 0:512]))
                            MMg(None, pairs, reads=wp(dreg, 0) + ("lnb", f"PS{pb}"),
                                writes=(f"PS{pb}",))
                            dst = AB[:, dst_off + t4 * 1024 + eh * 512:
                                     dst_off + t4 * 1024 + (eh + 1) * 512]
                            P.op("v", (lambda v, d=dst, so=PS[pb][:, 0:512]:
                                       v.tensor_copy(d, so)),
                                 reads=(f"PS{pb}",),
                                 writes=(("ktm",) if dst_off == KT else ("vtm",)))
            proj(QF, 0, "wAq", "fm")
            proj(KF, 8192, "wAk", "fm")
            proj(KT, 8192, "wAk", "tm")
            proj(VT, 16384, "wAv", "tm")

            for pp in range(T // 128):
                g = s * (T // 128) + pp
                par = g % 2
                stb_new = STB + 1024 * par
                stb_old = STB + 1024 * (1 - par)
                sb_new, sb_old = f"stateB{par}", f"stateB{1 - par}"
                for hg in range(2):
                    pairs = []
                    for hi in range(4):
                        h = hg * 4 + hi
                        lh = AB[:, KF + h * T + pp * 128: KF + h * T + pp * 128 + 128]
                        rh = AB[:, QF + h * T + pp * 128: QF + h * T + pp * 128 + 128]
                        pairs.append((lh, rh, hi == 0, hi == 3,
                                      PS[hg][:, hi * 128:(hi + 1) * 128]))
                    MMg(None, pairs, reads=("kfm", "qfm", f"PS{hg}"),
                        writes=(f"PS{hg}",))
                pairs = []
                for h in range(H):
                    kt = AB[:, KT + pp * 1024 + h * 128: KT + pp * 1024 + (h + 1) * 128]
                    vt = AB[:, VT + pp * 1024 + h * 128: VT + pp * 1024 + (h + 1) * 128]
                    bank = 2 + h // 4
                    oap = PS[bank][:, (h % 4) * 128:(h % 4 + 1) * 128]
                    pairs.append((kt, vt, h % 4 == 0, h % 4 == 3, oap))
                MMg(None, pairs, reads=("ktm", "vtm", "PS2", "PS3"),
                    writes=("PS2", "PS3"))
                sms = []
                for hg in range(2):
                    smx = BIGB[:, 6144 + hg * 512: 6656 + hg * 512]
                    smr = f"sm{hg}"
                    sms.append((smx, smr))
                    P.op("v", (lambda v, d=smx, pbs=hg:
                               v.tensor_tensor(d, PS[pbs][:, 0:512], MSK[:, :],
                                               ALU.mult)),
                         reads=(f"PS{hg}", "c_mask"), writes=(smr,))
                P.op("v", (lambda v: v.tensor_tensor(SQ[:, 0:512], SQ[:, 0:512],
                                                     PS[2][:, 0:512], ALU.add)),
                     reads=("PS2", "statef"), writes=("statef",))
                P.op("v", (lambda v: v.tensor_tensor(SQ[:, 512:1024], SQ[:, 512:1024],
                                                     PS[3][:, 0:512], ALU.add)),
                     reads=("PS3", "statef"), writes=("statef",))
                P.op("v", (lambda v, d=stb_new: v.tensor_copy(
                    AB[:, d:d + 1024], SQ[:, 0:1024])),
                     reads=("statef",), writes=(sb_new,))
                for hg in range(2):
                    smx, smr = sms[hg]
                    pb_o = 4 + hg
                    pairs = []
                    for hi in range(4):
                        h = hg * 4 + hi
                        vsl = AB[:, VT + pp * 1024 + h * 128:
                                 VT + pp * 1024 + (h + 1) * 128]
                        smh = smx[:, hi * 128:(hi + 1) * 128]
                        oap = PS[pb_o][:, hi * 128:(hi + 1) * 128]
                        pairs.append((vsl, smh, hi == 0, False, oap))
                        stt = AB[:, stb_old + h * 128: stb_old + (h + 1) * 128]
                        qsl = AB[:, QF + h * T + pp * 128: QF + h * T + pp * 128 + 128]
                        pairs.append((stt, qsl, False, hi == 3, oap))
                    MMg(None, pairs, reads=("vtm", smr, sb_old, "qfm",
                                            f"PS{pb_o}"),
                        writes=(f"PS{pb_o}",))
                    pairs = []
                    for hi in range(4):
                        h = hg * 4 + hi
                        cb = CARB[:, h:h + 1].broadcast_to([128, 128])
                        qsl = AB[:, QF + h * T + pp * 128: QF + h * T + pp * 128 + 128]
                        oap = PS[6][:, hi * 128:(hi + 1) * 128]
                        pairs.append((cb, qsl, hi == 0, False, oap))
                        smh = smx[:, hi * 128:(hi + 1) * 128]
                        pairs.append((CON[:, :], smh, False, hi == 3, oap))
                    MMg(None, pairs, reads=("carb", "qfm", smr, "c_ones", "PS6"),
                        writes=("PS6",))
                    o2 = BIGB[:, 5632:6144]
                    P.op("s", (lambda s_, d=o2, pbo=pb_o:
                               s_.activation(d, PS[pbo][:, 0:512], AF.Square)),
                         reads=(f"PS{pb_o}",), writes=("o2",))
                    sbc = BIG[:, 2304:2816]
                    P.op("s", (lambda s_, d=sbc: s_.activation(
                        d, PS[6][:, 0:512], AF.Square)),
                         reads=("PS6",), writes=("sbc",))
                    pairs = []
                    for hi in range(4):
                        pairs.append((CON[:, :], o2[:, hi * 128:(hi + 1) * 128],
                                      hi == 0, hi == 3,
                                      PS[7][:, hi * 128:(hi + 1) * 128]))
                    MMg(None, pairs, reads=("o2", "c_ones", "PS7"), writes=("PS7",))
                    P.op("v", (lambda v, d=sbc: v.tensor_scalar(
                        d, d, 1.0, RMS_EPS, ALU.max, ALU.mult)),
                         reads=("sbc",), writes=("sbc",))
                    P.op("v", (lambda v, d=sbc: v.scalar_tensor_tensor(
                        d, PS[7][:, 0:512], 1.0 / DH, d, ALU.mult, ALU.add)),
                         reads=("PS7", "sbc"), writes=("sbc",))
                    P.op("s", (lambda s_, d=sbc: s_.activation(d, d, AF.Ln)),
                         reads=("sbc",), writes=("sbc",))
                    P.op("s", (lambda s_, d=sbc: s_.activation(d, d, AF.Exp,
                                                               scale=-0.5)),
                         reads=("sbc",), writes=("sbc",))
                    onv = AB[:, ON:ON + 4096].rearrange(
                        "p (h k t) -> p h k t", h=H, k=T // 128)[
                        :, hg * 4:(hg + 1) * 4, pp, :]
                    piv = PS[pb_o][:, 0:512].rearrange("p (h t) -> p h t", h=4)
                    sbv = sbc.rearrange("p (h t) -> p h t", h=4)
                    P.op("v", (lambda v, d=onv, a=piv, b=sbv:
                               v.tensor_tensor(d, a, b, ALU.mult)),
                         reads=(f"PS{pb_o}", "sbc"), writes=("on",))
                kfv = AB[:, KF:KF + 4096].rearrange("p (h t) -> p h t", h=H)[
                    :, :, pp * 128:(pp + 1) * 128]
                P.op("v", (lambda v, d=K8[:, :], so=kfv:
                           v.tensor_reduce(d, so, AX.X, ALU.add)),
                     reads=("kfm",), writes=("k8",))
                P.op("v", (lambda v: v.tensor_tensor(CAR[:, :], CAR[:, :],
                                                     K8[:, :], ALU.add)),
                     reads=("car32", "k8"), writes=("car32",))
                P.op("v", (lambda v: v.tensor_copy(CARB[:, :], CAR[:, :])),
                     reads=("car32",), writes=("carb",))
            wprep_emit(ceil_div(len(wprep_jobs), nsl))
            if s + 1 < nsl:
                load_ln(s + 1)
            for doc in range(NCH):
                pb = doc % 2
                pairs = []
                for ec in range(NCH):
                    lh = AA[:, 24576 + ec * 1024 + doc * 128:
                            24576 + ec * 1024 + doc * 128 + 128]
                    rh = AB[:, ON + ec * T: ON + (ec + 1) * T]
                    pairs.append((lh, rh, ec == 0, ec == 7, PS[pb][:, 0:T]))
                MMg(None, pairs, reads=wp("wAo", 0) + ("on", f"PS{pb}"), writes=(f"PS{pb}",))
                xv = xb[:, doc * T:(doc + 1) * T]
                P.op("v", (lambda v, d=xv, pbb=pb: v.tensor_tensor(
                    d, d, PS[pbb][:, 0:T], ALU.add)),
                     reads=(f"PS{pb}",) + xregs, writes=(xregs[0], xregs[1]))
            x_store(xb, dst_dram, s, T, xregs[0], f"ds{sl}", f"{dst_pref}:{2*s}")
            P.reg[f"{dst_pref}:{2*s+1}"] = dict(P.reg[f"{dst_pref}:{2*s}"])

    def phase_ffn(src_dram, src_pref, dst_dram, dst_pref, final=False, wmode='plain'):
        T = 256
        nsl = S // T
        def prepass(s):
            sl = s % 4
            xb = (XB0 if sl < 2 else XB1).ap()[
                :, (sl % 2) * 2048:(sl % 2) * 2048 + 2048]
            xreg = f"xs{sl}"
            src = src_dram.rearrange("(c p) t -> p c t", p=128)[
                :, :, s * T:(s + 1) * T]
            dst = xb.rearrange("p (c t) -> p c t", c=NCH)
            P.dma(f"dx{sl}", (lambda y, d=dst, so=src: y.dma_start(d, so)),
                  reads=(f"{src_pref}:{s}",) if src_pref else (),
                  writes=(xreg,))
            mb = SQB[:, s * T:(s + 1) * T]
            rs = SQB[:, 4096 + s * T: 4096 + (s + 1) * T]
            ln_pass(T, xb, (xreg,), LNB.ap()[:, 0:2048], "lnb", 4, 5, "ffnpre",
                    mr_store=(mb, rs, f"mrm{s}", f"mrr{s}"), rsqrt_dve=True)

        def fctx(s):
            sl = s % 4
            xb = (XB0 if sl < 2 else XB1).ap()[
                :, (sl % 2) * 2048:(sl % 2) * 2048 + 2048]
            return sl, xb, f"xs{sl}"

        def fpre(s):
            sl, xb, xreg = fctx(s)
            mb = SQB[:, s * T:(s + 1) * T]
            rs = SQB[:, 4096 + s * T: 4096 + (s + 1) * T]
            for c in range(NCH):
                la = LNB[:, c * T:(c + 1) * T]
                P.op("v", (lambda v, d=la, so=xb[:, c * T:(c + 1) * T]:
                           v.tensor_copy(d, so)),
                     reads=(xreg,), writes=("lnb",) if c == 0 else ("lnb#",))
            for c in range(NCH):
                la = LNB[:, c * T:(c + 1) * T]
                P.op("v", (lambda v, d=la, m=mb: v.tensor_tensor(d, d, m, ALU.subtract)),
                     reads=("lnb", "lnb#", f"mrm{s}"), writes=("lnb",))
                P.op("v", (lambda v, d=la, r=rs: v.tensor_tensor(d, d, r, ALU.mult)),
                     reads=("lnb", f"mrr{s}"), writes=("lnb",))

        for _k in range(min(3, nsl)):
            prepass(_k)
        fpre(0)
        ABF = AB.ap().bitcast(F32)
        if wmode == "load":
            P.op("v", (lambda v: v.memset(K8[0:1, 0:1], 0.0)),
                 reads=(), writes=tuple(ARENA_REGS))
            for k in range(8):
                regs = tuple(f"wAq@{r}:{c}" for (r, c) in
                             [((k * 4 + i) // 4, (k * 4 + i) % 4) for i in range(4)])
                P.dma(f"dw{k % 2}", (lambda y, kk=k: y.dma_start(
                    AA[:, kk * 4096:(kk + 1) * 4096],
                    wcA[:, kk * 4096:(kk + 1) * 4096])),
                      reads=("wc:w1",), writes=regs)
            for k in range(8):
                regs = tuple(f"wB@{r}:0" for r in range(k * 4, k * 4 + 4))
                P.dma(f"dw{k % 2}", (lambda y, kk=k: y.dma_start(
                    AB[:, kk * 4096:(kk + 1) * 4096],
                    wcB[:, kk * 4096:(kk + 1) * 4096])),
                      reads=("wc:w2",), writes=regs)
        else:
            load_weights([("w1", 8, 4096, "wAq", AA, 0),
                          ("w2", 32, 1024, "wB", AB, 0)],
                         slot1=(ABF[:, 15360:16384], ("wB@30:0", "wB@31:0"),
                                (lambda dn, rc, cp: dn == "w2" and rc >= 30)))
            if wmode == "store":
                for k in range(8):
                    regs = tuple(f"wAq@{r}:{c}" for (r, c) in
                                 [((k * 4 + i) // 4, (k * 4 + i) % 4)
                                  for i in range(4)])
                    P.dma(f"dwc{k % 2}", (lambda y, kk=k: y.dma_start(
                        wcA[:, kk * 4096:(kk + 1) * 4096],
                        AA[:, kk * 4096:(kk + 1) * 4096])),
                          reads=regs, writes=("wcA",))
                for k in range(8):
                    regs = tuple(f"wB@{r}:0" for r in range(k * 4, k * 4 + 4))
                    P.dma(f"dwc{k % 2}", (lambda y, kk=k: y.dma_start(
                        wcB[:, kk * 4096:(kk + 1) * 4096],
                        AB[:, kk * 4096:(kk + 1) * 4096])),
                          reads=regs, writes=("wcB",))
        for s in range(nsl):
            sl, xb, xreg = fctx(s)
            for j in range(NJ):
                pb = j % 2
                pairs = []
                for dc in range(NCH):
                    lh = AA[:, dc * 4096 + j * 128: dc * 4096 + j * 128 + 128]
                    rh = LNB[:, dc * T:(dc + 1) * T]
                    pairs.append((lh, rh, dc == 0, dc == 7, PS[pb][:, 0:T]))
                MMg(None, pairs, reads=wp("wAq", j * 128, 1024) + ("lnb", f"PS{pb}"), writes=(f"PS{pb}",))
                hj = BIGB[:, j * T:(j + 1) * T]
                P.op("s", (lambda s_, d=hj, pbb=pb, jj=j: s_.activation(
                    d, PS[pbb][:, 0:T], AF.Gelu, bias=B1C[:, jj:jj + 1])),
                     reads=(f"PS{pb}", "c_b1"), writes=("hsb",) if j == 0 else ("hsb#",))
            if s + 1 < nsl:
                fpre(s + 1)
            if s + 3 < nsl:
                prepass(s + 3)
            for doc in range(NCH):
                pb = (2, 3, 6, 7)[doc % 4]
                pairs = []
                for j in range(NJ):
                    lh = AB[:, j * 1024 + doc * 128: j * 1024 + doc * 128 + 128]
                    rh = BIGB[:, j * T:(j + 1) * T]
                    pairs.append((lh, rh, j == 0, j == NJ - 1, PS[pb][:, 0:T]))
                MMg(None, pairs,
                    reads=tuple(f"wB@{r2}:0" for r2 in range(32)) +
                          ("hsb", "hsb#", f"PS{pb}"),
                    writes=(f"PS{pb}",))
                xv = xb[:, doc * T:(doc + 1) * T]
                P.op("v", (lambda v, d=xv, pbb=pb, dd=doc: v.scalar_tensor_tensor(
                    d, PS[pbb][:, 0:T], B2C[:, dd:dd + 1], d, ALU.add, ALU.add)),
                     reads=(f"PS{pb}", "c_b2", xreg), writes=(xreg,))
            dstd = dst_dram.rearrange("(c p) t -> p c t", p=128)[:, :, s * T:(s + 1) * T]
            srcs = xb.rearrange("p (c t) -> p c t", c=NCH)
            P.dma(f"ds{s % 2}", (lambda y, d=dstd, so=srcs: y.dma_start(d, so)),
                  reads=(xreg,), writes=(f"{dst_pref}:{s}",))

    def phase_attn_b(src_dram, src_pref, dst_dram, dst_pref, wmode='plain'):
        T = 512
        for i, base in enumerate((0, 32, 64)):
            g = BVE[:, base:base + 8]; b = BVE[:, base + 8:base + 16]
            m = BVE[:, base + 16:base + 24]; vv = BVE[:, base + 24:base + 32]
            al = ALF[:, i * 16:i * 16 + 8]; be = ALF[:, i * 16 + 8:i * 16 + 16]
            tmp = ALF[:, 48:56]
            P.op("s", (lambda s_, d=tmp, so=vv: s_.activation(d, so, AF.Ln,
                                                              bias=CV[:, 2:3])),
                 reads=("c_bv", "c_cv"), writes=("alf_t",))
            P.op("s", (lambda s_, d=tmp: s_.activation(d, d, AF.Exp, scale=-0.5)),
                 reads=("alf_t",), writes=("alf_t",))
            P.op("v", (lambda v, d=al, gg=g, t=tmp: v.scalar_tensor_tensor(
                d, gg, 4.0, t, ALU.mult, ALU.mult)),
                 reads=("c_bv", "alf_t"), writes=(f"alf{i}",))
            P.op("v", (lambda v, d=tmp, a=al, mm=m: v.tensor_tensor(d, a, mm, ALU.mult)),
                 reads=(f"alf{i}", "c_bv"), writes=("alf_t",))
            P.op("v", (lambda v, d=be, bb=b, t=tmp: v.scalar_tensor_tensor(
                d, bb, 4.0, t, ALU.mult, ALU.subtract)),
                 reads=("c_bv", "alf_t"), writes=(f"alf{i}b",))
            P.op("v", (lambda v, d=be: v.tensor_scalar(d, d, 4.0, None, ALU.subtract)),
                 reads=(f"alf{i}b",), writes=(f"alf{i}b",))
        nsl = S // T

        def bctx(s):
            sl = s % 2
            xb = (XB0 if sl == 0 else XB1).ap()
            return sl, xb, (f"xs{2*sl}", f"xs{2*sl+1}")

        def qk_sums(s):
            for (ii, woff, psb) in ((0, 0, 2), (1, 8192, 3)):
                wreg = "wAq" if ii == 0 else "wAk"
                for ec in range(NCH):
                    pb = ec % 2
                    pairs = []
                    for dc in range(NCH):
                        lh = AA[:, woff + dc * 1024 + ec * 128:
                                woff + dc * 1024 + ec * 128 + 128]
                        rh = LNB[:, dc * T:(dc + 1) * T]
                        pairs.append((lh, rh, dc == 0, dc == 7, PS[pb][:, 0:T]))
                    MMg(None, pairs, reads=wp(wreg, 0) + ("lnb", f"PS{pb}"),
                        writes=(f"PS{pb}",))
                    qs = BIG[:, (ec % 2) * 512:(ec % 2) * 512 + 512]
                    qreg = f"qsig{ec % 2}"
                    wl = (qreg,)
                    if ii == 0 and ec == 0:
                        wl = wl + ("vfm", "vfm#")
                    P.op("s", (lambda s_, d=qs, pbb=pb, i2=ii, e2=ec: s_.activation(
                        d, PS[pbb][:, 0:T], AF.Sigmoid,
                        bias=ALF[:, i2 * 16 + 8 + e2:i2 * 16 + 9 + e2],
                        scale=ALF[:, i2 * 16 + e2:i2 * 16 + 1 + e2])),
                         reads=(f"PS{pb}", f"alf{ii}", f"alf{ii}b"), writes=wl)
                    P.op("t", (lambda t, o=PS[psb][:, 0:T], q=qs, st=(ec == 0),
                               sp=(ec == NCH - 1):
                               t.matmul(o, CONF[:, :], q, start=st, stop=sp)),
                         reads=(qreg, "c_onesf", f"PS{psb}"), writes=(f"PS{psb}",))
            P.op("s", (lambda e, ss=s: e.copy(SQ[:, ss * T:(ss + 1) * T],
                                              PS[2][:, 0:T])),
                 reads=("PS2",), writes=("sq",) if s == 0 else ("sq#",))
            krow = BIG[0:1, 1024:1024 + T]
            P.op("s", (lambda s_, d=krow: s_.copy(d, PS[3][0:1, 0:T])),
                 reads=("PS3",), writes=("krow",))
            pairs = []
            for jseg in range(T // 128):
                pairs.append((krow[0:1, jseg * 128:(jseg + 1) * 128], ONEF[:, :],
                              jseg == 0, jseg == T // 128 - 1,
                              PS[3][:, jseg:jseg + 1]))
            MMg(None, pairs, reads=("krow", "c_onef", "PS3"), writes=("PS3",))
            P.op("s", (lambda e, ss=s: e.copy(
                SKC[:, ss * (T // 128):(ss + 1) * (T // 128)],
                PS[3][:, 0:T // 128])),
                 reads=("PS3",), writes=("skc",) if s == 0 else ("skc#",))

        def v_proj(s):
            for ec in range(NCH):
                pb = ec % 2
                pairs = []
                for dc in range(NCH):
                    lh = AA[:, 16384 + dc * 1024 + ec * 128:
                            16384 + dc * 1024 + ec * 128 + 128]
                    rh = LNB[:, dc * T:(dc + 1) * T]
                    pairs.append((lh, rh, dc == 0, dc == 7, PS[pb][:, 0:T]))
                MMg(None, pairs, reads=wp("wAv", 0) + ("lnb", f"PS{pb}"),
                    writes=(f"PS{pb}",))
                vf = BIGB[:, ec * 512:(ec + 1) * 512]
                wl = ("vfm",) if ec == 0 else ("vfm#",)
                if ec == 0:
                    wl = wl + ("qsig0", "qsig1", "krow", "lnm2", "ln_mb",
                               "ln_rs", "ln_tmp", "ln_tmp2")
                P.op("s", (lambda s_, d=vf, pbb=pb, e2=ec: s_.activation(
                    d, PS[pbb][:, 0:T], AF.Sigmoid,
                    bias=ALF[:, 40 + e2:41 + e2], scale=ALF[:, 32 + e2:33 + e2])),
                     reads=(f"PS{pb}", "alf2", "alf2b"), writes=wl)

        def vw_mm(s):
            for t4 in range(4):
                for doh in range(2):
                    pb = 6 + (t4 * 2 + doh) % 2
                    pairs = []
                    for ec in range(NCH):
                        lh = BIGB[:, ec * 512 + t4 * 128: ec * 512 + t4 * 128 + 128]
                        rh = AA[:, 24576 + ec * 1024 + doh * 512:
                                24576 + ec * 1024 + (doh + 1) * 512]
                        pairs.append((lh, rh, ec == 0, ec == 7, PS[pb][:, 0:512]))
                    MMg(None, pairs, reads=("vfm", "vfm#") + wp("wAo", 0) +
                        (f"PS{pb}",), writes=(f"PS{pb}",))
                    tc = s * 4 + t4
                    dstv = AB[:, tc * 1024 + doh * 512: tc * 1024 + (doh + 1) * 512]
                    P.op("s", (lambda e, d=dstv, pbb=pb: e.copy(
                        d, PS[pbb][:, 0:512])),
                         reads=(f"PS{pb}",), writes=("VW",) if (s == 0 and t4 == 0
                                                               and doh == 0)
                         else ("VW#",))

        def bload_ln(s, mid=None):
            sl, xb, xregs = bctx(s)
            src = src_dram.rearrange("(c p) t -> p c t", p=128)[
                :, :, s * T:(s + 1) * T]
            dst = xb.rearrange("p (c t) -> p c t", c=NCH)
            P.dma(f"dx{sl}", (lambda y, d=dst, so=src: y.dma_start(d, so)),
                  reads=(f"{src_pref}:{2*s}", f"{src_pref}:{2*s+1}"),
                  writes=(xregs[0], xregs[1]))
            ln_pass(T, xb, xregs, LNB.ap(), "lnb", 4, 5, 0,
                    mid=mid, x2_bb=7168, split_apply=False,
                    math_war=("qsig0", "qsig1", "krow", "vfm", "vfm#"),
                    apply_war=("vfm", "vfm#", "lnb", "lnb#"), rsqrt_dve=True,
                    cast_ping=True)

        bload_ln(0)
        if wmode == "load":
            P.op("v", (lambda v: v.memset(K8[0:1, 0:1], 0.0)),
                 reads=(), writes=tuple(ARENA_REGS))
            for wi, wreg in enumerate(("wAq", "wAk", "wAv", "wAo")):
                for half in range(2):
                    k = wi * 2 + half
                    regs = tuple(f"{wreg}@{r}:0" for r in
                                 range(half * 4, half * 4 + 4))
                    off = wi * 8192 + half * 4096
                    P.dma(f"dw{k % 2}", (lambda y, o=off: y.dma_start(
                        AA[:, o:o + 4096], wcC[:, o:o + 4096])),
                          reads=(f"wc:{('bq','bk','bv','bo')[wi]}",),
                          writes=regs)
        else:
            load_weights([("bq", 8, 1024, "wAq", AA, 0),
                          ("bk", 8, 1024, "wAk", AA, 8192),
                          ("bv", 8, 1024, "wAv", AA, 16384),
                          ("bo", 8, 1024, "wAo", AA, 24576)])
        for s in range(nsl):
            sl, xb, xregs = bctx(s)
            qk_sums(s)
            v_proj(s)
            if s + 1 < nsl:
                bload_ln(s + 1, mid=lambda s2=s: vw_mm(s2))
            else:
                vw_mm(s)
        pairs = []
        for doc in range(NCH):
            for jc in range(NTC):
                lh = AB[:, jc * 1024 + doc * 128: jc * 1024 + doc * 128 + 128]
                pairs.append((lh, CON[:, 0:1], (doc == 0 and jc == 0),
                              (doc == NCH - 1 and jc == NTC - 1),
                              PS[0][:, doc:doc + 1]))
        MMg(None, pairs, reads=("VW", "VW#", "c_ones", "PS0"), writes=("PS0",))
        P.op("v", (lambda v: v.scalar_tensor_tensor(
            CSB[:, :], PS[0][:, 0:NCH], 0.9, BOC[:, :], ALU.mult, ALU.add)),
             reads=("PS0", "c_boc"), writes=("csb",))
        nib = S // T
        for ib in range(nib):
            sl = ib % 2
            xb = (XB0 if sl == 0 else XB1).ap()
            xregs = (f"xs{2*sl}", f"xs{2*sl+1}")
            src = src_dram.rearrange("(c p) t -> p c t", p=128)[:, :, ib * T:(ib + 1) * T]
            dstx = xb.rearrange("p (c t) -> p c t", c=NCH)
            P.dma(f"dx{sl}", (lambda y, d=dstx, so=src: y.dma_start(d, so)),
                  reads=(f"{src_pref}:{2*ib}", f"{src_pref}:{2*ib+1}"),
                  writes=(xregs[0], xregs[1]))
            for jc in range(NTC):
                pu = jc % 2
                u = BIG[:, 512 + pu * 512: 1024 + pu * 512]
                ureg = f"u{pu}"
                f2 = BIG[:, 1536 + pu * 512: 2048 + pu * 512]
                f2reg = f"f2{pu}"
                mf = BIG[:, 2560:3072]
                at = BIGB[:, pu * 512:(pu + 1) * 512]
                atreg = f"at{pu}"
                sqv = SQ[:, ib * T:(ib + 1) * T]
                skcol = SKC[:, jc:jc + 1]
                du = BIG[:, 3072:3584] if pu == 0 else BIG[:, 3584:4096]
                dreg = f"du{pu}"
                P.op("v", (lambda v, d=du, sv=sqv, sk=skcol: v.tensor_scalar(
                    d, sv, sk, None, ALU.subtract)),
                     reads=("sq", "sq#", "skc", "skc#"), writes=(dreg,))
                P.op("s", (lambda s_, d=u, so=du: s_.activation(
                    d, so, AF.Abs, scale=2.0 / 1024.0)),
                     reads=(dreg,), writes=(ureg,))
                P.op("s", (lambda s_, d=f2, so=u: s_.activation(
                    d, so, AF.Exp, bias=CV[:, 0:1], scale=-1.0)),
                     reads=(ureg, "c_cv"), writes=(f2reg,))
                P.op("v", (lambda v, d=mf, sv=sqv, sk=skcol, f=f2:
                           v.scalar_tensor_tensor(d, sv, sk, f, ALU.is_gt, ALU.mult)),
                     reads=("sq", "sq#", "skc", "skc#", f2reg), writes=("mf",))
                P.op("v", (lambda v, d=at, m=mf, f=f2: v.scalar_tensor_tensor(
                    d, m, 2.0, f, ALU.mult, ALU.subtract)),
                     reads=("mf", f2reg), writes=(atreg,))
                pairs = []
                for doc in range(NCH):
                    lh = AB[:, jc * 1024 + doc * 128: jc * 1024 + doc * 128 + 128]
                    pairs.append((lh, at, jc == 0, jc == NTC - 1,
                                  PS[doc][:, 0:T]))
                MMg(None, pairs,
                    reads=("VW", "VW#", atreg) + tuple(f"PS{d2}" for d2 in range(8)),
                    writes=tuple(f"PS{d2}" for d2 in range(8)))
            for doc in range(NCH):
                xv = xb[:, doc * T:(doc + 1) * T]
                P.op("v", (lambda v, d=xv, dd=doc: v.scalar_tensor_tensor(
                    d, PS[dd][:, 0:T], CSB[:, dd:dd + 1], d, ALU.add, ALU.add)),
                     reads=(f"PS{doc}", "csb", xregs[0], xregs[1]),
                     writes=(xregs[0], xregs[1]))
            dstd = dst_dram.rearrange("(c p) t -> p c t", p=128)[:, :, ib * T:(ib + 1) * T]
            srcs = xb.rearrange("p (c t) -> p c t", c=NCH)
            P.dma(f"ds{sl}", (lambda y, d=dstd, so=srcs: y.dma_start(d, so)),
                  reads=(xregs[0],), writes=(f"{dst_pref}:{2*ib}",))
            P.reg[f"{dst_pref}:{2*ib+1}"] = dict(P.reg[f"{dst_pref}:{2*ib}"])

    chain = [p for p in ("attn_a", "ffn1", "attn_b", "ffn2") if p in phases]

    bufs = [xB, xC]
    cur_src, cur_pref = dr["xT"], None
    for i, name in enumerate(chain):
        last = i == len(chain) - 1
        dst = out if last else bufs[i % 2]
        dst_pref = "OUT" if last else ("hxB" if i % 2 == 0 else "hxC")
        has_a = "attn_a" in chain
        if name == "attn_a":
            phase_attn_a(cur_src, cur_pref, dst, dst_pref)
        elif name.startswith("ffn"):
            phase_ffn(cur_src, cur_pref, dst, dst_pref, final=last,
                      wmode="load" if has_a else "plain")
        else:
            phase_attn_b(cur_src, cur_pref, dst, dst_pref,
                         wmode="load" if has_a else "plain")
        cur_src, cur_pref = dst, dst_pref

    if DBG:
        P.dma("dbg", lambda y: y.dma_start(dbg1, SQ[:, 0:S]), reads=("sq", "sq#"))
        P.dma("dbg", lambda y: y.dma_start(dbg2, SKC[:, 0:NTC]), reads=("skc", "skc#"))
        P.dma("dbg", lambda y: y.dma_start(dbg3, AB[:, 0:NTC * 1024]), reads=("VW", "VW#"))
    P.final_waits("y")
    P.finalize()

    with nc.Block() as block:
        @block.tensor
        def _(t):
            P.run_engine("t", t)

        @block.vector
        def _(v):
            P.run_engine("v", v)

        @block.scalar
        def _(s):
            P.run_engine("s", s)

        @block.gpsimd
        def _(p):
            P.run_engine("p", p)

        @block.sync
        def _(y):
            P.run_engine("y", y)
    es.close()
    return nc


_NCACHE = {}


def _get_nc(S, phases):
    key = (S, tuple(phases))
    if key not in _NCACHE:
        _NCACHE[key] = build_nc(S, phases)
    return _NCACHE[key]


def _vec_cols(v, n):
    return np.ascontiguousarray(v.reshape(n, 128).T).astype(np.float32)


PHASES = ("attn_a", "ffn1", "attn_b", "ffn2")
RUNNER = None


def kernel(**inputs):
    import ml_dtypes
    phases = PHASES
    x = np.asarray(inputs["L0_attn_a_hidden_states"], dtype=np.float32)
    B = x.shape[0]
    S = x.shape[1]
    nc = _get_nc(S, phases)

    aq = np.ascontiguousarray(np.asarray(inputs["L0_attn_a_Wq"], np.float32).T)
    ak = np.ascontiguousarray(np.asarray(inputs["L0_attn_a_Wk"], np.float32).T)
    av = np.ascontiguousarray(np.asarray(inputs["L0_attn_a_Wv"], np.float32).T)
    ao = np.ascontiguousarray(np.asarray(inputs["L0_attn_a_Wo"], np.float32).T)
    anw = np.ascontiguousarray(
        np.asarray(inputs["L0_attn_a_norm_w"], np.float32).reshape(128, 1))
    w1 = np.ascontiguousarray(np.asarray(inputs["L0_ffn_w1"], np.float32))
    w2 = np.ascontiguousarray(np.asarray(inputs["L0_ffn_w2"], np.float32))
    b1c = _vec_cols(np.asarray(inputs["L0_ffn_b1"], np.float32), NJ)
    b2c = _vec_cols(np.asarray(inputs["L0_ffn_b2"], np.float32), NCH)
    bq = np.ascontiguousarray(np.asarray(inputs["L0_attn_b_Wq"], np.float32))
    bk = np.ascontiguousarray(np.asarray(inputs["L0_attn_b_Wk"], np.float32))
    bv = np.ascontiguousarray(np.asarray(inputs["L0_attn_b_Wv"], np.float32))
    bo = np.ascontiguousarray(np.asarray(inputs["L0_attn_b_Wo"], np.float32))
    bvecs = np.concatenate(
        [_vec_cols(np.asarray(inputs[f"L0_attn_b_{nm}"], np.float32), 8)
         for nm in ("gq", "bq", "mq", "vq", "gk", "bk", "mk", "vk",
                    "gv", "bv", "mv", "vv")], axis=1)
    boc = _vec_cols(np.asarray(inputs["L0_attn_b_bo"], np.float32), NCH)
    onesall = np.ones((128, 128), dtype=ml_dtypes.bfloat16)
    tk = np.arange(128)[:, None]
    tq = np.arange(128)[None, :]
    m128 = ((tk // 64 < tq // 64) | ((tk // 64 == tq // 64) & (tk <= tq))
            ).astype(np.float32)
    maskt = np.tile(m128, (1, 4)).astype(ml_dtypes.bfloat16)
    cvals = np.tile(np.array([[LN09, LN_EPS, A_EPS, RMS_EPS, 0, 0, 0, 0]],
                             np.float32), (128, 1))
    onef = np.ones((1, 1), np.float32)

    shared = dict(aq=aq, ak=ak, av=av, ao=ao, anw=anw, w1=w1, w2=w2, b1c=b1c,
                  b2c=b2c, bq=bq, bk=bk, bv=bv, bo=bo, bvecs=bvecs, boc=boc,
                  onesall=onesall, maskt=maskt, cvals=cvals, onef=onef,
                  onesf32=np.ones((128, 128), np.float32))
    in_maps = []
    for b in range(B):
        m = dict(shared)
        m["xT"] = np.ascontiguousarray(x[b].T)
        in_maps.append(m)
    runner = RUNNER if RUNNER is not None else run_bass_kernel_spmd
    res = runner(nc, in_maps, core_ids=list(range(B)))
    outs = [np.asarray(r["outT"]).T for r in res.results]
    return np.stack(outs, axis=0).astype(np.float32)


# version 22
# speedup vs baseline: 1.0569x; 1.0569x over previous
import numpy as np
from contextlib import ExitStack

import concourse.bass as bass
import concourse.mybir as mybir
from concourse.bass_utils import run_bass_kernel_spmd

F32 = mybir.dt.float32
BF16 = mybir.dt.bfloat16
AF = mybir.ActivationFunctionType
ALU = mybir.AluOpType
AX = mybir.AxisListType

D = 1024
NCH = 8
H = 8
DH = 128
CH = 64
FF = 4096
NJ = 32
LN_EPS = 1e-5
A_EPS = 1e-5
RMS_EPS = 1e-5
LN09 = float(np.log(0.9))

ENGS = ("t", "v", "s", "p", "y")


class Prog:

    def __init__(self, nc, es):
        self.nc = nc
        self.es = es
        self.ops = {e: [] for e in ENGS}
        self.seq = {e: 0 for e in ENGS}
        self.waited = {e: set() for e in ENGS}
        self.dma_cnt = {}
        self.dma_h = {}
        self.eng_h = {}
        self.reg = {}
        for e in ENGS:
            self.eng_h[e] = es.enter_context(nc.semaphore(f"se_{e}"))

    def dma_sem(self, name):
        if name not in self.dma_h:
            self.dma_h[name] = self.es.enter_context(self.nc.semaphore(f"sd_{name}"))
            self.dma_cnt[name] = 0
        return self.dma_h[name]

    def _dep(self, name):
        if name not in self.reg:
            self.reg[name] = {"w": None, "r": []}
        return self.reg[name]

    def _collect(self, reads, writes):
        waits = []
        for r in reads:
            d = self._dep(r)
            if d["w"] is not None:
                waits.append(d["w"])
        for w in writes:
            d = self._dep(w)
            if d["w"] is not None:
                waits.append(d["w"])
            waits.extend(d["r"])
        best = {}
        for kind, key, val in waits:
            k = (kind, key)
            if k not in best or best[k] < val:
                best[k] = val
        return [(kind, key, val) for (kind, key), val in best.items()]

    def _record(self, tok, reads, writes):
        for r in reads:
            self._dep(r)["r"].append(tok)
        for w in writes:
            self.reg[w] = {"w": tok, "r": []}

    def op(self, eng, fn, reads=(), writes=(), extra=()):
        waits = self._collect(reads, writes) + list(extra)
        my = []
        for kind, key, val in waits:
            if kind == "eng":
                if key == eng:
                    self.waited[key].add(val)
                else:
                    self.waited[key].add(val)
            my.append((kind, key, val))
        self.seq[eng] += 1
        seq = self.seq[eng]
        self.ops[eng].append({"fn": fn, "waits": my, "seq": seq})
        self._record(("eng", eng, seq), reads, writes)
        return seq

    def dma(self, sem_name, fn, reads=(), writes=()):
        self.dma_sem(sem_name)
        prev = self.dma_cnt[sem_name]
        waits = self._collect(reads, writes)
        if prev > 0:
            waits.append(("dma", sem_name, prev))
        for kind, key, val in waits:
            if kind == "eng":
                self.waited[key].add(val)
        self.dma_cnt[sem_name] = prev + 16
        val = self.dma_cnt[sem_name]
        self.seq["y"] += 1
        self.ops["y"].append(
            {"fn": fn, "waits": waits, "seq": None, "dma": (sem_name, val)}
        )
        self._record(("dma", sem_name, val), reads, writes)
        return val

    def final_waits(self, eng="y"):
        def mk(name, val):
            def run(e, name=name, val=val):
                e.wait_ge(self.dma_h[name], val)
            return run
        for name, val in self.dma_cnt.items():
            if val > 0:
                self.ops[eng].append({"fn": mk(name, val), "waits": (), "seq": None,
                                      "raw": True})

    def finalize(self):
        remap = {}
        for e in ENGS:
            wl = sorted(self.waited[e])
            remap[e] = {v: i + 1 for i, v in enumerate(wl)}
        self.real_inc = {e: set(self.waited[e]) for e in ENGS}
        self.remap = remap

    def run_engine(self, e, handle):
        already = {}
        for rec in self.ops[e]:
            if rec.get("raw"):
                rec["fn"](handle)
                continue
            for kind, key, val in rec["waits"]:
                if kind == "eng":
                    rv = self.remap[key].get(val)
                    if rv is None:
                        continue
                    k = ("e", key)
                    if already.get(k, 0) >= rv:
                        continue
                    already[k] = rv
                    handle.wait_ge(self.eng_h[key], rv)
                else:
                    k = ("d", key)
                    if already.get(k, 0) >= val:
                        continue
                    already[k] = val
                    handle.wait_ge(self.dma_h[key], val)
            inst = rec["fn"](handle)
            if rec.get("dma"):
                name, val = rec["dma"]
                inst.then_inc(self.dma_h[name], 16)
            elif rec["seq"] in self.real_inc[e]:
                inst.then_inc(self.eng_h[e], 1)


def ceil_div(a, b):
    return -(-a // b)


def build_nc(S=4096, phases=("attn_a", "ffn1", "attn_b", "ffn2")):
    nc = bass.Bass("TRN2", target_bir_lowering=False)
    NS = S // 512
    NS2 = S // 256
    NCHK = S // CH
    NTC = S // 128

    dr = {}
    def din(name, shape):
        dr[name] = nc.dram_tensor(name, shape, F32, kind="ExternalInput").ap()
    def dinb(name, shape):
        dr[name] = nc.dram_tensor(name, shape, BF16, kind="ExternalInput").ap()

    din("xT", [D, S])
    for w in ("aq", "ak", "av", "ao"):
        din(w, [D, D])
    din("anw", [128, 1])
    din("w1", [D, FF]); din("w2", [FF, D])
    din("b1c", [128, NJ]); din("b2c", [128, NCH])
    for w in ("bq", "bk", "bv", "bo"):
        din(w, [D, D])
    din("bvecs", [128, 96])
    din("boc", [128, NCH])
    dinb("onesall", [128, 128])
    din("onesf32", [128, 128])
    dinb("maskt", [128, 512])
    din("cvals", [128, 8])
    din("onef", [1, 1])
    out = nc.dram_tensor("outT", [D, S], F32, kind="ExternalOutput").ap()
    import os as _os
    DBG = bool(_os.environ.get("KDBG"))
    if DBG:
        dbg1 = nc.dram_tensor("dbg1", [128, S], F32, kind="ExternalOutput").ap()
        dbg2 = nc.dram_tensor("dbg2", [128, NTC], F32, kind="ExternalOutput").ap()
        dbg3 = nc.dram_tensor("dbg3", [128, NTC * 1024], BF16, kind="ExternalOutput").ap()
    xB = nc.dram_tensor("xBbuf", [D, S], F32, kind="Internal").ap()
    wcA = nc.dram_tensor("wcA", [128, 32768], BF16, kind="Internal").ap()
    wcB = nc.dram_tensor("wcB", [128, 32768], BF16, kind="Internal").ap()
    wcC = nc.dram_tensor("wcC", [128, 32768], BF16, kind="Internal").ap()
    xC = nc.dram_tensor("xCbuf", [D, S], F32, kind="Internal").ap()

    es = ExitStack()
    AA = es.enter_context(nc.sbuf_tensor("AA", [128, 32768], BF16))
    AB = es.enter_context(nc.sbuf_tensor("AB", [128, 32768], BF16))
    XB0 = es.enter_context(nc.sbuf_tensor("XB0", [128, 4096], F32))
    XB1 = es.enter_context(nc.sbuf_tensor("XB1", [128, 4096], F32))
    LNB = es.enter_context(nc.sbuf_tensor("LNB", [128, 4096], BF16))
    BIG = es.enter_context(nc.sbuf_tensor("BIG", [128, 4096], F32))
    SQ = es.enter_context(nc.sbuf_tensor("SQ", [128, 4096], F32))
    STG = es.enter_context(nc.sbuf_tensor("STG", [128, 1024], F32))
    CON = es.enter_context(nc.sbuf_tensor("CON", [128, 128], BF16))
    CONF = es.enter_context(nc.sbuf_tensor("CONF", [128, 128], F32))
    MSK = es.enter_context(nc.sbuf_tensor("MSK", [128, 512], BF16))
    CV = es.enter_context(nc.sbuf_tensor("CV", [128, 8], F32))
    NW = es.enter_context(nc.sbuf_tensor("NW", [128, 1], F32))
    B1C = es.enter_context(nc.sbuf_tensor("B1C", [128, NJ], F32))
    B2C = es.enter_context(nc.sbuf_tensor("B2C", [128, NCH], F32))
    BVE = es.enter_context(nc.sbuf_tensor("BVE", [128, 96], F32))
    BOC = es.enter_context(nc.sbuf_tensor("BOC", [128, NCH], F32))
    ALF = es.enter_context(nc.sbuf_tensor("ALF", [128, 64], F32))
    CAR = es.enter_context(nc.sbuf_tensor("CAR", [128, 8], F32))
    CARB = es.enter_context(nc.sbuf_tensor("CARB", [128, 8], BF16))
    K8 = es.enter_context(nc.sbuf_tensor("K8", [128, 8], F32))
    SKC = es.enter_context(nc.sbuf_tensor("SKC", [128, 32], F32))
    CSB = es.enter_context(nc.sbuf_tensor("CSB", [128, NCH], F32))
    ONEF = es.enter_context(nc.sbuf_tensor("ONEF", [1, 1], F32))
    PS = [es.enter_context(nc.psum_tensor(f"PS{i}", [128, 512], F32))
          for i in range(8)]

    BIGB = BIG.ap().bitcast(BF16)
    SQB = SQ.ap().bitcast(BF16)

    P = Prog(nc, es)

    def wp(reg, col, ncol=1024, dcs=range(8)):
        cp = col // ncol
        return tuple(f"{reg}@{dc}:{cp}" for dc in dcs)

    def MMg(psum_ap, pairs, reads, writes, starts=None):
        def fn(t, psum_ap=psum_ap, pairs=pairs):
            inst = None
            for (lh, rh, st, sp, oap) in pairs:
                inst = t.matmul(oap if oap is not None else psum_ap, lh, rh,
                                start=st, stop=sp)
            return inst
        return P.op("t", fn, reads=reads, writes=writes)

    P.dma("dc", lambda y: y.dma_start(CON[:, :], dr["onesall"]), writes=("c_ones",))
    P.dma("dc", lambda y: y.dma_start(CONF[:, :], dr["onesf32"]), writes=("c_onesf",))
    P.dma("dc", lambda y: y.dma_start(MSK[:, :], dr["maskt"]), writes=("c_mask",))
    P.dma("dc", lambda y: y.dma_start(CV[:, :], dr["cvals"]), writes=("c_cv",))
    P.dma("dc", lambda y: y.dma_start(NW[:, :], dr["anw"]), writes=("c_nw",))
    P.dma("dc", lambda y: y.dma_start(B1C[:, :], dr["b1c"]), writes=("c_b1",))
    P.dma("dc", lambda y: y.dma_start(B2C[:, :], dr["b2c"]), writes=("c_b2",))
    P.dma("dc", lambda y: y.dma_start(BVE[:, :], dr["bvecs"]), writes=("c_bv",))
    P.dma("dc", lambda y: y.dma_start(BOC[:, :], dr["boc"]), writes=("c_boc",))
    P.dma("dc", lambda y: y.dma_start(ONEF[:, :], dr["onef"]), writes=("c_onef",))

    ARENA_REGS = (
        ["qfm", "kfm", "ktm", "vtm", "on", "stateB0", "stateB1", "VW", "VW#"]
        + [f"wAq@{r}:{c}" for r in range(8) for c in range(4)]
        + [f"wAk@{r}:0" for r in range(8)]
        + [f"wAv@{r}:0" for r in range(8)]
        + [f"wAo@{r}:0" for r in range(8)]
        + [f"wB@{r}:0" for r in range(32)]
        + ["wps0", "wps1", "wcb0", "wcb1"]
    )

    ABF_T = AB.ap().bitcast(F32)

    def wprep_piece(dname, rc, cache, coff, k):
        slot = k % 2
        stg = ABF_T[:, 12288 + slot * 1024: 13312 + slot * 1024]
        bnc = AB[:, 22528 + slot * 1024: 23552 + slot * 1024]
        sreg, breg = f"wps{slot}", f"wcb{slot}"
        src = dr[dname][rc * 128:(rc + 1) * 128, :]
        P.dma(f"dwp{slot}", (lambda y, d=stg, so=src: y.dma_start(d, so)),
              reads=(), writes=(sreg,))
        P.op("p", (lambda p, d=bnc, so=stg: p.tensor_copy(d, so)),
             reads=(sreg,), writes=(breg,))
        dst = cache[:, coff + rc * 1024: coff + (rc + 1) * 1024]
        P.dma(f"dwo{slot}", (lambda y, d=dst, so=bnc: y.dma_start(d, so)),
              reads=(breg,), writes=(f"wc:{dname}",))

    def load_weights(src_list, slot1=None):
        P.op("v", (lambda v: v.memset(K8[0:1, 0:1], 0.0)),
             reads=(), writes=tuple(ARENA_REGS))
        si = 0
        AL1 = ("lnm2", "ln_mb", "ln_rs", "x2a", "x2b", "ln_tmp",
               "qsig0", "qsig1", "at0", "at1", "u0")
        if slot1 is None:
            s1_ap, s1_al, s1_tail = BIG, AL1, (lambda dn, rc, cp: False)
        else:
            s1_ap, s1_al, s1_tail = slot1
        for (dname, nrow, ncol, reg, dtile, coff) in src_list:
            npc = ceil_div(ncol, 1024)
            pc = ncol // npc
            for cp in range(npc):
                for rc in range(nrow):
                    slot = si % 2
                    si += 1
                    if s1_tail(dname, rc, cp):
                        slot = 0
                    stg_ap = (STG[:, 0:pc], s1_ap[:, 0:pc])[slot]
                    sreg = f"stg{slot}"
                    al = (() if slot == 0 else tuple(s1_al))
                    src = dr[dname][rc * 128:(rc + 1) * 128,
                                    cp * pc:(cp + 1) * pc]
                    P.dma(f"dw{slot}",
                          (lambda y, d=stg_ap, s=src: y.dma_start(d, s)),
                          reads=(), writes=(sreg,) + al)
                    off = coff + rc * ncol + cp * pc
                    dst = dtile[:, off: off + pc]
                    P.op("v", (lambda e, d=dst, s=stg_ap: e.tensor_copy(d, s)),
                         reads=(sreg,) + al, writes=(f"{reg}@{rc}:{cp}",))


    def ln_pass(T, xap, xregs, lnap, lnreg, ps_m, ps_s, sb, mr_store=None,
                extra_war=(), mid=None, x2_bb=None, split_apply=False,
                math_war=(), apply_war=(), rsqrt_dve=False):
        pm, psx = f"PS{ps_m}", f"PS{ps_s}"
        if sb == "ffnpre":
            LNBF2 = LNB.ap().bitcast(F32)
            lnm2 = LNBF2[:, 1152:1152 + T]
            x2a = LNB[:, 2816:2816 + T]
            x2b = LNB[:, 2816 + T:2816 + 2 * T]
            bb = None
        else:
            lnm2 = BIG[:, sb:sb + T]
            bb = 2 * (sb + 3 * T) if x2_bb is None else x2_bb
            x2a = BIGB[:, bb:bb + T]
            x2b = BIGB[:, bb + T:bb + 2 * T]
        if bb is None:
            tmp = tmp2 = None
        elif x2_bb is None:
            tmp = BIGB[:, bb + 2 * T:bb + 3 * T]
            tmp2 = tmp
        else:
            tmp = BIGB[:, 3072:3072 + T]
            tmp2 = BIGB[:, 3072 + T:3072 + 2 * T]
        if mr_store is None:
            mb = BIG[:, sb + T:sb + 2 * T]
            rs = BIG[:, sb + 2 * T:sb + 3 * T]
            mbr, rsr = "ln_mb", "ln_rs"
        else:
            mb, rs, mbr, rsr = mr_store
        for c in range(NCH):
            xa = xap[:, c * T:(c + 1) * T]
            if mr_store is None:
                la = lnap[:, c * T:(c + 1) * T]
                wl = (lnreg,) if c == 0 else (lnreg + "#",)
                lrd = ("c_ones", lnreg, lnreg + "#")
            else:
                la = LNB[:, 3328 + (c % 2) * T: 3328 + (c % 2) * T + T]
                wl = ("pxa",) if c % 2 == 0 else ("pxb",)
                lrd = ("c_ones",) + wl
            if c == 0:
                wl = wl + tuple(extra_war)
            P.op("v", (lambda v, d=la, so=xa: v.tensor_copy(d, so)),
                 reads=xregs, writes=wl)
            x2 = x2a if c % 2 == 0 else x2b
            x2r = "x2a" if c % 2 == 0 else "x2b"
            P.op("s", (lambda s_, d=x2, so=xa: s_.activation(d, so, AF.Square)),
                 reads=xregs, writes=(x2r,) + (tuple(extra_war) if c == 0 else ()))
            P.op("t", (lambda t, o=PS[ps_m][:, 0:T], l=la, st=(c == 0),
                       sp=(c == NCH - 1):
                       t.matmul(o, CON[:, :], l, start=st, stop=sp)),
                 reads=lrd, writes=(pm,))
            P.op("t", (lambda t, o=PS[ps_s][:, 0:T], l=x2, st=(c == 0),
                       sp=(c == NCH - 1):
                       t.matmul(o, CON[:, :], l, start=st, stop=sp)),
                 reads=("c_ones", x2r), writes=(psx,))
        if mid is not None:
            mid()
        P.op("v", (lambda v, d=mb: v.tensor_scalar(d, PS[ps_m][:, 0:T],
                                                   1.0 / D, None, ALU.mult)),
             reads=(pm,), writes=(mbr,) + tuple(math_war))
        P.op("v", (lambda v, d=lnm2, m=mb: v.tensor_tensor(d, m, m, ALU.mult)),
             reads=(mbr,), writes=("lnm2",))
        P.op("v", (lambda v, d=lnm2, m=lnm2: v.scalar_tensor_tensor(
            d, PS[ps_s][:, 0:T], 1.0 / D, m, ALU.mult, ALU.subtract)),
             reads=(psx, "lnm2"), writes=("lnm2",))
        if not rsqrt_dve:
            P.op("s", (lambda s_, d=lnm2: s_.activation(d, d, AF.Ln, bias=CV[:, 1:2])),
                 reads=("lnm2", "c_cv"), writes=("lnm2",))
            P.op("s", (lambda s_, d=rs, so=lnm2: s_.activation(d, so, AF.Exp,
                                                               scale=-0.5)),
                 reads=("lnm2",), writes=(rsr,))
        else:
            I32 = mybir.dt.int32
            if sb == "ffnpre":
                LNBF3 = LNB.ap().bitcast(F32)
                yv = LNBF3[:, 1408:1408 + T]
                t1v = LNBF3[:, 1664:1664 + T]
            else:
                yv = BIG[:, 3584:3584 + T]
                t1v = BIG[:, 1536:1536 + T]
            yregs = ("x2a", "x2b")
            tregs = ("pxa", "pxb") if sb == "ffnpre" else ("ln_tmp", "ln_tmp2")
            yi = yv.bitcast(I32)
            li = lnm2.bitcast(I32)
            P.op("v", (lambda v, d=lnm2: v.tensor_scalar(d, d, LN_EPS, None,
                                                         ALU.add)),
                 reads=("lnm2",), writes=("lnm2",))
            P.op("v", (lambda v, d=yi, so=li: v.tensor_scalar(
                d, so, 1, None, ALU.arith_shift_right)),
                 reads=("lnm2",), writes=yregs)
            P.op("v", (lambda v, d=yi: v.tensor_scalar(d, d, 0x5f3759df, None,
                                                       ALU.subtract)),
                 reads=yregs, writes=yregs)
            P.op("v", (lambda v, d=yi: v.tensor_scalar(d, d, -1, None,
                                                       ALU.bitwise_xor)),
                 reads=yregs, writes=yregs)
            P.op("v", (lambda v, d=yi: v.tensor_scalar(d, d, 1, None, ALU.add)),
                 reads=yregs, writes=yregs)
            for it in range(3):
                last = it == 2
                P.op("v", (lambda v, d=t1v, y2=yv: v.tensor_tensor(
                    d, y2, y2, ALU.mult)),
                     reads=yregs, writes=tregs)
                P.op("v", (lambda v, d=t1v, l2=lnm2: v.tensor_tensor(
                    d, d, l2, ALU.mult)),
                     reads=tregs + ("lnm2",), writes=tregs)
                P.op("v", (lambda v, d=t1v: v.tensor_scalar(
                    d, d, -0.5, 1.5, ALU.mult, ALU.add)),
                     reads=tregs, writes=tregs)
                if last:
                    P.op("v", (lambda v, d=rs, y2=yv, t2=t1v: v.tensor_tensor(
                        d, y2, t2, ALU.mult)),
                         reads=yregs + tregs, writes=(rsr,))
                else:
                    P.op("v", (lambda v, d=yv, t2=t1v: v.tensor_tensor(
                        d, d, t2, ALU.mult)),
                         reads=yregs + tregs, writes=yregs)

        if mr_store is None:
            for c in range(NCH):
                eng = "p" if (split_apply and c >= NCH // 2) else "v"
                tm_ = tmp2 if eng == "p" else tmp
                tr_ = "ln_tmp2" if eng == "p" else "ln_tmp"
                aw = tuple(apply_war) if c in (0, NCH // 2) else ()
                xa = xap[:, c * T:(c + 1) * T]
                la = lnap[:, c * T:(c + 1) * T]
                P.op(eng, (lambda e, d=tm_, so=xa, m=mb:
                           e.tensor_tensor(d, so, m, ALU.subtract)),
                     reads=xregs + (mbr, tr_), writes=(tr_,) + aw)
                P.op(eng, (lambda e, d=la, t=tm_, r=rs:
                           e.tensor_tensor(d, t, r, ALU.mult)),
                     reads=(tr_, rsr, lnreg, lnreg + "#"), writes=(lnreg,))

    def x_load(buf_ap, dram_ap, s, T, slot_reg, sem):
        src = dram_ap.rearrange("(c p) t -> p c t", p=128)[:, :, s * T:(s + 1) * T]
        dst = buf_ap.rearrange("p (c t) -> p c t", c=NCH)
        return P.dma(sem, (lambda y, d=dst, so=src: y.dma_start(d, so)),
                     reads=(), writes=(slot_reg,))

    def x_store(buf_ap, dram_ap, s, T, slot_reg, sem, hreg):
        dst = dram_ap.rearrange("(c p) t -> p c t", p=128)[:, :, s * T:(s + 1) * T]
        src = buf_ap.rearrange("p (c t) -> p c t", c=NCH)
        return P.dma(sem, (lambda y, d=dst, so=src: y.dma_start(d, so)),
                     reads=(slot_reg,), writes=(hreg,))

    def phase_attn_a(src_dram, src_regs, dst_dram, dst_pref):
        T = 512
        QF, KF, KT, VT, ON, STB = 0, 4096, 8192, 12288, 16384, 20480
        P.op("v", lambda v: v.memset(AB[:, STB:STB + 2048], 0.0), writes=("stateB0", "stateB1"))
        P.op("v", lambda v: v.memset(SQ[:, 0:1024], 0.0), writes=("statef",))
        P.op("v", lambda v: v.memset(CAR[:, :], 0.0), writes=("car32",))
        P.op("v", lambda v: v.memset(CARB[:, :], 0.0), writes=("carb",))

        nsl = S // T

        def sctx(s):
            sl = s % 2
            xb = (XB0 if sl == 0 else XB1).ap()
            return sl, xb, (f"xs{2*sl}", f"xs{2*sl+1}")

        def load_ln(s):
            sl, xb, xregs = sctx(s)
            x_load(xb, src_dram, s, T, xregs[0], f"dx{sl}")
            P.reg[xregs[1]] = dict(P.reg[xregs[0]])
            ln_pass(T, xb, xregs, LNB.ap(), "lnb", 6, 7, 0)

        load_ln(0)
        load_weights([("aq", 8, 1024, "wAq", AA, 0),
                      ("ak", 8, 1024, "wAk", AA, 8192),
                      ("av", 8, 1024, "wAv", AA, 16384),
                      ("ao", 8, 1024, "wAo", AA, 24576)])
        wprep_jobs = []
        for k in range(32):
            rc, cp = k // 4, k % 4
            wprep_jobs.append(("w1pc", (rc, cp)))
        for rc in range(32):
            wprep_jobs.append(("w2", rc))
        for wi, nm in enumerate(("bq", "bk", "bv", "bo")):
            for rc in range(8):
                wprep_jobs.append((nm, (wi, rc)))
        wprep_k = [0]

        def wprep_emit(n):
            for _ in range(n):
                if wprep_k[0] >= len(wprep_jobs):
                    return
                job = wprep_jobs[wprep_k[0]]
                k = wprep_k[0]
                wprep_k[0] += 1
                if job[0] == "w1pc":
                    rc, cp = job[1]
                    slot = k % 2
                    stg = ABF_T[:, 12288 + slot * 1024: 13312 + slot * 1024]
                    bnc = AB[:, 22528 + slot * 1024: 23552 + slot * 1024]
                    sreg, breg = f"wps{slot}", f"wcb{slot}"
                    src = dr["w1"][rc * 128:(rc + 1) * 128,
                                   cp * 1024:(cp + 1) * 1024]
                    P.dma(f"dwp{slot}", (lambda y, d=stg, so=src:
                                         y.dma_start(d, so)),
                          reads=(), writes=(sreg,))
                    P.op("p", (lambda p, d=bnc, so=stg: p.tensor_copy(d, so)),
                         reads=(sreg,), writes=(breg,))
                    dst = wcA[:, rc * 4096 + cp * 1024: rc * 4096 + (cp + 1) * 1024]
                    P.dma(f"dwo{slot}", (lambda y, d=dst, so=bnc:
                                         y.dma_start(d, so)),
                          reads=(breg,), writes=("wc:w1",))
                elif job[0] == "w2":
                    wprep_piece("w2", job[1], wcB, 0, k)
                else:
                    wi, rc = job[1]
                    wprep_piece(job[0], rc, wcC, wi * 8192, k)
        P.op("v", (lambda v: v.tensor_scalar(AA[:, 24576:32768], AA[:, 24576:32768],
                                             NW[:, 0:1], None, ALU.mult)),
             reads=wp("wAo", 0) + ("c_nw",), writes=wp("wAo", 0))
        for s in range(nsl):
            sl, xb, xregs = sctx(s)
            def proj(dst_off, woff, dreg, kind):
                if kind == "fm":
                    for ec in range(NCH):
                        pb = ec % 2
                        pairs = []
                        for dc in range(NCH):
                            lh = AA[:, woff + dc * 1024 + ec * 128:
                                    woff + dc * 1024 + ec * 128 + 128]
                            rh = LNB[:, dc * T:(dc + 1) * T]
                            pairs.append((lh, rh, dc == 0, dc == 7, PS[pb][:, 0:T]))
                        MMg(None, pairs, reads=wp(dreg, 0) + ("lnb", f"PS{pb}"),
                            writes=(f"PS{pb}",))
                        dst = AB[:, dst_off + ec * T: dst_off + (ec + 1) * T]
                        P.op("v", (lambda v, d=dst, so=PS[pb][:, 0:T]:
                                   v.tensor_copy(d, so)),
                             reads=(f"PS{pb}",),
                             writes=(("qfm",) if dst_off == QF else ("kfm",)))
                else:
                    for t4 in range(4):
                        for eh in range(2):
                            pb = (t4 * 2 + eh) % 2
                            pairs = []
                            for dc in range(NCH):
                                lh = LNB[:, dc * T + t4 * 128: dc * T + t4 * 128 + 128]
                                rh = AA[:, woff + dc * 1024 + eh * 512:
                                        woff + dc * 1024 + (eh + 1) * 512]
                                pairs.append((lh, rh, dc == 0, dc == 7,
                                              PS[pb][:, 0:512]))
                            MMg(None, pairs, reads=wp(dreg, 0) + ("lnb", f"PS{pb}"),
                                writes=(f"PS{pb}",))
                            dst = AB[:, dst_off + t4 * 1024 + eh * 512:
                                     dst_off + t4 * 1024 + (eh + 1) * 512]
                            P.op("v", (lambda v, d=dst, so=PS[pb][:, 0:512]:
                                       v.tensor_copy(d, so)),
                                 reads=(f"PS{pb}",),
                                 writes=(("ktm",) if dst_off == KT else ("vtm",)))
            proj(QF, 0, "wAq", "fm")
            proj(KF, 8192, "wAk", "fm")
            proj(KT, 8192, "wAk", "tm")
            proj(VT, 16384, "wAv", "tm")

            for pp in range(T // 128):
                g = s * (T // 128) + pp
                par = g % 2
                stb_new = STB + 1024 * par
                stb_old = STB + 1024 * (1 - par)
                sb_new, sb_old = f"stateB{par}", f"stateB{1 - par}"
                for hg in range(2):
                    pairs = []
                    for hi in range(4):
                        h = hg * 4 + hi
                        lh = AB[:, KF + h * T + pp * 128: KF + h * T + pp * 128 + 128]
                        rh = AB[:, QF + h * T + pp * 128: QF + h * T + pp * 128 + 128]
                        pairs.append((lh, rh, hi == 0, hi == 3,
                                      PS[hg][:, hi * 128:(hi + 1) * 128]))
                    MMg(None, pairs, reads=("kfm", "qfm", f"PS{hg}"),
                        writes=(f"PS{hg}",))
                pairs = []
                for h in range(H):
                    kt = AB[:, KT + pp * 1024 + h * 128: KT + pp * 1024 + (h + 1) * 128]
                    vt = AB[:, VT + pp * 1024 + h * 128: VT + pp * 1024 + (h + 1) * 128]
                    bank = 2 + h // 4
                    oap = PS[bank][:, (h % 4) * 128:(h % 4 + 1) * 128]
                    pairs.append((kt, vt, h % 4 == 0, h % 4 == 3, oap))
                MMg(None, pairs, reads=("ktm", "vtm", "PS2", "PS3"),
                    writes=("PS2", "PS3"))
                sms = []
                for hg in range(2):
                    smx = BIGB[:, 6144 + hg * 512: 6656 + hg * 512]
                    smr = f"sm{hg}"
                    sms.append((smx, smr))
                    P.op("v", (lambda v, d=smx, pbs=hg:
                               v.tensor_tensor(d, PS[pbs][:, 0:512], MSK[:, :],
                                               ALU.mult)),
                         reads=(f"PS{hg}", "c_mask"), writes=(smr,))
                P.op("v", (lambda v: v.tensor_tensor(SQ[:, 0:512], SQ[:, 0:512],
                                                     PS[2][:, 0:512], ALU.add)),
                     reads=("PS2", "statef"), writes=("statef",))
                P.op("v", (lambda v: v.tensor_tensor(SQ[:, 512:1024], SQ[:, 512:1024],
                                                     PS[3][:, 0:512], ALU.add)),
                     reads=("PS3", "statef"), writes=("statef",))
                P.op("v", (lambda v, d=stb_new: v.tensor_copy(
                    AB[:, d:d + 1024], SQ[:, 0:1024])),
                     reads=("statef",), writes=(sb_new,))
                for hg in range(2):
                    smx, smr = sms[hg]
                    pb_o = 4 + hg
                    pairs = []
                    for hi in range(4):
                        h = hg * 4 + hi
                        vsl = AB[:, VT + pp * 1024 + h * 128:
                                 VT + pp * 1024 + (h + 1) * 128]
                        smh = smx[:, hi * 128:(hi + 1) * 128]
                        oap = PS[pb_o][:, hi * 128:(hi + 1) * 128]
                        pairs.append((vsl, smh, hi == 0, False, oap))
                        stt = AB[:, stb_old + h * 128: stb_old + (h + 1) * 128]
                        qsl = AB[:, QF + h * T + pp * 128: QF + h * T + pp * 128 + 128]
                        pairs.append((stt, qsl, False, hi == 3, oap))
                    MMg(None, pairs, reads=("vtm", smr, sb_old, "qfm",
                                            f"PS{pb_o}"),
                        writes=(f"PS{pb_o}",))
                    pairs = []
                    for hi in range(4):
                        h = hg * 4 + hi
                        cb = CARB[:, h:h + 1].broadcast_to([128, 128])
                        qsl = AB[:, QF + h * T + pp * 128: QF + h * T + pp * 128 + 128]
                        oap = PS[6][:, hi * 128:(hi + 1) * 128]
                        pairs.append((cb, qsl, hi == 0, False, oap))
                        smh = smx[:, hi * 128:(hi + 1) * 128]
                        pairs.append((CON[:, :], smh, False, hi == 3, oap))
                    MMg(None, pairs, reads=("carb", "qfm", smr, "c_ones", "PS6"),
                        writes=("PS6",))
                    o2 = BIGB[:, 5632:6144]
                    P.op("s", (lambda s_, d=o2, pbo=pb_o:
                               s_.activation(d, PS[pbo][:, 0:512], AF.Square)),
                         reads=(f"PS{pb_o}",), writes=("o2",))
                    sbc = BIG[:, 2304:2816]
                    P.op("s", (lambda s_, d=sbc: s_.activation(
                        d, PS[6][:, 0:512], AF.Square)),
                         reads=("PS6",), writes=("sbc",))
                    pairs = []
                    for hi in range(4):
                        pairs.append((CON[:, :], o2[:, hi * 128:(hi + 1) * 128],
                                      hi == 0, hi == 3,
                                      PS[7][:, hi * 128:(hi + 1) * 128]))
                    MMg(None, pairs, reads=("o2", "c_ones", "PS7"), writes=("PS7",))
                    P.op("v", (lambda v, d=sbc: v.tensor_scalar(
                        d, d, 1.0, RMS_EPS, ALU.max, ALU.mult)),
                         reads=("sbc",), writes=("sbc",))
                    P.op("v", (lambda v, d=sbc: v.scalar_tensor_tensor(
                        d, PS[7][:, 0:512], 1.0 / DH, d, ALU.mult, ALU.add)),
                         reads=("PS7", "sbc"), writes=("sbc",))
                    P.op("s", (lambda s_, d=sbc: s_.activation(d, d, AF.Ln)),
                         reads=("sbc",), writes=("sbc",))
                    P.op("s", (lambda s_, d=sbc: s_.activation(d, d, AF.Exp,
                                                               scale=-0.5)),
                         reads=("sbc",), writes=("sbc",))
                    onv = AB[:, ON:ON + 4096].rearrange(
                        "p (h k t) -> p h k t", h=H, k=T // 128)[
                        :, hg * 4:(hg + 1) * 4, pp, :]
                    piv = PS[pb_o][:, 0:512].rearrange("p (h t) -> p h t", h=4)
                    sbv = sbc.rearrange("p (h t) -> p h t", h=4)
                    P.op("v", (lambda v, d=onv, a=piv, b=sbv:
                               v.tensor_tensor(d, a, b, ALU.mult)),
                         reads=(f"PS{pb_o}", "sbc"), writes=("on",))
                kfv = AB[:, KF:KF + 4096].rearrange("p (h t) -> p h t", h=H)[
                    :, :, pp * 128:(pp + 1) * 128]
                P.op("v", (lambda v, d=K8[:, :], so=kfv:
                           v.tensor_reduce(d, so, AX.X, ALU.add)),
                     reads=("kfm",), writes=("k8",))
                P.op("v", (lambda v: v.tensor_tensor(CAR[:, :], CAR[:, :],
                                                     K8[:, :], ALU.add)),
                     reads=("car32", "k8"), writes=("car32",))
                P.op("v", (lambda v: v.tensor_copy(CARB[:, :], CAR[:, :])),
                     reads=("car32",), writes=("carb",))
            wprep_emit(ceil_div(len(wprep_jobs), nsl))
            if s + 1 < nsl:
                load_ln(s + 1)
            for doc in range(NCH):
                pb = doc % 2
                pairs = []
                for ec in range(NCH):
                    lh = AA[:, 24576 + ec * 1024 + doc * 128:
                            24576 + ec * 1024 + doc * 128 + 128]
                    rh = AB[:, ON + ec * T: ON + (ec + 1) * T]
                    pairs.append((lh, rh, ec == 0, ec == 7, PS[pb][:, 0:T]))
                MMg(None, pairs, reads=wp("wAo", 0) + ("on", f"PS{pb}"), writes=(f"PS{pb}",))
                xv = xb[:, doc * T:(doc + 1) * T]
                P.op("v", (lambda v, d=xv, pbb=pb: v.tensor_tensor(
                    d, d, PS[pbb][:, 0:T], ALU.add)),
                     reads=(f"PS{pb}",) + xregs, writes=(xregs[0], xregs[1]))
            x_store(xb, dst_dram, s, T, xregs[0], f"ds{sl}", f"{dst_pref}:{2*s}")
            P.reg[f"{dst_pref}:{2*s+1}"] = dict(P.reg[f"{dst_pref}:{2*s}"])

    def phase_ffn(src_dram, src_pref, dst_dram, dst_pref, final=False, wmode='plain'):
        T = 256
        nsl = S // T
        def prepass(s):
            sl = s % 4
            xb = (XB0 if sl < 2 else XB1).ap()[
                :, (sl % 2) * 2048:(sl % 2) * 2048 + 2048]
            xreg = f"xs{sl}"
            src = src_dram.rearrange("(c p) t -> p c t", p=128)[
                :, :, s * T:(s + 1) * T]
            dst = xb.rearrange("p (c t) -> p c t", c=NCH)
            P.dma(f"dx{sl}", (lambda y, d=dst, so=src: y.dma_start(d, so)),
                  reads=(f"{src_pref}:{s}",) if src_pref else (),
                  writes=(xreg,))
            mb = SQB[:, s * T:(s + 1) * T]
            rs = SQB[:, 4096 + s * T: 4096 + (s + 1) * T]
            ln_pass(T, xb, (xreg,), LNB.ap()[:, 0:2048], "lnb", 4, 5, "ffnpre",
                    mr_store=(mb, rs, f"mrm{s}", f"mrr{s}"), rsqrt_dve=True)

        def fctx(s):
            sl = s % 4
            xb = (XB0 if sl < 2 else XB1).ap()[
                :, (sl % 2) * 2048:(sl % 2) * 2048 + 2048]
            return sl, xb, f"xs{sl}"

        def fpre(s):
            sl, xb, xreg = fctx(s)
            mb = SQB[:, s * T:(s + 1) * T]
            rs = SQB[:, 4096 + s * T: 4096 + (s + 1) * T]
            for c in range(NCH):
                la = LNB[:, c * T:(c + 1) * T]
                P.op("v", (lambda v, d=la, so=xb[:, c * T:(c + 1) * T]:
                           v.tensor_copy(d, so)),
                     reads=(xreg,), writes=("lnb",) if c == 0 else ("lnb#",))
            for c in range(NCH):
                la = LNB[:, c * T:(c + 1) * T]
                P.op("v", (lambda v, d=la, m=mb: v.tensor_tensor(d, d, m, ALU.subtract)),
                     reads=("lnb", "lnb#", f"mrm{s}"), writes=("lnb",))
                P.op("v", (lambda v, d=la, r=rs: v.tensor_tensor(d, d, r, ALU.mult)),
                     reads=("lnb", f"mrr{s}"), writes=("lnb",))

        for _k in range(min(3, nsl)):
            prepass(_k)
        fpre(0)
        ABF = AB.ap().bitcast(F32)
        if wmode == "load":
            P.op("v", (lambda v: v.memset(K8[0:1, 0:1], 0.0)),
                 reads=(), writes=tuple(ARENA_REGS))
            for k in range(8):
                regs = tuple(f"wAq@{r}:{c}" for (r, c) in
                             [((k * 4 + i) // 4, (k * 4 + i) % 4) for i in range(4)])
                P.dma(f"dw{k % 2}", (lambda y, kk=k: y.dma_start(
                    AA[:, kk * 4096:(kk + 1) * 4096],
                    wcA[:, kk * 4096:(kk + 1) * 4096])),
                      reads=("wc:w1",), writes=regs)
            for k in range(8):
                regs = tuple(f"wB@{r}:0" for r in range(k * 4, k * 4 + 4))
                P.dma(f"dw{k % 2}", (lambda y, kk=k: y.dma_start(
                    AB[:, kk * 4096:(kk + 1) * 4096],
                    wcB[:, kk * 4096:(kk + 1) * 4096])),
                      reads=("wc:w2",), writes=regs)
        else:
            load_weights([("w1", 8, 4096, "wAq", AA, 0),
                          ("w2", 32, 1024, "wB", AB, 0)],
                         slot1=(ABF[:, 15360:16384], ("wB@30:0", "wB@31:0"),
                                (lambda dn, rc, cp: dn == "w2" and rc >= 30)))
            if wmode == "store":
                for k in range(8):
                    regs = tuple(f"wAq@{r}:{c}" for (r, c) in
                                 [((k * 4 + i) // 4, (k * 4 + i) % 4)
                                  for i in range(4)])
                    P.dma(f"dwc{k % 2}", (lambda y, kk=k: y.dma_start(
                        wcA[:, kk * 4096:(kk + 1) * 4096],
                        AA[:, kk * 4096:(kk + 1) * 4096])),
                          reads=regs, writes=("wcA",))
                for k in range(8):
                    regs = tuple(f"wB@{r}:0" for r in range(k * 4, k * 4 + 4))
                    P.dma(f"dwc{k % 2}", (lambda y, kk=k: y.dma_start(
                        wcB[:, kk * 4096:(kk + 1) * 4096],
                        AB[:, kk * 4096:(kk + 1) * 4096])),
                          reads=regs, writes=("wcB",))
        for s in range(nsl):
            sl, xb, xreg = fctx(s)
            for j in range(NJ):
                pb = j % 2
                pairs = []
                for dc in range(NCH):
                    lh = AA[:, dc * 4096 + j * 128: dc * 4096 + j * 128 + 128]
                    rh = LNB[:, dc * T:(dc + 1) * T]
                    pairs.append((lh, rh, dc == 0, dc == 7, PS[pb][:, 0:T]))
                MMg(None, pairs, reads=wp("wAq", j * 128, 1024) + ("lnb", f"PS{pb}"), writes=(f"PS{pb}",))
                hj = BIGB[:, j * T:(j + 1) * T]
                P.op("s", (lambda s_, d=hj, pbb=pb, jj=j: s_.activation(
                    d, PS[pbb][:, 0:T], AF.Gelu, bias=B1C[:, jj:jj + 1])),
                     reads=(f"PS{pb}", "c_b1"), writes=("hsb",) if j == 0 else ("hsb#",))
            if s + 1 < nsl:
                fpre(s + 1)
            if s + 3 < nsl:
                prepass(s + 3)
            for doc in range(NCH):
                pb = (2, 3, 6, 7)[doc % 4]
                pairs = []
                for j in range(NJ):
                    lh = AB[:, j * 1024 + doc * 128: j * 1024 + doc * 128 + 128]
                    rh = BIGB[:, j * T:(j + 1) * T]
                    pairs.append((lh, rh, j == 0, j == NJ - 1, PS[pb][:, 0:T]))
                MMg(None, pairs,
                    reads=tuple(f"wB@{r2}:0" for r2 in range(32)) +
                          ("hsb", "hsb#", f"PS{pb}"),
                    writes=(f"PS{pb}",))
                xv = xb[:, doc * T:(doc + 1) * T]
                P.op("v", (lambda v, d=xv, pbb=pb, dd=doc: v.scalar_tensor_tensor(
                    d, PS[pbb][:, 0:T], B2C[:, dd:dd + 1], d, ALU.add, ALU.add)),
                     reads=(f"PS{pb}", "c_b2", xreg), writes=(xreg,))
            dstd = dst_dram.rearrange("(c p) t -> p c t", p=128)[:, :, s * T:(s + 1) * T]
            srcs = xb.rearrange("p (c t) -> p c t", c=NCH)
            P.dma(f"ds{s % 2}", (lambda y, d=dstd, so=srcs: y.dma_start(d, so)),
                  reads=(xreg,), writes=(f"{dst_pref}:{s}",))

    def phase_attn_b(src_dram, src_pref, dst_dram, dst_pref, wmode='plain'):
        T = 512
        for i, base in enumerate((0, 32, 64)):
            g = BVE[:, base:base + 8]; b = BVE[:, base + 8:base + 16]
            m = BVE[:, base + 16:base + 24]; vv = BVE[:, base + 24:base + 32]
            al = ALF[:, i * 16:i * 16 + 8]; be = ALF[:, i * 16 + 8:i * 16 + 16]
            tmp = ALF[:, 48:56]
            P.op("s", (lambda s_, d=tmp, so=vv: s_.activation(d, so, AF.Ln,
                                                              bias=CV[:, 2:3])),
                 reads=("c_bv", "c_cv"), writes=("alf_t",))
            P.op("s", (lambda s_, d=tmp: s_.activation(d, d, AF.Exp, scale=-0.5)),
                 reads=("alf_t",), writes=("alf_t",))
            P.op("v", (lambda v, d=al, gg=g, t=tmp: v.scalar_tensor_tensor(
                d, gg, 4.0, t, ALU.mult, ALU.mult)),
                 reads=("c_bv", "alf_t"), writes=(f"alf{i}",))
            P.op("v", (lambda v, d=tmp, a=al, mm=m: v.tensor_tensor(d, a, mm, ALU.mult)),
                 reads=(f"alf{i}", "c_bv"), writes=("alf_t",))
            P.op("v", (lambda v, d=be, bb=b, t=tmp: v.scalar_tensor_tensor(
                d, bb, 4.0, t, ALU.mult, ALU.subtract)),
                 reads=("c_bv", "alf_t"), writes=(f"alf{i}b",))
            P.op("v", (lambda v, d=be: v.tensor_scalar(d, d, 4.0, None, ALU.subtract)),
                 reads=(f"alf{i}b",), writes=(f"alf{i}b",))
        nsl = S // T

        def bctx(s):
            sl = s % 2
            xb = (XB0 if sl == 0 else XB1).ap()
            return sl, xb, (f"xs{2*sl}", f"xs{2*sl+1}")

        def qk_sums(s):
            for (ii, woff, psb) in ((0, 0, 2), (1, 8192, 3)):
                wreg = "wAq" if ii == 0 else "wAk"
                for ec in range(NCH):
                    pb = ec % 2
                    pairs = []
                    for dc in range(NCH):
                        lh = AA[:, woff + dc * 1024 + ec * 128:
                                woff + dc * 1024 + ec * 128 + 128]
                        rh = LNB[:, dc * T:(dc + 1) * T]
                        pairs.append((lh, rh, dc == 0, dc == 7, PS[pb][:, 0:T]))
                    MMg(None, pairs, reads=wp(wreg, 0) + ("lnb", f"PS{pb}"),
                        writes=(f"PS{pb}",))
                    qs = BIG[:, (ec % 2) * 512:(ec % 2) * 512 + 512]
                    qreg = f"qsig{ec % 2}"
                    wl = (qreg,)
                    if ii == 0 and ec == 0:
                        wl = wl + ("vfm", "vfm#")
                    P.op("s", (lambda s_, d=qs, pbb=pb, i2=ii, e2=ec: s_.activation(
                        d, PS[pbb][:, 0:T], AF.Sigmoid,
                        bias=ALF[:, i2 * 16 + 8 + e2:i2 * 16 + 9 + e2],
                        scale=ALF[:, i2 * 16 + e2:i2 * 16 + 1 + e2])),
                         reads=(f"PS{pb}", f"alf{ii}", f"alf{ii}b"), writes=wl)
                    P.op("t", (lambda t, o=PS[psb][:, 0:T], q=qs, st=(ec == 0),
                               sp=(ec == NCH - 1):
                               t.matmul(o, CONF[:, :], q, start=st, stop=sp)),
                         reads=(qreg, "c_onesf", f"PS{psb}"), writes=(f"PS{psb}",))
            P.op("s", (lambda e, ss=s: e.copy(SQ[:, ss * T:(ss + 1) * T],
                                              PS[2][:, 0:T])),
                 reads=("PS2",), writes=("sq",) if s == 0 else ("sq#",))
            krow = BIG[0:1, 1024:1024 + T]
            P.op("s", (lambda s_, d=krow: s_.copy(d, PS[3][0:1, 0:T])),
                 reads=("PS3",), writes=("krow",))
            pairs = []
            for jseg in range(T // 128):
                pairs.append((krow[0:1, jseg * 128:(jseg + 1) * 128], ONEF[:, :],
                              jseg == 0, jseg == T // 128 - 1,
                              PS[3][:, jseg:jseg + 1]))
            MMg(None, pairs, reads=("krow", "c_onef", "PS3"), writes=("PS3",))
            P.op("s", (lambda e, ss=s: e.copy(
                SKC[:, ss * (T // 128):(ss + 1) * (T // 128)],
                PS[3][:, 0:T // 128])),
                 reads=("PS3",), writes=("skc",) if s == 0 else ("skc#",))

        def v_proj(s):
            for ec in range(NCH):
                pb = ec % 2
                pairs = []
                for dc in range(NCH):
                    lh = AA[:, 16384 + dc * 1024 + ec * 128:
                            16384 + dc * 1024 + ec * 128 + 128]
                    rh = LNB[:, dc * T:(dc + 1) * T]
                    pairs.append((lh, rh, dc == 0, dc == 7, PS[pb][:, 0:T]))
                MMg(None, pairs, reads=wp("wAv", 0) + ("lnb", f"PS{pb}"),
                    writes=(f"PS{pb}",))
                vf = BIGB[:, ec * 512:(ec + 1) * 512]
                wl = ("vfm",) if ec == 0 else ("vfm#",)
                if ec == 0:
                    wl = wl + ("qsig0", "qsig1", "krow", "lnm2", "ln_mb",
                               "ln_rs", "ln_tmp", "ln_tmp2")
                P.op("s", (lambda s_, d=vf, pbb=pb, e2=ec: s_.activation(
                    d, PS[pbb][:, 0:T], AF.Sigmoid,
                    bias=ALF[:, 40 + e2:41 + e2], scale=ALF[:, 32 + e2:33 + e2])),
                     reads=(f"PS{pb}", "alf2", "alf2b"), writes=wl)

        def vw_mm(s):
            for t4 in range(4):
                for doh in range(2):
                    pb = 6 + (t4 * 2 + doh) % 2
                    pairs = []
                    for ec in range(NCH):
                        lh = BIGB[:, ec * 512 + t4 * 128: ec * 512 + t4 * 128 + 128]
                        rh = AA[:, 24576 + ec * 1024 + doh * 512:
                                24576 + ec * 1024 + (doh + 1) * 512]
                        pairs.append((lh, rh, ec == 0, ec == 7, PS[pb][:, 0:512]))
                    MMg(None, pairs, reads=("vfm", "vfm#") + wp("wAo", 0) +
                        (f"PS{pb}",), writes=(f"PS{pb}",))
                    tc = s * 4 + t4
                    dstv = AB[:, tc * 1024 + doh * 512: tc * 1024 + (doh + 1) * 512]
                    P.op("s", (lambda e, d=dstv, pbb=pb: e.copy(
                        d, PS[pbb][:, 0:512])),
                         reads=(f"PS{pb}",), writes=("VW",) if (s == 0 and t4 == 0
                                                               and doh == 0)
                         else ("VW#",))

        def bload_ln(s, mid=None):
            sl, xb, xregs = bctx(s)
            src = src_dram.rearrange("(c p) t -> p c t", p=128)[
                :, :, s * T:(s + 1) * T]
            dst = xb.rearrange("p (c t) -> p c t", c=NCH)
            P.dma(f"dx{sl}", (lambda y, d=dst, so=src: y.dma_start(d, so)),
                  reads=(f"{src_pref}:{2*s}", f"{src_pref}:{2*s+1}"),
                  writes=(xregs[0],))
            P.reg[xregs[1]] = dict(P.reg[xregs[0]])
            ln_pass(T, xb, xregs, LNB.ap(), "lnb", 4, 5, 0,
                    mid=mid, x2_bb=7168, split_apply=False,
                    math_war=("qsig0", "qsig1", "krow", "vfm", "vfm#"),
                    apply_war=("vfm", "vfm#"), rsqrt_dve=True)

        bload_ln(0)
        if wmode == "load":
            P.op("v", (lambda v: v.memset(K8[0:1, 0:1], 0.0)),
                 reads=(), writes=tuple(ARENA_REGS))
            for wi, wreg in enumerate(("wAq", "wAk", "wAv", "wAo")):
                for half in range(2):
                    k = wi * 2 + half
                    regs = tuple(f"{wreg}@{r}:0" for r in
                                 range(half * 4, half * 4 + 4))
                    off = wi * 8192 + half * 4096
                    P.dma(f"dw{k % 2}", (lambda y, o=off: y.dma_start(
                        AA[:, o:o + 4096], wcC[:, o:o + 4096])),
                          reads=(f"wc:{('bq','bk','bv','bo')[wi]}",),
                          writes=regs)
        else:
            load_weights([("bq", 8, 1024, "wAq", AA, 0),
                          ("bk", 8, 1024, "wAk", AA, 8192),
                          ("bv", 8, 1024, "wAv", AA, 16384),
                          ("bo", 8, 1024, "wAo", AA, 24576)])
        for s in range(nsl):
            sl, xb, xregs = bctx(s)
            qk_sums(s)
            v_proj(s)
            if s + 1 < nsl:
                bload_ln(s + 1, mid=lambda s2=s: vw_mm(s2))
            else:
                vw_mm(s)
        pairs = []
        for doc in range(NCH):
            for jc in range(NTC):
                lh = AB[:, jc * 1024 + doc * 128: jc * 1024 + doc * 128 + 128]
                pairs.append((lh, CON[:, 0:1], (doc == 0 and jc == 0),
                              (doc == NCH - 1 and jc == NTC - 1),
                              PS[0][:, doc:doc + 1]))
        MMg(None, pairs, reads=("VW", "VW#", "c_ones", "PS0"), writes=("PS0",))
        P.op("v", (lambda v: v.scalar_tensor_tensor(
            CSB[:, :], PS[0][:, 0:NCH], 0.9, BOC[:, :], ALU.mult, ALU.add)),
             reads=("PS0", "c_boc"), writes=("csb",))
        nib = S // T
        for ib in range(nib):
            sl = ib % 2
            xb = (XB0 if sl == 0 else XB1).ap()
            xregs = (f"xs{2*sl}", f"xs{2*sl+1}")
            src = src_dram.rearrange("(c p) t -> p c t", p=128)[:, :, ib * T:(ib + 1) * T]
            dstx = xb.rearrange("p (c t) -> p c t", c=NCH)
            P.dma(f"dx{sl}", (lambda y, d=dstx, so=src: y.dma_start(d, so)),
                  reads=(f"{src_pref}:{2*ib}", f"{src_pref}:{2*ib+1}"),
                  writes=(xregs[0],))
            P.reg[xregs[1]] = dict(P.reg[xregs[0]])
            for jc in range(NTC):
                pu = jc % 2
                u = BIG[:, 512 + pu * 512: 1024 + pu * 512]
                ureg = f"u{pu}"
                f2 = BIG[:, 1536 + pu * 512: 2048 + pu * 512]
                f2reg = f"f2{pu}"
                mf = BIG[:, 2560:3072]
                at = BIGB[:, pu * 512:(pu + 1) * 512]
                atreg = f"at{pu}"
                sqv = SQ[:, ib * T:(ib + 1) * T]
                skcol = SKC[:, jc:jc + 1]
                du = BIG[:, 3072:3584] if pu == 0 else BIG[:, 3584:4096]
                dreg = f"du{pu}"
                P.op("v", (lambda v, d=du, sv=sqv, sk=skcol: v.tensor_scalar(
                    d, sv, sk, None, ALU.subtract)),
                     reads=("sq", "sq#", "skc", "skc#"), writes=(dreg,))
                P.op("s", (lambda s_, d=u, so=du: s_.activation(
                    d, so, AF.Abs, scale=2.0 / 1024.0)),
                     reads=(dreg,), writes=(ureg,))
                P.op("s", (lambda s_, d=f2, so=u: s_.activation(
                    d, so, AF.Exp, bias=CV[:, 0:1], scale=-1.0)),
                     reads=(ureg, "c_cv"), writes=(f2reg,))
                P.op("v", (lambda v, d=mf, sv=sqv, sk=skcol, f=f2:
                           v.scalar_tensor_tensor(d, sv, sk, f, ALU.is_gt, ALU.mult)),
                     reads=("sq", "sq#", "skc", "skc#", f2reg), writes=("mf",))
                P.op("v", (lambda v, d=at, m=mf, f=f2: v.scalar_tensor_tensor(
                    d, m, 2.0, f, ALU.mult, ALU.subtract)),
                     reads=("mf", f2reg), writes=(atreg,))
                pairs = []
                for doc in range(NCH):
                    lh = AB[:, jc * 1024 + doc * 128: jc * 1024 + doc * 128 + 128]
                    pairs.append((lh, at, jc == 0, jc == NTC - 1,
                                  PS[doc][:, 0:T]))
                MMg(None, pairs,
                    reads=("VW", "VW#", atreg) + tuple(f"PS{d2}" for d2 in range(8)),
                    writes=tuple(f"PS{d2}" for d2 in range(8)))
            for doc in range(NCH):
                xv = xb[:, doc * T:(doc + 1) * T]
                P.op("v", (lambda v, d=xv, dd=doc: v.scalar_tensor_tensor(
                    d, PS[dd][:, 0:T], CSB[:, dd:dd + 1], d, ALU.add, ALU.add)),
                     reads=(f"PS{doc}", "csb", xregs[0], xregs[1]),
                     writes=(xregs[0], xregs[1]))
            dstd = dst_dram.rearrange("(c p) t -> p c t", p=128)[:, :, ib * T:(ib + 1) * T]
            srcs = xb.rearrange("p (c t) -> p c t", c=NCH)
            P.dma(f"ds{sl}", (lambda y, d=dstd, so=srcs: y.dma_start(d, so)),
                  reads=(xregs[0],), writes=(f"{dst_pref}:{2*ib}",))
            P.reg[f"{dst_pref}:{2*ib+1}"] = dict(P.reg[f"{dst_pref}:{2*ib}"])

    chain = [p for p in ("attn_a", "ffn1", "attn_b", "ffn2") if p in phases]

    bufs = [xB, xC]
    cur_src, cur_pref = dr["xT"], None
    for i, name in enumerate(chain):
        last = i == len(chain) - 1
        dst = out if last else bufs[i % 2]
        dst_pref = "OUT" if last else ("hxB" if i % 2 == 0 else "hxC")
        has_a = "attn_a" in chain
        if name == "attn_a":
            phase_attn_a(cur_src, cur_pref, dst, dst_pref)
        elif name.startswith("ffn"):
            phase_ffn(cur_src, cur_pref, dst, dst_pref, final=last,
                      wmode="load" if has_a else "plain")
        else:
            phase_attn_b(cur_src, cur_pref, dst, dst_pref,
                         wmode="load" if has_a else "plain")
        cur_src, cur_pref = dst, dst_pref

    if DBG:
        P.dma("dbg", lambda y: y.dma_start(dbg1, SQ[:, 0:S]), reads=("sq", "sq#"))
        P.dma("dbg", lambda y: y.dma_start(dbg2, SKC[:, 0:NTC]), reads=("skc", "skc#"))
        P.dma("dbg", lambda y: y.dma_start(dbg3, AB[:, 0:NTC * 1024]), reads=("VW", "VW#"))
    P.final_waits("y")
    P.finalize()

    with nc.Block() as block:
        @block.tensor
        def _(t):
            P.run_engine("t", t)

        @block.vector
        def _(v):
            P.run_engine("v", v)

        @block.scalar
        def _(s):
            P.run_engine("s", s)

        @block.gpsimd
        def _(p):
            P.run_engine("p", p)

        @block.sync
        def _(y):
            P.run_engine("y", y)
    es.close()
    return nc


_NCACHE = {}


def _get_nc(S, phases):
    key = (S, tuple(phases))
    if key not in _NCACHE:
        _NCACHE[key] = build_nc(S, phases)
    return _NCACHE[key]


def _vec_cols(v, n):
    return np.ascontiguousarray(v.reshape(n, 128).T).astype(np.float32)


PHASES = ("attn_a", "ffn1", "attn_b", "ffn2")
RUNNER = None


def kernel(**inputs):
    import ml_dtypes
    phases = PHASES
    x = np.asarray(inputs["L0_attn_a_hidden_states"], dtype=np.float32)
    B = x.shape[0]
    S = x.shape[1]
    nc = _get_nc(S, phases)

    aq = np.ascontiguousarray(np.asarray(inputs["L0_attn_a_Wq"], np.float32).T)
    ak = np.ascontiguousarray(np.asarray(inputs["L0_attn_a_Wk"], np.float32).T)
    av = np.ascontiguousarray(np.asarray(inputs["L0_attn_a_Wv"], np.float32).T)
    ao = np.ascontiguousarray(np.asarray(inputs["L0_attn_a_Wo"], np.float32).T)
    anw = np.ascontiguousarray(
        np.asarray(inputs["L0_attn_a_norm_w"], np.float32).reshape(128, 1))
    w1 = np.ascontiguousarray(np.asarray(inputs["L0_ffn_w1"], np.float32))
    w2 = np.ascontiguousarray(np.asarray(inputs["L0_ffn_w2"], np.float32))
    b1c = _vec_cols(np.asarray(inputs["L0_ffn_b1"], np.float32), NJ)
    b2c = _vec_cols(np.asarray(inputs["L0_ffn_b2"], np.float32), NCH)
    bq = np.ascontiguousarray(np.asarray(inputs["L0_attn_b_Wq"], np.float32))
    bk = np.ascontiguousarray(np.asarray(inputs["L0_attn_b_Wk"], np.float32))
    bv = np.ascontiguousarray(np.asarray(inputs["L0_attn_b_Wv"], np.float32))
    bo = np.ascontiguousarray(np.asarray(inputs["L0_attn_b_Wo"], np.float32))
    bvecs = np.concatenate(
        [_vec_cols(np.asarray(inputs[f"L0_attn_b_{nm}"], np.float32), 8)
         for nm in ("gq", "bq", "mq", "vq", "gk", "bk", "mk", "vk",
                    "gv", "bv", "mv", "vv")], axis=1)
    boc = _vec_cols(np.asarray(inputs["L0_attn_b_bo"], np.float32), NCH)
    onesall = np.ones((128, 128), dtype=ml_dtypes.bfloat16)
    tk = np.arange(128)[:, None]
    tq = np.arange(128)[None, :]
    m128 = ((tk // 64 < tq // 64) | ((tk // 64 == tq // 64) & (tk <= tq))
            ).astype(np.float32)
    maskt = np.tile(m128, (1, 4)).astype(ml_dtypes.bfloat16)
    cvals = np.tile(np.array([[LN09, LN_EPS, A_EPS, RMS_EPS, 0, 0, 0, 0]],
                             np.float32), (128, 1))
    onef = np.ones((1, 1), np.float32)

    shared = dict(aq=aq, ak=ak, av=av, ao=ao, anw=anw, w1=w1, w2=w2, b1c=b1c,
                  b2c=b2c, bq=bq, bk=bk, bv=bv, bo=bo, bvecs=bvecs, boc=boc,
                  onesall=onesall, maskt=maskt, cvals=cvals, onef=onef,
                  onesf32=np.ones((128, 128), np.float32))
    in_maps = []
    for b in range(B):
        m = dict(shared)
        m["xT"] = np.ascontiguousarray(x[b].T)
        in_maps.append(m)
    runner = RUNNER if RUNNER is not None else run_bass_kernel_spmd
    res = runner(nc, in_maps, core_ids=list(range(B)))
    outs = [np.asarray(r["outT"]).T for r in res.results]
    return np.stack(outs, axis=0).astype(np.float32)


# version 23
# speedup vs baseline: 1.0636x; 1.0063x over previous
import numpy as np
from contextlib import ExitStack

import concourse.bass as bass
import concourse.mybir as mybir
from concourse.bass_utils import run_bass_kernel_spmd

F32 = mybir.dt.float32
BF16 = mybir.dt.bfloat16
AF = mybir.ActivationFunctionType
ALU = mybir.AluOpType
AX = mybir.AxisListType

D = 1024
NCH = 8
H = 8
DH = 128
CH = 64
FF = 4096
NJ = 32
LN_EPS = 1e-5
A_EPS = 1e-5
RMS_EPS = 1e-5
LN09 = float(np.log(0.9))

ENGS = ("t", "v", "s", "p", "y")


class Prog:

    def __init__(self, nc, es):
        self.nc = nc
        self.es = es
        self.ops = {e: [] for e in ENGS}
        self.seq = {e: 0 for e in ENGS}
        self.waited = {e: set() for e in ENGS}
        self.dma_cnt = {}
        self.dma_h = {}
        self.eng_h = {}
        self.reg = {}
        for e in ENGS:
            self.eng_h[e] = es.enter_context(nc.semaphore(f"se_{e}"))

    def dma_sem(self, name):
        if name not in self.dma_h:
            self.dma_h[name] = self.es.enter_context(self.nc.semaphore(f"sd_{name}"))
            self.dma_cnt[name] = 0
        return self.dma_h[name]

    def _dep(self, name):
        if name not in self.reg:
            self.reg[name] = {"w": None, "r": []}
        return self.reg[name]

    def _collect(self, reads, writes):
        waits = []
        for r in reads:
            d = self._dep(r)
            if d["w"] is not None:
                waits.append(d["w"])
        for w in writes:
            d = self._dep(w)
            if d["w"] is not None:
                waits.append(d["w"])
            waits.extend(d["r"])
        best = {}
        for kind, key, val in waits:
            k = (kind, key)
            if k not in best or best[k] < val:
                best[k] = val
        return [(kind, key, val) for (kind, key), val in best.items()]

    def _record(self, tok, reads, writes):
        for r in reads:
            self._dep(r)["r"].append(tok)
        for w in writes:
            self.reg[w] = {"w": tok, "r": []}

    def op(self, eng, fn, reads=(), writes=(), extra=()):
        waits = self._collect(reads, writes) + list(extra)
        my = []
        for kind, key, val in waits:
            if kind == "eng":
                if key == eng:
                    self.waited[key].add(val)
                else:
                    self.waited[key].add(val)
            my.append((kind, key, val))
        self.seq[eng] += 1
        seq = self.seq[eng]
        self.ops[eng].append({"fn": fn, "waits": my, "seq": seq})
        self._record(("eng", eng, seq), reads, writes)
        return seq

    def dma(self, sem_name, fn, reads=(), writes=()):
        self.dma_sem(sem_name)
        prev = self.dma_cnt[sem_name]
        waits = self._collect(reads, writes)
        if prev > 0:
            waits.append(("dma", sem_name, prev))
        for kind, key, val in waits:
            if kind == "eng":
                self.waited[key].add(val)
        self.dma_cnt[sem_name] = prev + 16
        val = self.dma_cnt[sem_name]
        self.seq["y"] += 1
        self.ops["y"].append(
            {"fn": fn, "waits": waits, "seq": None, "dma": (sem_name, val)}
        )
        self._record(("dma", sem_name, val), reads, writes)
        return val

    def final_waits(self, eng="y"):
        def mk(name, val):
            def run(e, name=name, val=val):
                e.wait_ge(self.dma_h[name], val)
            return run
        for name, val in self.dma_cnt.items():
            if val > 0:
                self.ops[eng].append({"fn": mk(name, val), "waits": (), "seq": None,
                                      "raw": True})

    def finalize(self):
        remap = {}
        for e in ENGS:
            wl = sorted(self.waited[e])
            remap[e] = {v: i + 1 for i, v in enumerate(wl)}
        self.real_inc = {e: set(self.waited[e]) for e in ENGS}
        self.remap = remap

    def run_engine(self, e, handle):
        already = {}
        for rec in self.ops[e]:
            if rec.get("raw"):
                rec["fn"](handle)
                continue
            for kind, key, val in rec["waits"]:
                if kind == "eng":
                    rv = self.remap[key].get(val)
                    if rv is None:
                        continue
                    k = ("e", key)
                    if already.get(k, 0) >= rv:
                        continue
                    already[k] = rv
                    handle.wait_ge(self.eng_h[key], rv)
                else:
                    k = ("d", key)
                    if already.get(k, 0) >= val:
                        continue
                    already[k] = val
                    handle.wait_ge(self.dma_h[key], val)
            inst = rec["fn"](handle)
            if rec.get("dma"):
                name, val = rec["dma"]
                inst.then_inc(self.dma_h[name], 16)
            elif rec["seq"] in self.real_inc[e]:
                inst.then_inc(self.eng_h[e], 1)


def ceil_div(a, b):
    return -(-a // b)


def build_nc(S=4096, phases=("attn_a", "ffn1", "attn_b", "ffn2")):
    nc = bass.Bass("TRN2", target_bir_lowering=False)
    NS = S // 512
    NS2 = S // 256
    NCHK = S // CH
    NTC = S // 128

    dr = {}
    def din(name, shape):
        dr[name] = nc.dram_tensor(name, shape, F32, kind="ExternalInput").ap()
    def dinb(name, shape):
        dr[name] = nc.dram_tensor(name, shape, BF16, kind="ExternalInput").ap()

    din("xT", [128, NCH * S])
    for w in ("aq", "ak", "av", "ao"):
        din(w, [D, D])
    din("anw", [128, 1])
    din("w1", [D, FF]); din("w2", [FF, D])
    din("b1c", [128, NJ]); din("b2c", [128, NCH])
    for w in ("bq", "bk", "bv", "bo"):
        din(w, [D, D])
    din("bvecs", [128, 96])
    din("boc", [128, NCH])
    dinb("onesall", [128, 128])
    din("onesf32", [128, 128])
    dinb("maskt", [128, 512])
    din("cvals", [128, 8])
    din("onef", [1, 1])
    out = nc.dram_tensor("outT", [128, NCH * S], F32, kind="ExternalOutput").ap()
    import os as _os
    DBG = bool(_os.environ.get("KDBG"))
    if DBG:
        dbg1 = nc.dram_tensor("dbg1", [128, S], F32, kind="ExternalOutput").ap()
        dbg2 = nc.dram_tensor("dbg2", [128, NTC], F32, kind="ExternalOutput").ap()
        dbg3 = nc.dram_tensor("dbg3", [128, NTC * 1024], BF16, kind="ExternalOutput").ap()
    xB = nc.dram_tensor("xBbuf", [128, NCH * S], F32, kind="Internal").ap()
    wcA = nc.dram_tensor("wcA", [128, 32768], BF16, kind="Internal").ap()
    wcB = nc.dram_tensor("wcB", [128, 32768], BF16, kind="Internal").ap()
    wcC = nc.dram_tensor("wcC", [128, 32768], BF16, kind="Internal").ap()
    xC = nc.dram_tensor("xCbuf", [128, NCH * S], F32, kind="Internal").ap()

    es = ExitStack()
    AA = es.enter_context(nc.sbuf_tensor("AA", [128, 32768], BF16))
    AB = es.enter_context(nc.sbuf_tensor("AB", [128, 32768], BF16))
    XB0 = es.enter_context(nc.sbuf_tensor("XB0", [128, 4096], F32))
    XB1 = es.enter_context(nc.sbuf_tensor("XB1", [128, 4096], F32))
    LNB = es.enter_context(nc.sbuf_tensor("LNB", [128, 4096], BF16))
    BIG = es.enter_context(nc.sbuf_tensor("BIG", [128, 4096], F32))
    SQ = es.enter_context(nc.sbuf_tensor("SQ", [128, 4096], F32))
    STG = es.enter_context(nc.sbuf_tensor("STG", [128, 1024], F32))
    CON = es.enter_context(nc.sbuf_tensor("CON", [128, 128], BF16))
    CONF = es.enter_context(nc.sbuf_tensor("CONF", [128, 128], F32))
    MSK = es.enter_context(nc.sbuf_tensor("MSK", [128, 512], BF16))
    CV = es.enter_context(nc.sbuf_tensor("CV", [128, 8], F32))
    NW = es.enter_context(nc.sbuf_tensor("NW", [128, 1], F32))
    B1C = es.enter_context(nc.sbuf_tensor("B1C", [128, NJ], F32))
    B2C = es.enter_context(nc.sbuf_tensor("B2C", [128, NCH], F32))
    BVE = es.enter_context(nc.sbuf_tensor("BVE", [128, 96], F32))
    BOC = es.enter_context(nc.sbuf_tensor("BOC", [128, NCH], F32))
    ALF = es.enter_context(nc.sbuf_tensor("ALF", [128, 64], F32))
    CAR = es.enter_context(nc.sbuf_tensor("CAR", [128, 8], F32))
    CARB = es.enter_context(nc.sbuf_tensor("CARB", [128, 8], BF16))
    K8 = es.enter_context(nc.sbuf_tensor("K8", [128, 8], F32))
    SKC = es.enter_context(nc.sbuf_tensor("SKC", [128, 32], F32))
    CSB = es.enter_context(nc.sbuf_tensor("CSB", [128, NCH], F32))
    ONEF = es.enter_context(nc.sbuf_tensor("ONEF", [1, 1], F32))
    PS = [es.enter_context(nc.psum_tensor(f"PS{i}", [128, 512], F32))
          for i in range(8)]

    BIGB = BIG.ap().bitcast(BF16)
    SQB = SQ.ap().bitcast(BF16)

    P = Prog(nc, es)

    def wp(reg, col, ncol=1024, dcs=range(8)):
        cp = col // ncol
        return tuple(f"{reg}@{dc}:{cp}" for dc in dcs)

    def MMg(psum_ap, pairs, reads, writes, starts=None):
        def fn(t, psum_ap=psum_ap, pairs=pairs):
            inst = None
            for (lh, rh, st, sp, oap) in pairs:
                inst = t.matmul(oap if oap is not None else psum_ap, lh, rh,
                                start=st, stop=sp)
            return inst
        return P.op("t", fn, reads=reads, writes=writes)

    P.dma("dc", lambda y: y.dma_start(CON[:, :], dr["onesall"]), writes=("c_ones",))
    P.dma("dc", lambda y: y.dma_start(CONF[:, :], dr["onesf32"]), writes=("c_onesf",))
    P.dma("dc", lambda y: y.dma_start(MSK[:, :], dr["maskt"]), writes=("c_mask",))
    P.dma("dc", lambda y: y.dma_start(CV[:, :], dr["cvals"]), writes=("c_cv",))
    P.dma("dc", lambda y: y.dma_start(NW[:, :], dr["anw"]), writes=("c_nw",))
    P.dma("dc", lambda y: y.dma_start(B1C[:, :], dr["b1c"]), writes=("c_b1",))
    P.dma("dc", lambda y: y.dma_start(B2C[:, :], dr["b2c"]), writes=("c_b2",))
    P.dma("dc", lambda y: y.dma_start(BVE[:, :], dr["bvecs"]), writes=("c_bv",))
    P.dma("dc", lambda y: y.dma_start(BOC[:, :], dr["boc"]), writes=("c_boc",))
    P.dma("dc", lambda y: y.dma_start(ONEF[:, :], dr["onef"]), writes=("c_onef",))

    ARENA_REGS = (
        ["qfm", "kfm", "ktm", "vtm", "on", "stateB0", "stateB1", "VW", "VW#"]
        + [f"wAq@{r}:{c}" for r in range(8) for c in range(4)]
        + [f"wAk@{r}:0" for r in range(8)]
        + [f"wAv@{r}:0" for r in range(8)]
        + [f"wAo@{r}:0" for r in range(8)]
        + [f"wB@{r}:0" for r in range(32)]
        + ["wps0", "wps1", "wcb0", "wcb1"]
    )

    ABF_T = AB.ap().bitcast(F32)

    def wprep_piece(dname, rc, cache, coff, k):
        slot = k % 2
        stg = ABF_T[:, 12288 + slot * 1024: 13312 + slot * 1024]
        bnc = AB[:, 22528 + slot * 1024: 23552 + slot * 1024]
        sreg, breg = f"wps{slot}", f"wcb{slot}"
        src = dr[dname][rc * 128:(rc + 1) * 128, :]
        P.dma(f"dwp{slot}", (lambda y, d=stg, so=src: y.dma_start(d, so)),
              reads=(), writes=(sreg,))
        P.op("p", (lambda p, d=bnc, so=stg: p.tensor_copy(d, so)),
             reads=(sreg,), writes=(breg,))
        dst = cache[:, coff + rc * 1024: coff + (rc + 1) * 1024]
        P.dma(f"dwo{slot}", (lambda y, d=dst, so=bnc: y.dma_start(d, so)),
              reads=(breg,), writes=(f"wc:{dname}",))

    def load_weights(src_list, slot1=None):
        P.op("v", (lambda v: v.memset(K8[0:1, 0:1], 0.0)),
             reads=(), writes=tuple(ARENA_REGS))
        si = 0
        AL1 = ("lnm2", "ln_mb", "ln_rs", "x2a", "x2b", "ln_tmp",
               "qsig0", "qsig1", "at0", "at1", "u0")
        if slot1 is None:
            s1_ap, s1_al, s1_tail = BIG, AL1, (lambda dn, rc, cp: False)
        else:
            s1_ap, s1_al, s1_tail = slot1
        for (dname, nrow, ncol, reg, dtile, coff) in src_list:
            npc = ceil_div(ncol, 1024)
            pc = ncol // npc
            for cp in range(npc):
                for rc in range(nrow):
                    slot = si % 2
                    si += 1
                    if s1_tail(dname, rc, cp):
                        slot = 0
                    stg_ap = (STG[:, 0:pc], s1_ap[:, 0:pc])[slot]
                    sreg = f"stg{slot}"
                    al = (() if slot == 0 else tuple(s1_al))
                    src = dr[dname][rc * 128:(rc + 1) * 128,
                                    cp * pc:(cp + 1) * pc]
                    P.dma(f"dw{slot}",
                          (lambda y, d=stg_ap, s=src: y.dma_start(d, s)),
                          reads=(), writes=(sreg,) + al)
                    off = coff + rc * ncol + cp * pc
                    dst = dtile[:, off: off + pc]
                    P.op("v", (lambda e, d=dst, s=stg_ap: e.tensor_copy(d, s)),
                         reads=(sreg,) + al, writes=(f"{reg}@{rc}:{cp}",))


    def ln_pass(T, xap, xregs, lnap, lnreg, ps_m, ps_s, sb, mr_store=None,
                extra_war=(), mid=None, x2_bb=None, split_apply=False,
                math_war=(), apply_war=(), rsqrt_dve=False):
        pm, psx = f"PS{ps_m}", f"PS{ps_s}"
        if sb == "ffnpre":
            LNBF2 = LNB.ap().bitcast(F32)
            lnm2 = LNBF2[:, 1152:1152 + T]
            x2a = LNB[:, 2816:2816 + T]
            x2b = LNB[:, 2816 + T:2816 + 2 * T]
            bb = None
        else:
            lnm2 = BIG[:, sb:sb + T]
            bb = 2 * (sb + 3 * T) if x2_bb is None else x2_bb
            x2a = BIGB[:, bb:bb + T]
            x2b = BIGB[:, bb + T:bb + 2 * T]
        if bb is None:
            tmp = tmp2 = None
        elif x2_bb is None:
            tmp = BIGB[:, bb + 2 * T:bb + 3 * T]
            tmp2 = tmp
        else:
            tmp = BIGB[:, 3072:3072 + T]
            tmp2 = BIGB[:, 3072 + T:3072 + 2 * T]
        if mr_store is None:
            mb = BIG[:, sb + T:sb + 2 * T]
            rs = BIG[:, sb + 2 * T:sb + 3 * T]
            mbr, rsr = "ln_mb", "ln_rs"
        else:
            mb, rs, mbr, rsr = mr_store
        for c in range(NCH):
            xa = xap[:, c * T:(c + 1) * T]
            if mr_store is None:
                la = lnap[:, c * T:(c + 1) * T]
                wl = (lnreg,) if c == 0 else (lnreg + "#",)
                lrd = ("c_ones", lnreg, lnreg + "#")
            else:
                la = LNB[:, 3328 + (c % 2) * T: 3328 + (c % 2) * T + T]
                wl = ("pxa",) if c % 2 == 0 else ("pxb",)
                lrd = ("c_ones",) + wl
            if c == 0:
                wl = wl + tuple(extra_war)
            P.op("v", (lambda v, d=la, so=xa: v.tensor_copy(d, so)),
                 reads=xregs, writes=wl)
            x2 = x2a if c % 2 == 0 else x2b
            x2r = "x2a" if c % 2 == 0 else "x2b"
            P.op("s", (lambda s_, d=x2, so=xa: s_.activation(d, so, AF.Square)),
                 reads=xregs, writes=(x2r,) + (tuple(extra_war) if c == 0 else ()))
            P.op("t", (lambda t, o=PS[ps_m][:, 0:T], l=la, st=(c == 0),
                       sp=(c == NCH - 1):
                       t.matmul(o, CON[:, :], l, start=st, stop=sp)),
                 reads=lrd, writes=(pm,))
            P.op("t", (lambda t, o=PS[ps_s][:, 0:T], l=x2, st=(c == 0),
                       sp=(c == NCH - 1):
                       t.matmul(o, CON[:, :], l, start=st, stop=sp)),
                 reads=("c_ones", x2r), writes=(psx,))
        if mid is not None:
            mid()
        P.op("v", (lambda v, d=mb: v.tensor_scalar(d, PS[ps_m][:, 0:T],
                                                   1.0 / D, None, ALU.mult)),
             reads=(pm,), writes=(mbr,) + tuple(math_war))
        P.op("v", (lambda v, d=lnm2, m=mb: v.tensor_tensor(d, m, m, ALU.mult)),
             reads=(mbr,), writes=("lnm2",))
        P.op("v", (lambda v, d=lnm2, m=lnm2: v.scalar_tensor_tensor(
            d, PS[ps_s][:, 0:T], 1.0 / D, m, ALU.mult, ALU.subtract)),
             reads=(psx, "lnm2"), writes=("lnm2",))
        if not rsqrt_dve:
            P.op("s", (lambda s_, d=lnm2: s_.activation(d, d, AF.Ln, bias=CV[:, 1:2])),
                 reads=("lnm2", "c_cv"), writes=("lnm2",))
            P.op("s", (lambda s_, d=rs, so=lnm2: s_.activation(d, so, AF.Exp,
                                                               scale=-0.5)),
                 reads=("lnm2",), writes=(rsr,))
        else:
            I32 = mybir.dt.int32
            if sb == "ffnpre":
                LNBF3 = LNB.ap().bitcast(F32)
                yv = LNBF3[:, 1408:1408 + T]
                t1v = LNBF3[:, 1664:1664 + T]
            else:
                yv = BIG[:, 3584:3584 + T]
                t1v = BIG[:, 1536:1536 + T]
            yregs = ("x2a", "x2b")
            tregs = ("pxa", "pxb") if sb == "ffnpre" else ("ln_tmp", "ln_tmp2")
            yi = yv.bitcast(I32)
            li = lnm2.bitcast(I32)
            P.op("v", (lambda v, d=lnm2: v.tensor_scalar(d, d, LN_EPS, None,
                                                         ALU.add)),
                 reads=("lnm2",), writes=("lnm2",))
            P.op("v", (lambda v, d=yi, so=li: v.tensor_scalar(
                d, so, 1, None, ALU.arith_shift_right)),
                 reads=("lnm2",), writes=yregs)
            P.op("v", (lambda v, d=yi: v.tensor_scalar(d, d, 0x5f3759df, None,
                                                       ALU.subtract)),
                 reads=yregs, writes=yregs)
            P.op("v", (lambda v, d=yi: v.tensor_scalar(d, d, -1, None,
                                                       ALU.bitwise_xor)),
                 reads=yregs, writes=yregs)
            P.op("v", (lambda v, d=yi: v.tensor_scalar(d, d, 1, None, ALU.add)),
                 reads=yregs, writes=yregs)
            for it in range(3):
                last = it == 2
                P.op("v", (lambda v, d=t1v, y2=yv: v.tensor_tensor(
                    d, y2, y2, ALU.mult)),
                     reads=yregs, writes=tregs)
                P.op("v", (lambda v, d=t1v, l2=lnm2: v.tensor_tensor(
                    d, d, l2, ALU.mult)),
                     reads=tregs + ("lnm2",), writes=tregs)
                P.op("v", (lambda v, d=t1v: v.tensor_scalar(
                    d, d, -0.5, 1.5, ALU.mult, ALU.add)),
                     reads=tregs, writes=tregs)
                if last:
                    P.op("v", (lambda v, d=rs, y2=yv, t2=t1v: v.tensor_tensor(
                        d, y2, t2, ALU.mult)),
                         reads=yregs + tregs, writes=(rsr,))
                else:
                    P.op("v", (lambda v, d=yv, t2=t1v: v.tensor_tensor(
                        d, d, t2, ALU.mult)),
                         reads=yregs + tregs, writes=yregs)

        if mr_store is None:
            for c in range(NCH):
                eng = "p" if (split_apply and c >= NCH // 2) else "v"
                tm_ = tmp2 if eng == "p" else tmp
                tr_ = "ln_tmp2" if eng == "p" else "ln_tmp"
                aw = tuple(apply_war) if c in (0, NCH // 2) else ()
                xa = xap[:, c * T:(c + 1) * T]
                la = lnap[:, c * T:(c + 1) * T]
                P.op(eng, (lambda e, d=tm_, so=xa, m=mb:
                           e.tensor_tensor(d, so, m, ALU.subtract)),
                     reads=xregs + (mbr, tr_), writes=(tr_,) + aw)
                P.op(eng, (lambda e, d=la, t=tm_, r=rs:
                           e.tensor_tensor(d, t, r, ALU.mult)),
                     reads=(tr_, rsr, lnreg, lnreg + "#"), writes=(lnreg,))

    def x_load(buf_ap, dram_ap, s, T, slot_reg, sem):
        src = dram_ap[:, s * NCH * T:(s + 1) * NCH * T]
        return P.dma(sem, (lambda y, d=buf_ap, so=src: y.dma_start(d, so)),
                     reads=(), writes=(slot_reg,))

    def x_store(buf_ap, dram_ap, s, T, slot_reg, sem, hreg):
        dst = dram_ap[:, s * NCH * T:(s + 1) * NCH * T]
        return P.dma(sem, (lambda y, d=dst, so=buf_ap: y.dma_start(d, so)),
                     reads=(slot_reg,), writes=(hreg,))

    def phase_attn_a(src_dram, src_regs, dst_dram, dst_pref):
        T = 512
        QF, KF, KT, VT, ON, STB = 0, 4096, 8192, 12288, 16384, 20480
        P.op("v", lambda v: v.memset(AB[:, STB:STB + 2048], 0.0), writes=("stateB0", "stateB1"))
        P.op("v", lambda v: v.memset(SQ[:, 0:1024], 0.0), writes=("statef",))
        P.op("v", lambda v: v.memset(CAR[:, :], 0.0), writes=("car32",))
        P.op("v", lambda v: v.memset(CARB[:, :], 0.0), writes=("carb",))

        nsl = S // T

        def sctx(s):
            sl = s % 2
            xb = (XB0 if sl == 0 else XB1).ap()
            return sl, xb, (f"xs{2*sl}", f"xs{2*sl+1}")

        def load_ln(s):
            sl, xb, xregs = sctx(s)
            x_load(xb, src_dram, s, T, xregs[0], f"dx{sl}")
            P.reg[xregs[1]] = dict(P.reg[xregs[0]])
            ln_pass(T, xb, xregs, LNB.ap(), "lnb", 6, 7, 0)

        load_ln(0)
        load_weights([("aq", 8, 1024, "wAq", AA, 0),
                      ("ak", 8, 1024, "wAk", AA, 8192),
                      ("av", 8, 1024, "wAv", AA, 16384),
                      ("ao", 8, 1024, "wAo", AA, 24576)])
        wprep_jobs = []
        for k in range(32):
            rc, cp = k // 4, k % 4
            wprep_jobs.append(("w1pc", (rc, cp)))
        for rc in range(32):
            wprep_jobs.append(("w2", rc))
        for wi, nm in enumerate(("bq", "bk", "bv", "bo")):
            for rc in range(8):
                wprep_jobs.append((nm, (wi, rc)))
        wprep_k = [0]

        def wprep_emit(n):
            for _ in range(n):
                if wprep_k[0] >= len(wprep_jobs):
                    return
                job = wprep_jobs[wprep_k[0]]
                k = wprep_k[0]
                wprep_k[0] += 1
                if job[0] == "w1pc":
                    rc, cp = job[1]
                    slot = k % 2
                    stg = ABF_T[:, 12288 + slot * 1024: 13312 + slot * 1024]
                    bnc = AB[:, 22528 + slot * 1024: 23552 + slot * 1024]
                    sreg, breg = f"wps{slot}", f"wcb{slot}"
                    src = dr["w1"][rc * 128:(rc + 1) * 128,
                                   cp * 1024:(cp + 1) * 1024]
                    P.dma(f"dwp{slot}", (lambda y, d=stg, so=src:
                                         y.dma_start(d, so)),
                          reads=(), writes=(sreg,))
                    P.op("p", (lambda p, d=bnc, so=stg: p.tensor_copy(d, so)),
                         reads=(sreg,), writes=(breg,))
                    dst = wcA[:, rc * 4096 + cp * 1024: rc * 4096 + (cp + 1) * 1024]
                    P.dma(f"dwo{slot}", (lambda y, d=dst, so=bnc:
                                         y.dma_start(d, so)),
                          reads=(breg,), writes=("wc:w1",))
                elif job[0] == "w2":
                    wprep_piece("w2", job[1], wcB, 0, k)
                else:
                    wi, rc = job[1]
                    wprep_piece(job[0], rc, wcC, wi * 8192, k)
        P.op("v", (lambda v: v.tensor_scalar(AA[:, 24576:32768], AA[:, 24576:32768],
                                             NW[:, 0:1], None, ALU.mult)),
             reads=wp("wAo", 0) + ("c_nw",), writes=wp("wAo", 0))
        for s in range(nsl):
            sl, xb, xregs = sctx(s)
            def proj(dst_off, woff, dreg, kind):
                if kind == "fm":
                    for ec in range(NCH):
                        pb = ec % 2
                        pairs = []
                        for dc in range(NCH):
                            lh = AA[:, woff + dc * 1024 + ec * 128:
                                    woff + dc * 1024 + ec * 128 + 128]
                            rh = LNB[:, dc * T:(dc + 1) * T]
                            pairs.append((lh, rh, dc == 0, dc == 7, PS[pb][:, 0:T]))
                        MMg(None, pairs, reads=wp(dreg, 0) + ("lnb", f"PS{pb}"),
                            writes=(f"PS{pb}",))
                        dst = AB[:, dst_off + ec * T: dst_off + (ec + 1) * T]
                        P.op("v", (lambda v, d=dst, so=PS[pb][:, 0:T]:
                                   v.tensor_copy(d, so)),
                             reads=(f"PS{pb}",),
                             writes=(("qfm",) if dst_off == QF else ("kfm",)))
                else:
                    for t4 in range(4):
                        for eh in range(2):
                            pb = (t4 * 2 + eh) % 2
                            pairs = []
                            for dc in range(NCH):
                                lh = LNB[:, dc * T + t4 * 128: dc * T + t4 * 128 + 128]
                                rh = AA[:, woff + dc * 1024 + eh * 512:
                                        woff + dc * 1024 + (eh + 1) * 512]
                                pairs.append((lh, rh, dc == 0, dc == 7,
                                              PS[pb][:, 0:512]))
                            MMg(None, pairs, reads=wp(dreg, 0) + ("lnb", f"PS{pb}"),
                                writes=(f"PS{pb}",))
                            dst = AB[:, dst_off + t4 * 1024 + eh * 512:
                                     dst_off + t4 * 1024 + (eh + 1) * 512]
                            P.op("v", (lambda v, d=dst, so=PS[pb][:, 0:512]:
                                       v.tensor_copy(d, so)),
                                 reads=(f"PS{pb}",),
                                 writes=(("ktm",) if dst_off == KT else ("vtm",)))
            proj(QF, 0, "wAq", "fm")
            proj(KF, 8192, "wAk", "fm")
            proj(KT, 8192, "wAk", "tm")
            proj(VT, 16384, "wAv", "tm")

            for pp in range(T // 128):
                g = s * (T // 128) + pp
                par = g % 2
                stb_new = STB + 1024 * par
                stb_old = STB + 1024 * (1 - par)
                sb_new, sb_old = f"stateB{par}", f"stateB{1 - par}"
                for hg in range(2):
                    pairs = []
                    for hi in range(4):
                        h = hg * 4 + hi
                        lh = AB[:, KF + h * T + pp * 128: KF + h * T + pp * 128 + 128]
                        rh = AB[:, QF + h * T + pp * 128: QF + h * T + pp * 128 + 128]
                        pairs.append((lh, rh, hi == 0, hi == 3,
                                      PS[hg][:, hi * 128:(hi + 1) * 128]))
                    MMg(None, pairs, reads=("kfm", "qfm", f"PS{hg}"),
                        writes=(f"PS{hg}",))
                pairs = []
                for h in range(H):
                    kt = AB[:, KT + pp * 1024 + h * 128: KT + pp * 1024 + (h + 1) * 128]
                    vt = AB[:, VT + pp * 1024 + h * 128: VT + pp * 1024 + (h + 1) * 128]
                    bank = 2 + h // 4
                    oap = PS[bank][:, (h % 4) * 128:(h % 4 + 1) * 128]
                    pairs.append((kt, vt, h % 4 == 0, h % 4 == 3, oap))
                MMg(None, pairs, reads=("ktm", "vtm", "PS2", "PS3"),
                    writes=("PS2", "PS3"))
                sms = []
                for hg in range(2):
                    smx = BIGB[:, 6144 + hg * 512: 6656 + hg * 512]
                    smr = f"sm{hg}"
                    sms.append((smx, smr))
                    P.op("v", (lambda v, d=smx, pbs=hg:
                               v.tensor_tensor(d, PS[pbs][:, 0:512], MSK[:, :],
                                               ALU.mult)),
                         reads=(f"PS{hg}", "c_mask"), writes=(smr,))
                P.op("v", (lambda v: v.tensor_tensor(SQ[:, 0:512], SQ[:, 0:512],
                                                     PS[2][:, 0:512], ALU.add)),
                     reads=("PS2", "statef"), writes=("statef",))
                P.op("v", (lambda v: v.tensor_tensor(SQ[:, 512:1024], SQ[:, 512:1024],
                                                     PS[3][:, 0:512], ALU.add)),
                     reads=("PS3", "statef"), writes=("statef",))
                P.op("v", (lambda v, d=stb_new: v.tensor_copy(
                    AB[:, d:d + 1024], SQ[:, 0:1024])),
                     reads=("statef",), writes=(sb_new,))
                for hg in range(2):
                    smx, smr = sms[hg]
                    pb_o = 4 + hg
                    pairs = []
                    for hi in range(4):
                        h = hg * 4 + hi
                        vsl = AB[:, VT + pp * 1024 + h * 128:
                                 VT + pp * 1024 + (h + 1) * 128]
                        smh = smx[:, hi * 128:(hi + 1) * 128]
                        oap = PS[pb_o][:, hi * 128:(hi + 1) * 128]
                        pairs.append((vsl, smh, hi == 0, False, oap))
                        stt = AB[:, stb_old + h * 128: stb_old + (h + 1) * 128]
                        qsl = AB[:, QF + h * T + pp * 128: QF + h * T + pp * 128 + 128]
                        pairs.append((stt, qsl, False, hi == 3, oap))
                    MMg(None, pairs, reads=("vtm", smr, sb_old, "qfm",
                                            f"PS{pb_o}"),
                        writes=(f"PS{pb_o}",))
                    pairs = []
                    for hi in range(4):
                        h = hg * 4 + hi
                        cb = CARB[:, h:h + 1].broadcast_to([128, 128])
                        qsl = AB[:, QF + h * T + pp * 128: QF + h * T + pp * 128 + 128]
                        oap = PS[6][:, hi * 128:(hi + 1) * 128]
                        pairs.append((cb, qsl, hi == 0, False, oap))
                        smh = smx[:, hi * 128:(hi + 1) * 128]
                        pairs.append((CON[:, :], smh, False, hi == 3, oap))
                    MMg(None, pairs, reads=("carb", "qfm", smr, "c_ones", "PS6"),
                        writes=("PS6",))
                    o2 = BIGB[:, 5632:6144]
                    P.op("s", (lambda s_, d=o2, pbo=pb_o:
                               s_.activation(d, PS[pbo][:, 0:512], AF.Square)),
                         reads=(f"PS{pb_o}",), writes=("o2",))
                    sbc = BIG[:, 2304:2816]
                    P.op("s", (lambda s_, d=sbc: s_.activation(
                        d, PS[6][:, 0:512], AF.Square)),
                         reads=("PS6",), writes=("sbc",))
                    pairs = []
                    for hi in range(4):
                        pairs.append((CON[:, :], o2[:, hi * 128:(hi + 1) * 128],
                                      hi == 0, hi == 3,
                                      PS[7][:, hi * 128:(hi + 1) * 128]))
                    MMg(None, pairs, reads=("o2", "c_ones", "PS7"), writes=("PS7",))
                    P.op("v", (lambda v, d=sbc: v.tensor_scalar(
                        d, d, 1.0, RMS_EPS, ALU.max, ALU.mult)),
                         reads=("sbc",), writes=("sbc",))
                    P.op("v", (lambda v, d=sbc: v.scalar_tensor_tensor(
                        d, PS[7][:, 0:512], 1.0 / DH, d, ALU.mult, ALU.add)),
                         reads=("PS7", "sbc"), writes=("sbc",))
                    P.op("s", (lambda s_, d=sbc: s_.activation(d, d, AF.Ln)),
                         reads=("sbc",), writes=("sbc",))
                    P.op("s", (lambda s_, d=sbc: s_.activation(d, d, AF.Exp,
                                                               scale=-0.5)),
                         reads=("sbc",), writes=("sbc",))
                    onv = AB[:, ON:ON + 4096].rearrange(
                        "p (h k t) -> p h k t", h=H, k=T // 128)[
                        :, hg * 4:(hg + 1) * 4, pp, :]
                    piv = PS[pb_o][:, 0:512].rearrange("p (h t) -> p h t", h=4)
                    sbv = sbc.rearrange("p (h t) -> p h t", h=4)
                    P.op("v", (lambda v, d=onv, a=piv, b=sbv:
                               v.tensor_tensor(d, a, b, ALU.mult)),
                         reads=(f"PS{pb_o}", "sbc"), writes=("on",))
                kfv = AB[:, KF:KF + 4096].rearrange("p (h t) -> p h t", h=H)[
                    :, :, pp * 128:(pp + 1) * 128]
                P.op("v", (lambda v, d=K8[:, :], so=kfv:
                           v.tensor_reduce(d, so, AX.X, ALU.add)),
                     reads=("kfm",), writes=("k8",))
                P.op("v", (lambda v: v.tensor_tensor(CAR[:, :], CAR[:, :],
                                                     K8[:, :], ALU.add)),
                     reads=("car32", "k8"), writes=("car32",))
                P.op("v", (lambda v: v.tensor_copy(CARB[:, :], CAR[:, :])),
                     reads=("car32",), writes=("carb",))
            wprep_emit(ceil_div(len(wprep_jobs), nsl))
            if s + 1 < nsl:
                load_ln(s + 1)
            for doc in range(NCH):
                pb = doc % 2
                pairs = []
                for ec in range(NCH):
                    lh = AA[:, 24576 + ec * 1024 + doc * 128:
                            24576 + ec * 1024 + doc * 128 + 128]
                    rh = AB[:, ON + ec * T: ON + (ec + 1) * T]
                    pairs.append((lh, rh, ec == 0, ec == 7, PS[pb][:, 0:T]))
                MMg(None, pairs, reads=wp("wAo", 0) + ("on", f"PS{pb}"), writes=(f"PS{pb}",))
                xv = xb[:, doc * T:(doc + 1) * T]
                P.op("v", (lambda v, d=xv, pbb=pb: v.tensor_tensor(
                    d, d, PS[pbb][:, 0:T], ALU.add)),
                     reads=(f"PS{pb}",) + xregs, writes=(xregs[0], xregs[1]))
            x_store(xb, dst_dram, s, T, xregs[0], f"ds{sl}", f"{dst_pref}:{2*s}")
            P.reg[f"{dst_pref}:{2*s+1}"] = dict(P.reg[f"{dst_pref}:{2*s}"])

    def phase_ffn(src_dram, src_pref, dst_dram, dst_pref, final=False, wmode='plain'):
        T = 256
        nsl = S // T
        def prepass(s):
            sl = s % 4
            xb = (XB0 if sl < 2 else XB1).ap()[
                :, (sl % 2) * 2048:(sl % 2) * 2048 + 2048]
            xreg = f"xs{sl}"
            blk = src_dram[:, (s // 2) * (NCH * 512):(s // 2 + 1) * (NCH * 512)]
            src = blk.rearrange("p (c t) -> p c t", c=NCH)[
                :, :, (s % 2) * T:(s % 2) * T + T]
            dst = xb.rearrange("p (c t) -> p c t", c=NCH)
            P.dma(f"dx{sl}", (lambda y, d=dst, so=src: y.dma_start(d, so)),
                  reads=(f"{src_pref}:{s}",) if src_pref else (),
                  writes=(xreg,))
            mb = SQB[:, s * T:(s + 1) * T]
            rs = SQB[:, 4096 + s * T: 4096 + (s + 1) * T]
            ln_pass(T, xb, (xreg,), LNB.ap()[:, 0:2048], "lnb", 4, 5, "ffnpre",
                    mr_store=(mb, rs, f"mrm{s}", f"mrr{s}"), rsqrt_dve=True)

        def fctx(s):
            sl = s % 4
            xb = (XB0 if sl < 2 else XB1).ap()[
                :, (sl % 2) * 2048:(sl % 2) * 2048 + 2048]
            return sl, xb, f"xs{sl}"

        def fpre(s):
            sl, xb, xreg = fctx(s)
            mb = SQB[:, s * T:(s + 1) * T]
            rs = SQB[:, 4096 + s * T: 4096 + (s + 1) * T]
            for c in range(NCH):
                la = LNB[:, c * T:(c + 1) * T]
                P.op("v", (lambda v, d=la, so=xb[:, c * T:(c + 1) * T]:
                           v.tensor_copy(d, so)),
                     reads=(xreg,), writes=("lnb",) if c == 0 else ("lnb#",))
            for c in range(NCH):
                la = LNB[:, c * T:(c + 1) * T]
                P.op("v", (lambda v, d=la, m=mb: v.tensor_tensor(d, d, m, ALU.subtract)),
                     reads=("lnb", "lnb#", f"mrm{s}"), writes=("lnb",))
                P.op("v", (lambda v, d=la, r=rs: v.tensor_tensor(d, d, r, ALU.mult)),
                     reads=("lnb", f"mrr{s}"), writes=("lnb",))

        for _k in range(min(3, nsl)):
            prepass(_k)
        fpre(0)
        ABF = AB.ap().bitcast(F32)
        if wmode == "load":
            P.op("v", (lambda v: v.memset(K8[0:1, 0:1], 0.0)),
                 reads=(), writes=tuple(ARENA_REGS))
            for k in range(8):
                regs = tuple(f"wAq@{r}:{c}" for (r, c) in
                             [((k * 4 + i) // 4, (k * 4 + i) % 4) for i in range(4)])
                P.dma(f"dw{k % 2}", (lambda y, kk=k: y.dma_start(
                    AA[:, kk * 4096:(kk + 1) * 4096],
                    wcA[:, kk * 4096:(kk + 1) * 4096])),
                      reads=("wc:w1",), writes=regs)
            for k in range(8):
                regs = tuple(f"wB@{r}:0" for r in range(k * 4, k * 4 + 4))
                P.dma(f"dw{k % 2}", (lambda y, kk=k: y.dma_start(
                    AB[:, kk * 4096:(kk + 1) * 4096],
                    wcB[:, kk * 4096:(kk + 1) * 4096])),
                      reads=("wc:w2",), writes=regs)
        else:
            load_weights([("w1", 8, 4096, "wAq", AA, 0),
                          ("w2", 32, 1024, "wB", AB, 0)],
                         slot1=(ABF[:, 15360:16384], ("wB@30:0", "wB@31:0"),
                                (lambda dn, rc, cp: dn == "w2" and rc >= 30)))
            if wmode == "store":
                for k in range(8):
                    regs = tuple(f"wAq@{r}:{c}" for (r, c) in
                                 [((k * 4 + i) // 4, (k * 4 + i) % 4)
                                  for i in range(4)])
                    P.dma(f"dwc{k % 2}", (lambda y, kk=k: y.dma_start(
                        wcA[:, kk * 4096:(kk + 1) * 4096],
                        AA[:, kk * 4096:(kk + 1) * 4096])),
                          reads=regs, writes=("wcA",))
                for k in range(8):
                    regs = tuple(f"wB@{r}:0" for r in range(k * 4, k * 4 + 4))
                    P.dma(f"dwc{k % 2}", (lambda y, kk=k: y.dma_start(
                        wcB[:, kk * 4096:(kk + 1) * 4096],
                        AB[:, kk * 4096:(kk + 1) * 4096])),
                          reads=regs, writes=("wcB",))
        for s in range(nsl):
            sl, xb, xreg = fctx(s)
            for j in range(NJ):
                pb = j % 2
                pairs = []
                for dc in range(NCH):
                    lh = AA[:, dc * 4096 + j * 128: dc * 4096 + j * 128 + 128]
                    rh = LNB[:, dc * T:(dc + 1) * T]
                    pairs.append((lh, rh, dc == 0, dc == 7, PS[pb][:, 0:T]))
                MMg(None, pairs, reads=wp("wAq", j * 128, 1024) + ("lnb", f"PS{pb}"), writes=(f"PS{pb}",))
                hj = BIGB[:, j * T:(j + 1) * T]
                P.op("s", (lambda s_, d=hj, pbb=pb, jj=j: s_.activation(
                    d, PS[pbb][:, 0:T], AF.Gelu, bias=B1C[:, jj:jj + 1])),
                     reads=(f"PS{pb}", "c_b1"), writes=("hsb",) if j == 0 else ("hsb#",))
            if s + 1 < nsl:
                fpre(s + 1)
            if s + 3 < nsl:
                prepass(s + 3)
            for doc in range(NCH):
                pb = (2, 3, 6, 7)[doc % 4]
                pairs = []
                for j in range(NJ):
                    lh = AB[:, j * 1024 + doc * 128: j * 1024 + doc * 128 + 128]
                    rh = BIGB[:, j * T:(j + 1) * T]
                    pairs.append((lh, rh, j == 0, j == NJ - 1, PS[pb][:, 0:T]))
                MMg(None, pairs,
                    reads=tuple(f"wB@{r2}:0" for r2 in range(32)) +
                          ("hsb", "hsb#", f"PS{pb}"),
                    writes=(f"PS{pb}",))
                xv = xb[:, doc * T:(doc + 1) * T]
                P.op("v", (lambda v, d=xv, pbb=pb, dd=doc: v.scalar_tensor_tensor(
                    d, PS[pbb][:, 0:T], B2C[:, dd:dd + 1], d, ALU.add, ALU.add)),
                     reads=(f"PS{pb}", "c_b2", xreg), writes=(xreg,))
            dblk = dst_dram[:, (s // 2) * (NCH * 512):(s // 2 + 1) * (NCH * 512)]
            dstd = dblk.rearrange("p (c t) -> p c t", c=NCH)[
                :, :, (s % 2) * T:(s % 2) * T + T]
            srcs = xb.rearrange("p (c t) -> p c t", c=NCH)
            P.dma(f"ds{s % 2}", (lambda y, d=dstd, so=srcs: y.dma_start(d, so)),
                  reads=(xreg,), writes=(f"{dst_pref}:{s}",))

    def phase_attn_b(src_dram, src_pref, dst_dram, dst_pref, wmode='plain'):
        T = 512
        for i, base in enumerate((0, 32, 64)):
            g = BVE[:, base:base + 8]; b = BVE[:, base + 8:base + 16]
            m = BVE[:, base + 16:base + 24]; vv = BVE[:, base + 24:base + 32]
            al = ALF[:, i * 16:i * 16 + 8]; be = ALF[:, i * 16 + 8:i * 16 + 16]
            tmp = ALF[:, 48:56]
            P.op("s", (lambda s_, d=tmp, so=vv: s_.activation(d, so, AF.Ln,
                                                              bias=CV[:, 2:3])),
                 reads=("c_bv", "c_cv"), writes=("alf_t",))
            P.op("s", (lambda s_, d=tmp: s_.activation(d, d, AF.Exp, scale=-0.5)),
                 reads=("alf_t",), writes=("alf_t",))
            P.op("v", (lambda v, d=al, gg=g, t=tmp: v.scalar_tensor_tensor(
                d, gg, 4.0, t, ALU.mult, ALU.mult)),
                 reads=("c_bv", "alf_t"), writes=(f"alf{i}",))
            P.op("v", (lambda v, d=tmp, a=al, mm=m: v.tensor_tensor(d, a, mm, ALU.mult)),
                 reads=(f"alf{i}", "c_bv"), writes=("alf_t",))
            P.op("v", (lambda v, d=be, bb=b, t=tmp: v.scalar_tensor_tensor(
                d, bb, 4.0, t, ALU.mult, ALU.subtract)),
                 reads=("c_bv", "alf_t"), writes=(f"alf{i}b",))
            P.op("v", (lambda v, d=be: v.tensor_scalar(d, d, 4.0, None, ALU.subtract)),
                 reads=(f"alf{i}b",), writes=(f"alf{i}b",))
        nsl = S // T

        def bctx(s):
            sl = s % 2
            xb = (XB0 if sl == 0 else XB1).ap()
            return sl, xb, (f"xs{2*sl}", f"xs{2*sl+1}")

        def qk_sums(s):
            for (ii, woff, psb) in ((0, 0, 2), (1, 8192, 3)):
                wreg = "wAq" if ii == 0 else "wAk"
                for ec in range(NCH):
                    pb = ec % 2
                    pairs = []
                    for dc in range(NCH):
                        lh = AA[:, woff + dc * 1024 + ec * 128:
                                woff + dc * 1024 + ec * 128 + 128]
                        rh = LNB[:, dc * T:(dc + 1) * T]
                        pairs.append((lh, rh, dc == 0, dc == 7, PS[pb][:, 0:T]))
                    MMg(None, pairs, reads=wp(wreg, 0) + ("lnb", f"PS{pb}"),
                        writes=(f"PS{pb}",))
                    qs = BIG[:, (ec % 2) * 512:(ec % 2) * 512 + 512]
                    qreg = f"qsig{ec % 2}"
                    wl = (qreg,)
                    if ii == 0 and ec == 0:
                        wl = wl + ("vfm", "vfm#")
                    P.op("s", (lambda s_, d=qs, pbb=pb, i2=ii, e2=ec: s_.activation(
                        d, PS[pbb][:, 0:T], AF.Sigmoid,
                        bias=ALF[:, i2 * 16 + 8 + e2:i2 * 16 + 9 + e2],
                        scale=ALF[:, i2 * 16 + e2:i2 * 16 + 1 + e2])),
                         reads=(f"PS{pb}", f"alf{ii}", f"alf{ii}b"), writes=wl)
                    P.op("t", (lambda t, o=PS[psb][:, 0:T], q=qs, st=(ec == 0),
                               sp=(ec == NCH - 1):
                               t.matmul(o, CONF[:, :], q, start=st, stop=sp)),
                         reads=(qreg, "c_onesf", f"PS{psb}"), writes=(f"PS{psb}",))
            P.op("s", (lambda e, ss=s: e.copy(SQ[:, ss * T:(ss + 1) * T],
                                              PS[2][:, 0:T])),
                 reads=("PS2",), writes=("sq",) if s == 0 else ("sq#",))
            krow = BIG[0:1, 1024:1024 + T]
            P.op("s", (lambda s_, d=krow: s_.copy(d, PS[3][0:1, 0:T])),
                 reads=("PS3",), writes=("krow",))
            pairs = []
            for jseg in range(T // 128):
                pairs.append((krow[0:1, jseg * 128:(jseg + 1) * 128], ONEF[:, :],
                              jseg == 0, jseg == T // 128 - 1,
                              PS[3][:, jseg:jseg + 1]))
            MMg(None, pairs, reads=("krow", "c_onef", "PS3"), writes=("PS3",))
            P.op("s", (lambda e, ss=s: e.copy(
                SKC[:, ss * (T // 128):(ss + 1) * (T // 128)],
                PS[3][:, 0:T // 128])),
                 reads=("PS3",), writes=("skc",) if s == 0 else ("skc#",))

        def v_proj(s):
            for ec in range(NCH):
                pb = ec % 2
                pairs = []
                for dc in range(NCH):
                    lh = AA[:, 16384 + dc * 1024 + ec * 128:
                            16384 + dc * 1024 + ec * 128 + 128]
                    rh = LNB[:, dc * T:(dc + 1) * T]
                    pairs.append((lh, rh, dc == 0, dc == 7, PS[pb][:, 0:T]))
                MMg(None, pairs, reads=wp("wAv", 0) + ("lnb", f"PS{pb}"),
                    writes=(f"PS{pb}",))
                vf = BIGB[:, ec * 512:(ec + 1) * 512]
                wl = ("vfm",) if ec == 0 else ("vfm#",)
                if ec == 0:
                    wl = wl + ("qsig0", "qsig1", "krow", "lnm2", "ln_mb",
                               "ln_rs", "ln_tmp", "ln_tmp2")
                P.op("s", (lambda s_, d=vf, pbb=pb, e2=ec: s_.activation(
                    d, PS[pbb][:, 0:T], AF.Sigmoid,
                    bias=ALF[:, 40 + e2:41 + e2], scale=ALF[:, 32 + e2:33 + e2])),
                     reads=(f"PS{pb}", "alf2", "alf2b"), writes=wl)

        def vw_mm(s):
            for t4 in range(4):
                for doh in range(2):
                    pb = 6 + (t4 * 2 + doh) % 2
                    pairs = []
                    for ec in range(NCH):
                        lh = BIGB[:, ec * 512 + t4 * 128: ec * 512 + t4 * 128 + 128]
                        rh = AA[:, 24576 + ec * 1024 + doh * 512:
                                24576 + ec * 1024 + (doh + 1) * 512]
                        pairs.append((lh, rh, ec == 0, ec == 7, PS[pb][:, 0:512]))
                    MMg(None, pairs, reads=("vfm", "vfm#") + wp("wAo", 0) +
                        (f"PS{pb}",), writes=(f"PS{pb}",))
                    tc = s * 4 + t4
                    dstv = AB[:, tc * 1024 + doh * 512: tc * 1024 + (doh + 1) * 512]
                    P.op("s", (lambda e, d=dstv, pbb=pb: e.copy(
                        d, PS[pbb][:, 0:512])),
                         reads=(f"PS{pb}",), writes=("VW",) if (s == 0 and t4 == 0
                                                               and doh == 0)
                         else ("VW#",))

        def bload_ln(s, mid=None):
            sl, xb, xregs = bctx(s)
            src = src_dram[:, s * NCH * T:(s + 1) * NCH * T]
            P.dma(f"dx{sl}", (lambda y, d=xb, so=src: y.dma_start(d, so)),
                  reads=(f"{src_pref}:{2*s}", f"{src_pref}:{2*s+1}"),
                  writes=(xregs[0],))
            P.reg[xregs[1]] = dict(P.reg[xregs[0]])
            ln_pass(T, xb, xregs, LNB.ap(), "lnb", 4, 5, 0,
                    mid=mid, x2_bb=7168, split_apply=False,
                    math_war=("qsig0", "qsig1", "krow", "vfm", "vfm#"),
                    apply_war=("vfm", "vfm#"), rsqrt_dve=True)

        bload_ln(0)
        if wmode == "load":
            P.op("v", (lambda v: v.memset(K8[0:1, 0:1], 0.0)),
                 reads=(), writes=tuple(ARENA_REGS))
            for wi, wreg in enumerate(("wAq", "wAk", "wAv", "wAo")):
                for half in range(2):
                    k = wi * 2 + half
                    regs = tuple(f"{wreg}@{r}:0" for r in
                                 range(half * 4, half * 4 + 4))
                    off = wi * 8192 + half * 4096
                    P.dma(f"dw{k % 2}", (lambda y, o=off: y.dma_start(
                        AA[:, o:o + 4096], wcC[:, o:o + 4096])),
                          reads=(f"wc:{('bq','bk','bv','bo')[wi]}",),
                          writes=regs)
        else:
            load_weights([("bq", 8, 1024, "wAq", AA, 0),
                          ("bk", 8, 1024, "wAk", AA, 8192),
                          ("bv", 8, 1024, "wAv", AA, 16384),
                          ("bo", 8, 1024, "wAo", AA, 24576)])
        for s in range(nsl):
            sl, xb, xregs = bctx(s)
            qk_sums(s)
            v_proj(s)
            if s + 1 < nsl:
                bload_ln(s + 1, mid=lambda s2=s: vw_mm(s2))
            else:
                vw_mm(s)
        pairs = []
        for doc in range(NCH):
            for jc in range(NTC):
                lh = AB[:, jc * 1024 + doc * 128: jc * 1024 + doc * 128 + 128]
                pairs.append((lh, CON[:, 0:1], (doc == 0 and jc == 0),
                              (doc == NCH - 1 and jc == NTC - 1),
                              PS[0][:, doc:doc + 1]))
        MMg(None, pairs, reads=("VW", "VW#", "c_ones", "PS0"), writes=("PS0",))
        P.op("v", (lambda v: v.scalar_tensor_tensor(
            CSB[:, :], PS[0][:, 0:NCH], 0.9, BOC[:, :], ALU.mult, ALU.add)),
             reads=("PS0", "c_boc"), writes=("csb",))
        nib = S // T
        for ib in range(nib):
            sl = ib % 2
            xb = (XB0 if sl == 0 else XB1).ap()
            xregs = (f"xs{2*sl}", f"xs{2*sl+1}")
            src = src_dram[:, ib * NCH * T:(ib + 1) * NCH * T]
            P.dma(f"dx{sl}", (lambda y, d=xb, so=src: y.dma_start(d, so)),
                  reads=(f"{src_pref}:{2*ib}", f"{src_pref}:{2*ib+1}"),
                  writes=(xregs[0],))
            P.reg[xregs[1]] = dict(P.reg[xregs[0]])
            for jc in range(NTC):
                pu = jc % 2
                u = BIG[:, 512 + pu * 512: 1024 + pu * 512]
                ureg = f"u{pu}"
                f2 = BIG[:, 1536 + pu * 512: 2048 + pu * 512]
                f2reg = f"f2{pu}"
                mf = BIG[:, 2560:3072]
                at = BIGB[:, pu * 512:(pu + 1) * 512]
                atreg = f"at{pu}"
                sqv = SQ[:, ib * T:(ib + 1) * T]
                skcol = SKC[:, jc:jc + 1]
                du = BIG[:, 3072:3584] if pu == 0 else BIG[:, 3584:4096]
                dreg = f"du{pu}"
                P.op("v", (lambda v, d=du, sv=sqv, sk=skcol: v.tensor_scalar(
                    d, sv, sk, None, ALU.subtract)),
                     reads=("sq", "sq#", "skc", "skc#"), writes=(dreg,))
                P.op("s", (lambda s_, d=u, so=du: s_.activation(
                    d, so, AF.Abs, scale=2.0 / 1024.0)),
                     reads=(dreg,), writes=(ureg,))
                P.op("s", (lambda s_, d=f2, so=u: s_.activation(
                    d, so, AF.Exp, bias=CV[:, 0:1], scale=-1.0)),
                     reads=(ureg, "c_cv"), writes=(f2reg,))
                P.op("v", (lambda v, d=mf, sv=sqv, sk=skcol, f=f2:
                           v.scalar_tensor_tensor(d, sv, sk, f, ALU.is_gt, ALU.mult)),
                     reads=("sq", "sq#", "skc", "skc#", f2reg), writes=("mf",))
                P.op("v", (lambda v, d=at, m=mf, f=f2: v.scalar_tensor_tensor(
                    d, m, 2.0, f, ALU.mult, ALU.subtract)),
                     reads=("mf", f2reg), writes=(atreg,))
                pairs = []
                for doc in range(NCH):
                    lh = AB[:, jc * 1024 + doc * 128: jc * 1024 + doc * 128 + 128]
                    pairs.append((lh, at, jc == 0, jc == NTC - 1,
                                  PS[doc][:, 0:T]))
                MMg(None, pairs,
                    reads=("VW", "VW#", atreg) + tuple(f"PS{d2}" for d2 in range(8)),
                    writes=tuple(f"PS{d2}" for d2 in range(8)))
            for doc in range(NCH):
                xv = xb[:, doc * T:(doc + 1) * T]
                P.op("v", (lambda v, d=xv, dd=doc: v.scalar_tensor_tensor(
                    d, PS[dd][:, 0:T], CSB[:, dd:dd + 1], d, ALU.add, ALU.add)),
                     reads=(f"PS{doc}", "csb", xregs[0], xregs[1]),
                     writes=(xregs[0], xregs[1]))
            dstd = dst_dram[:, ib * NCH * T:(ib + 1) * NCH * T]
            P.dma(f"ds{sl}", (lambda y, d=dstd, so=xb: y.dma_start(d, so)),
                  reads=(xregs[0],), writes=(f"{dst_pref}:{2*ib}",))
            P.reg[f"{dst_pref}:{2*ib+1}"] = dict(P.reg[f"{dst_pref}:{2*ib}"])

    chain = [p for p in ("attn_a", "ffn1", "attn_b", "ffn2") if p in phases]

    bufs = [xB, xC]
    cur_src, cur_pref = dr["xT"], None
    for i, name in enumerate(chain):
        last = i == len(chain) - 1
        dst = out if last else bufs[i % 2]
        dst_pref = "OUT" if last else ("hxB" if i % 2 == 0 else "hxC")
        has_a = "attn_a" in chain
        if name == "attn_a":
            phase_attn_a(cur_src, cur_pref, dst, dst_pref)
        elif name.startswith("ffn"):
            phase_ffn(cur_src, cur_pref, dst, dst_pref, final=last,
                      wmode="load" if has_a else "plain")
        else:
            phase_attn_b(cur_src, cur_pref, dst, dst_pref,
                         wmode="load" if has_a else "plain")
        cur_src, cur_pref = dst, dst_pref

    if DBG:
        P.dma("dbg", lambda y: y.dma_start(dbg1, SQ[:, 0:S]), reads=("sq", "sq#"))
        P.dma("dbg", lambda y: y.dma_start(dbg2, SKC[:, 0:NTC]), reads=("skc", "skc#"))
        P.dma("dbg", lambda y: y.dma_start(dbg3, AB[:, 0:NTC * 1024]), reads=("VW", "VW#"))
    P.final_waits("y")
    P.finalize()

    with nc.Block() as block:
        @block.tensor
        def _(t):
            P.run_engine("t", t)

        @block.vector
        def _(v):
            P.run_engine("v", v)

        @block.scalar
        def _(s):
            P.run_engine("s", s)

        @block.gpsimd
        def _(p):
            P.run_engine("p", p)

        @block.sync
        def _(y):
            P.run_engine("y", y)
    es.close()
    return nc


_NCACHE = {}


def _get_nc(S, phases):
    key = (S, tuple(phases))
    if key not in _NCACHE:
        _NCACHE[key] = build_nc(S, phases)
    return _NCACHE[key]


def _vec_cols(v, n):
    return np.ascontiguousarray(v.reshape(n, 128).T).astype(np.float32)


PHASES = ("attn_a", "ffn1", "attn_b", "ffn2")
RUNNER = None


def kernel(**inputs):
    import ml_dtypes
    phases = PHASES
    x = np.asarray(inputs["L0_attn_a_hidden_states"], dtype=np.float32)
    B = x.shape[0]
    S = x.shape[1]
    nc = _get_nc(S, phases)

    aq = np.ascontiguousarray(np.asarray(inputs["L0_attn_a_Wq"], np.float32).T)
    ak = np.ascontiguousarray(np.asarray(inputs["L0_attn_a_Wk"], np.float32).T)
    av = np.ascontiguousarray(np.asarray(inputs["L0_attn_a_Wv"], np.float32).T)
    ao = np.ascontiguousarray(np.asarray(inputs["L0_attn_a_Wo"], np.float32).T)
    anw = np.ascontiguousarray(
        np.asarray(inputs["L0_attn_a_norm_w"], np.float32).reshape(128, 1))
    w1 = np.ascontiguousarray(np.asarray(inputs["L0_ffn_w1"], np.float32))
    w2 = np.ascontiguousarray(np.asarray(inputs["L0_ffn_w2"], np.float32))
    b1c = _vec_cols(np.asarray(inputs["L0_ffn_b1"], np.float32), NJ)
    b2c = _vec_cols(np.asarray(inputs["L0_ffn_b2"], np.float32), NCH)
    bq = np.ascontiguousarray(np.asarray(inputs["L0_attn_b_Wq"], np.float32))
    bk = np.ascontiguousarray(np.asarray(inputs["L0_attn_b_Wk"], np.float32))
    bv = np.ascontiguousarray(np.asarray(inputs["L0_attn_b_Wv"], np.float32))
    bo = np.ascontiguousarray(np.asarray(inputs["L0_attn_b_Wo"], np.float32))
    bvecs = np.concatenate(
        [_vec_cols(np.asarray(inputs[f"L0_attn_b_{nm}"], np.float32), 8)
         for nm in ("gq", "bq", "mq", "vq", "gk", "bk", "mk", "vk",
                    "gv", "bv", "mv", "vv")], axis=1)
    boc = _vec_cols(np.asarray(inputs["L0_attn_b_bo"], np.float32), NCH)
    onesall = np.ones((128, 128), dtype=ml_dtypes.bfloat16)
    tk = np.arange(128)[:, None]
    tq = np.arange(128)[None, :]
    m128 = ((tk // 64 < tq // 64) | ((tk // 64 == tq // 64) & (tk <= tq))
            ).astype(np.float32)
    maskt = np.tile(m128, (1, 4)).astype(ml_dtypes.bfloat16)
    cvals = np.tile(np.array([[LN09, LN_EPS, A_EPS, RMS_EPS, 0, 0, 0, 0]],
                             np.float32), (128, 1))
    onef = np.ones((1, 1), np.float32)

    shared = dict(aq=aq, ak=ak, av=av, ao=ao, anw=anw, w1=w1, w2=w2, b1c=b1c,
                  b2c=b2c, bq=bq, bk=bk, bv=bv, bo=bo, bvecs=bvecs, boc=boc,
                  onesall=onesall, maskt=maskt, cvals=cvals, onef=onef,
                  onesf32=np.ones((128, 128), np.float32))
    NB = S // 512
    in_maps = []
    for b in range(B):
        m = dict(shared)
        m["xT"] = np.ascontiguousarray(
            x[b].T.reshape(NCH, 128, NB, 512).transpose(1, 2, 0, 3)
            .reshape(128, NCH * S))
        in_maps.append(m)
    runner = RUNNER if RUNNER is not None else run_bass_kernel_spmd
    res = runner(nc, in_maps, core_ids=list(range(B)))
    outs = [np.asarray(r["outT"]).reshape(128, NB, NCH, 512)
            .transpose(1, 3, 2, 0).reshape(S, D) for r in res.results]
    return np.stack(outs, axis=0).astype(np.float32)


# version 24 (final)
# speedup vs baseline: 1.0840x; 1.0192x over previous
import numpy as np
from contextlib import ExitStack

import concourse.bass as bass
import concourse.mybir as mybir
from concourse.bass_utils import run_bass_kernel_spmd

F32 = mybir.dt.float32
BF16 = mybir.dt.bfloat16
AF = mybir.ActivationFunctionType
ALU = mybir.AluOpType
AX = mybir.AxisListType

D = 1024
NCH = 8
H = 8
DH = 128
CH = 64
FF = 4096
NJ = 32
LN_EPS = 1e-5
A_EPS = 1e-5
RMS_EPS = 1e-5
LN09 = float(np.log(0.9))

ENGS = ("t", "v", "s", "p", "y")


class Prog:

    def __init__(self, nc, es):
        self.nc = nc
        self.es = es
        self.ops = {e: [] for e in ENGS}
        self.seq = {e: 0 for e in ENGS}
        self.waited = {e: set() for e in ENGS}
        self.dma_cnt = {}
        self.dma_h = {}
        self.eng_h = {}
        self.reg = {}
        for e in ENGS:
            self.eng_h[e] = es.enter_context(nc.semaphore(f"se_{e}"))

    def dma_sem(self, name):
        if name not in self.dma_h:
            self.dma_h[name] = self.es.enter_context(self.nc.semaphore(f"sd_{name}"))
            self.dma_cnt[name] = 0
        return self.dma_h[name]

    def _dep(self, name):
        if name not in self.reg:
            self.reg[name] = {"w": None, "r": []}
        return self.reg[name]

    def _collect(self, reads, writes):
        waits = []
        for r in reads:
            d = self._dep(r)
            if d["w"] is not None:
                waits.append(d["w"])
        for w in writes:
            d = self._dep(w)
            if d["w"] is not None:
                waits.append(d["w"])
            waits.extend(d["r"])
        best = {}
        for kind, key, val in waits:
            k = (kind, key)
            if k not in best or best[k] < val:
                best[k] = val
        return [(kind, key, val) for (kind, key), val in best.items()]

    def _record(self, tok, reads, writes):
        for r in reads:
            self._dep(r)["r"].append(tok)
        for w in writes:
            self.reg[w] = {"w": tok, "r": []}

    def op(self, eng, fn, reads=(), writes=(), extra=()):
        waits = self._collect(reads, writes) + list(extra)
        my = []
        for kind, key, val in waits:
            if kind == "eng":
                if key == eng:
                    self.waited[key].add(val)
                else:
                    self.waited[key].add(val)
            my.append((kind, key, val))
        self.seq[eng] += 1
        seq = self.seq[eng]
        self.ops[eng].append({"fn": fn, "waits": my, "seq": seq})
        self._record(("eng", eng, seq), reads, writes)
        return seq

    def dma(self, sem_name, fn, reads=(), writes=()):
        self.dma_sem(sem_name)
        prev = self.dma_cnt[sem_name]
        waits = self._collect(reads, writes)
        if prev > 0:
            waits.append(("dma", sem_name, prev))
        for kind, key, val in waits:
            if kind == "eng":
                self.waited[key].add(val)
        self.dma_cnt[sem_name] = prev + 16
        val = self.dma_cnt[sem_name]
        self.seq["y"] += 1
        self.ops["y"].append(
            {"fn": fn, "waits": waits, "seq": None, "dma": (sem_name, val)}
        )
        self._record(("dma", sem_name, val), reads, writes)
        return val

    def final_waits(self, eng="y"):
        def mk(name, val):
            def run(e, name=name, val=val):
                e.wait_ge(self.dma_h[name], val)
            return run
        for name, val in self.dma_cnt.items():
            if val > 0:
                self.ops[eng].append({"fn": mk(name, val), "waits": (), "seq": None,
                                      "raw": True})

    def finalize(self):
        remap = {}
        for e in ENGS:
            wl = sorted(self.waited[e])
            remap[e] = {v: i + 1 for i, v in enumerate(wl)}
        self.real_inc = {e: set(self.waited[e]) for e in ENGS}
        self.remap = remap

    def run_engine(self, e, handle):
        already = {}
        for rec in self.ops[e]:
            if rec.get("raw"):
                rec["fn"](handle)
                continue
            for kind, key, val in rec["waits"]:
                if kind == "eng":
                    rv = self.remap[key].get(val)
                    if rv is None:
                        continue
                    k = ("e", key)
                    if already.get(k, 0) >= rv:
                        continue
                    already[k] = rv
                    handle.wait_ge(self.eng_h[key], rv)
                else:
                    k = ("d", key)
                    if already.get(k, 0) >= val:
                        continue
                    already[k] = val
                    handle.wait_ge(self.dma_h[key], val)
            inst = rec["fn"](handle)
            if rec.get("dma"):
                name, val = rec["dma"]
                inst.then_inc(self.dma_h[name], 16)
            elif rec["seq"] in self.real_inc[e]:
                inst.then_inc(self.eng_h[e], 1)


def ceil_div(a, b):
    return -(-a // b)


def build_nc(S=4096, phases=("attn_a", "ffn1", "attn_b", "ffn2")):
    nc = bass.Bass("TRN2", target_bir_lowering=False)
    NS = S // 512
    NS2 = S // 256
    NCHK = S // CH
    NTC = S // 128

    dr = {}
    def din(name, shape):
        dr[name] = nc.dram_tensor(name, shape, F32, kind="ExternalInput").ap()
    def dinb(name, shape):
        dr[name] = nc.dram_tensor(name, shape, BF16, kind="ExternalInput").ap()

    din("xT", [128, NCH * S])
    for w in ("aq", "ak", "av", "ao"):
        din(w, [D, D])
    din("anw", [128, 1])
    din("w1", [D, FF]); din("w2", [FF, D])
    din("b1c", [128, NJ]); din("b2c", [128, NCH])
    for w in ("bq", "bk", "bv", "bo"):
        din(w, [D, D])
    din("bvecs", [128, 96])
    din("boc", [128, NCH])
    dinb("onesall", [128, 128])
    din("onesf32", [128, 128])
    dinb("maskt", [128, 512])
    din("cvals", [128, 8])
    din("onef", [1, 1])
    out = nc.dram_tensor("outT", [128, NCH * S], F32, kind="ExternalOutput").ap()
    import os as _os
    DBG = bool(_os.environ.get("KDBG"))
    if DBG:
        dbg1 = nc.dram_tensor("dbg1", [128, S], F32, kind="ExternalOutput").ap()
        dbg2 = nc.dram_tensor("dbg2", [128, NTC], F32, kind="ExternalOutput").ap()
        dbg3 = nc.dram_tensor("dbg3", [128, NTC * 1024], BF16, kind="ExternalOutput").ap()
    xB = nc.dram_tensor("xBbuf", [128, NCH * S], F32, kind="Internal").ap()
    wcA = nc.dram_tensor("wcA", [128, 32768], BF16, kind="Internal").ap()
    wcB = nc.dram_tensor("wcB", [128, 32768], BF16, kind="Internal").ap()
    wcC = nc.dram_tensor("wcC", [128, 32768], BF16, kind="Internal").ap()
    xC = nc.dram_tensor("xCbuf", [128, NCH * S], F32, kind="Internal").ap()

    es = ExitStack()
    AA = es.enter_context(nc.sbuf_tensor("AA", [128, 32768], BF16))
    AB = es.enter_context(nc.sbuf_tensor("AB", [128, 32768], BF16))
    XB0 = es.enter_context(nc.sbuf_tensor("XB0", [128, 4096], F32))
    XB1 = es.enter_context(nc.sbuf_tensor("XB1", [128, 4096], F32))
    LNB = es.enter_context(nc.sbuf_tensor("LNB", [128, 4096], BF16))
    BIG = es.enter_context(nc.sbuf_tensor("BIG", [128, 4096], F32))
    SQ = es.enter_context(nc.sbuf_tensor("SQ", [128, 4096], F32))
    STG = es.enter_context(nc.sbuf_tensor("STG", [128, 1024], F32))
    CON = es.enter_context(nc.sbuf_tensor("CON", [128, 128], BF16))
    CONF = es.enter_context(nc.sbuf_tensor("CONF", [128, 128], F32))
    MSK = es.enter_context(nc.sbuf_tensor("MSK", [128, 512], BF16))
    CV = es.enter_context(nc.sbuf_tensor("CV", [128, 8], F32))
    NW = es.enter_context(nc.sbuf_tensor("NW", [128, 1], F32))
    B1C = es.enter_context(nc.sbuf_tensor("B1C", [128, NJ], F32))
    B2C = es.enter_context(nc.sbuf_tensor("B2C", [128, NCH], F32))
    BVE = es.enter_context(nc.sbuf_tensor("BVE", [128, 96], F32))
    BOC = es.enter_context(nc.sbuf_tensor("BOC", [128, NCH], F32))
    ALF = es.enter_context(nc.sbuf_tensor("ALF", [128, 64], F32))
    CAR = es.enter_context(nc.sbuf_tensor("CAR", [128, 8], F32))
    CARB = es.enter_context(nc.sbuf_tensor("CARB", [128, 8], BF16))
    K8 = es.enter_context(nc.sbuf_tensor("K8", [128, 8], F32))
    SKC = es.enter_context(nc.sbuf_tensor("SKC", [128, 32], F32))
    CSB = es.enter_context(nc.sbuf_tensor("CSB", [128, NCH], F32))
    ONEF = es.enter_context(nc.sbuf_tensor("ONEF", [1, 1], F32))
    PS = [es.enter_context(nc.psum_tensor(f"PS{i}", [128, 512], F32))
          for i in range(8)]

    BIGB = BIG.ap().bitcast(BF16)
    SQB = SQ.ap().bitcast(BF16)

    P = Prog(nc, es)

    def wp(reg, col, ncol=1024, dcs=range(8)):
        cp = col // ncol
        return tuple(f"{reg}@{dc}:{cp}" for dc in dcs)

    def MMg(psum_ap, pairs, reads, writes, starts=None):
        def fn(t, psum_ap=psum_ap, pairs=pairs):
            inst = None
            for (lh, rh, st, sp, oap) in pairs:
                inst = t.matmul(oap if oap is not None else psum_ap, lh, rh,
                                start=st, stop=sp)
            return inst
        return P.op("t", fn, reads=reads, writes=writes)

    P.dma("dc", lambda y: y.dma_start(CON[:, :], dr["onesall"]), writes=("c_ones",))
    P.dma("dc", lambda y: y.dma_start(CONF[:, :], dr["onesf32"]), writes=("c_onesf",))
    P.dma("dc", lambda y: y.dma_start(MSK[:, :], dr["maskt"]), writes=("c_mask",))
    P.dma("dc", lambda y: y.dma_start(CV[:, :], dr["cvals"]), writes=("c_cv",))
    P.dma("dc", lambda y: y.dma_start(NW[:, :], dr["anw"]), writes=("c_nw",))
    P.dma("dc", lambda y: y.dma_start(B1C[:, :], dr["b1c"]), writes=("c_b1",))
    P.dma("dc", lambda y: y.dma_start(B2C[:, :], dr["b2c"]), writes=("c_b2",))
    P.dma("dc", lambda y: y.dma_start(BVE[:, :], dr["bvecs"]), writes=("c_bv",))
    P.dma("dc", lambda y: y.dma_start(BOC[:, :], dr["boc"]), writes=("c_boc",))
    P.dma("dc", lambda y: y.dma_start(ONEF[:, :], dr["onef"]), writes=("c_onef",))

    ARENA_REGS = (
        ["qfm", "kfm", "ktm", "vtm", "on", "stateB0", "stateB1", "VW", "VW#",
         "lnbB"]
        + [f"wAq@{r}:{c}" for r in range(8) for c in range(4)]
        + [f"wAk@{r}:0" for r in range(8)]
        + [f"wAv@{r}:0" for r in range(8)]
        + [f"wAo@{r}:0" for r in range(8)]
        + [f"wB@{r}:0" for r in range(32)]
        + ["wps0", "wps1", "wcb0", "wcb1"]
    )

    ABF_T = AB.ap().bitcast(F32)

    def wprep_piece(dname, rc, cache, coff, k):
        slot = k % 2
        stg = ABF_T[:, 12288 + slot * 1024: 13312 + slot * 1024]
        bnc = AB[:, 22528 + slot * 1024: 23552 + slot * 1024]
        sreg, breg = f"wps{slot}", f"wcb{slot}"
        src = dr[dname][rc * 128:(rc + 1) * 128, :]
        P.dma(f"dwp{slot}", (lambda y, d=stg, so=src: y.dma_start(d, so)),
              reads=(), writes=(sreg,))
        P.op("p", (lambda p, d=bnc, so=stg: p.tensor_copy(d, so)),
             reads=(sreg,), writes=(breg,))
        dst = cache[:, coff + rc * 1024: coff + (rc + 1) * 1024]
        P.dma(f"dwo{slot}", (lambda y, d=dst, so=bnc: y.dma_start(d, so)),
              reads=(breg,), writes=(f"wc:{dname}",))

    def load_weights(src_list, slot1=None):
        P.op("v", (lambda v: v.memset(K8[0:1, 0:1], 0.0)),
             reads=(), writes=tuple(ARENA_REGS))
        si = 0
        AL1 = ("lnm2", "ln_mb", "ln_rs", "x2a", "x2b", "ln_tmp",
               "qsig0", "qsig1", "at0", "at1", "u0")
        if slot1 is None:
            s1_ap, s1_al, s1_tail = BIG, AL1, (lambda dn, rc, cp: False)
        else:
            s1_ap, s1_al, s1_tail = slot1
        for (dname, nrow, ncol, reg, dtile, coff) in src_list:
            npc = ceil_div(ncol, 1024)
            pc = ncol // npc
            for cp in range(npc):
                for rc in range(nrow):
                    slot = si % 2
                    si += 1
                    if s1_tail(dname, rc, cp):
                        slot = 0
                    stg_ap = (STG[:, 0:pc], s1_ap[:, 0:pc])[slot]
                    sreg = f"stg{slot}"
                    al = (() if slot == 0 else tuple(s1_al))
                    src = dr[dname][rc * 128:(rc + 1) * 128,
                                    cp * pc:(cp + 1) * pc]
                    P.dma(f"dw{slot}",
                          (lambda y, d=stg_ap, s=src: y.dma_start(d, s)),
                          reads=(), writes=(sreg,) + al)
                    off = coff + rc * ncol + cp * pc
                    dst = dtile[:, off: off + pc]
                    P.op("v", (lambda e, d=dst, s=stg_ap: e.tensor_copy(d, s)),
                         reads=(sreg,) + al, writes=(f"{reg}@{rc}:{cp}",))


    def ln_pass(T, xap, xregs, lnap, lnreg, ps_m, ps_s, sb, mr_store=None,
                extra_war=(), mid=None, x2_bb=None, split_apply=False,
                math_war=(), apply_war=(), rsqrt_dve=False, alt_scratch=False):
        pm, psx = f"PS{ps_m}", f"PS{ps_s}"
        if sb == "ffnpre":
            LNBF2 = LNB.ap().bitcast(F32)
            lnm2 = LNBF2[:, 1152:1152 + T]
            x2a = LNB[:, 2816:2816 + T]
            x2b = LNB[:, 2816 + T:2816 + 2 * T]
            bb = None
        else:
            lnm2 = BIG[:, 2048:2048 + T] if alt_scratch else BIG[:, sb:sb + T]
            bb = 2 * (sb + 3 * T) if x2_bb is None else x2_bb
            x2a = BIGB[:, bb:bb + T]
            x2b = BIGB[:, bb + T:bb + 2 * T]
        if bb is None:
            tmp = tmp2 = None
        elif alt_scratch:
            STGB2 = STG.ap().bitcast(BF16)
            tmp = STGB2[:, 1024:1024 + T]
            tmp2 = STGB2[:, 1024 + T:1024 + 2 * T]
        elif x2_bb is None:
            tmp = BIGB[:, bb + 2 * T:bb + 3 * T]
            tmp2 = tmp
        else:
            tmp = BIGB[:, 3072:3072 + T]
            tmp2 = BIGB[:, 3072 + T:3072 + 2 * T]
        if mr_store is None:
            if alt_scratch:
                mb = BIG[:, 2560:2560 + T]
                rs = BIG[:, 3072:3072 + T]
            else:
                mb = BIG[:, sb + T:sb + 2 * T]
                rs = BIG[:, sb + 2 * T:sb + 3 * T]
            mbr, rsr = "ln_mb", "ln_rs"
        else:
            mb, rs, mbr, rsr = mr_store
        for c in range(NCH):
            xa = xap[:, c * T:(c + 1) * T]
            if mr_store is None:
                la = lnap[:, c * T:(c + 1) * T]
                wl = (lnreg,) if c == 0 else (lnreg + "#",)
                lrd = ("c_ones", lnreg, lnreg + "#")
            else:
                la = LNB[:, 3328 + (c % 2) * T: 3328 + (c % 2) * T + T]
                wl = ("pxa",) if c % 2 == 0 else ("pxb",)
                lrd = ("c_ones",) + wl
            if c == 0:
                wl = wl + tuple(extra_war)
            P.op("v", (lambda v, d=la, so=xa: v.tensor_copy(d, so)),
                 reads=xregs, writes=wl)
            x2 = x2a if c % 2 == 0 else x2b
            x2r = "x2a" if c % 2 == 0 else "x2b"
            P.op("s", (lambda s_, d=x2, so=xa: s_.activation(d, so, AF.Square)),
                 reads=xregs, writes=(x2r,) + (tuple(extra_war) if c == 0 else ()))
            P.op("t", (lambda t, o=PS[ps_m][:, 0:T], l=la, st=(c == 0),
                       sp=(c == NCH - 1):
                       t.matmul(o, CON[:, :], l, start=st, stop=sp)),
                 reads=lrd, writes=(pm,))
            P.op("t", (lambda t, o=PS[ps_s][:, 0:T], l=x2, st=(c == 0),
                       sp=(c == NCH - 1):
                       t.matmul(o, CON[:, :], l, start=st, stop=sp)),
                 reads=("c_ones", x2r), writes=(psx,))
        if mid is not None:
            mid()
        P.op("v", (lambda v, d=mb: v.tensor_scalar(d, PS[ps_m][:, 0:T],
                                                   1.0 / D, None, ALU.mult)),
             reads=(pm,), writes=(mbr,) + tuple(math_war))
        P.op("v", (lambda v, d=lnm2, m=mb: v.tensor_tensor(d, m, m, ALU.mult)),
             reads=(mbr,), writes=("lnm2",))
        P.op("v", (lambda v, d=lnm2, m=lnm2: v.scalar_tensor_tensor(
            d, PS[ps_s][:, 0:T], 1.0 / D, m, ALU.mult, ALU.subtract)),
             reads=(psx, "lnm2"), writes=("lnm2",))
        if not rsqrt_dve:
            P.op("s", (lambda s_, d=lnm2: s_.activation(d, d, AF.Ln, bias=CV[:, 1:2])),
                 reads=("lnm2", "c_cv"), writes=("lnm2",))
            P.op("s", (lambda s_, d=rs, so=lnm2: s_.activation(d, so, AF.Exp,
                                                               scale=-0.5)),
                 reads=("lnm2",), writes=(rsr,))
        else:
            I32 = mybir.dt.int32
            if sb == "ffnpre":
                LNBF3 = LNB.ap().bitcast(F32)
                yv = LNBF3[:, 1408:1408 + T]
                t1v = LNBF3[:, 1664:1664 + T]
            else:
                yv = BIG[:, 3584:3584 + T]
                t1v = STG[:, 0:T] if alt_scratch else BIG[:, 1536:1536 + T]
            yregs = ("x2a", "x2b")
            if sb == "ffnpre":
                tregs = ("pxa", "pxb")
            elif alt_scratch:
                tregs = ("ln_tmp", "ln_tmp2", "stg0")
            else:
                tregs = ("ln_tmp", "ln_tmp2")
            yi = yv.bitcast(I32)
            li = lnm2.bitcast(I32)
            P.op("v", (lambda v, d=lnm2: v.tensor_scalar(d, d, LN_EPS, None,
                                                         ALU.add)),
                 reads=("lnm2",), writes=("lnm2",))
            P.op("v", (lambda v, d=yi, so=li: v.tensor_scalar(
                d, so, 1, None, ALU.arith_shift_right)),
                 reads=("lnm2",), writes=yregs)
            P.op("v", (lambda v, d=yi: v.tensor_scalar(d, d, 0x5f3759df, None,
                                                       ALU.subtract)),
                 reads=yregs, writes=yregs)
            P.op("v", (lambda v, d=yi: v.tensor_scalar(d, d, -1, None,
                                                       ALU.bitwise_xor)),
                 reads=yregs, writes=yregs)
            P.op("v", (lambda v, d=yi: v.tensor_scalar(d, d, 1, None, ALU.add)),
                 reads=yregs, writes=yregs)
            for it in range(3):
                last = it == 2
                P.op("v", (lambda v, d=t1v, y2=yv: v.tensor_tensor(
                    d, y2, y2, ALU.mult)),
                     reads=yregs, writes=tregs)
                P.op("v", (lambda v, d=t1v, l2=lnm2: v.tensor_tensor(
                    d, d, l2, ALU.mult)),
                     reads=tregs + ("lnm2",), writes=tregs)
                P.op("v", (lambda v, d=t1v: v.tensor_scalar(
                    d, d, -0.5, 1.5, ALU.mult, ALU.add)),
                     reads=tregs, writes=tregs)
                if last:
                    P.op("v", (lambda v, d=rs, y2=yv, t2=t1v: v.tensor_tensor(
                        d, y2, t2, ALU.mult)),
                         reads=yregs + tregs, writes=(rsr,))
                else:
                    P.op("v", (lambda v, d=yv, t2=t1v: v.tensor_tensor(
                        d, d, t2, ALU.mult)),
                         reads=yregs + tregs, writes=yregs)

        if mr_store is None:
            for c in range(NCH):
                eng = "p" if (split_apply and c >= NCH // 2) else "v"
                tm_ = tmp2 if eng == "p" else tmp
                tr_ = "ln_tmp2" if eng == "p" else "ln_tmp"
                aw = tuple(apply_war) if c in (0, NCH // 2) else ()
                xa = xap[:, c * T:(c + 1) * T]
                la = lnap[:, c * T:(c + 1) * T]
                P.op(eng, (lambda e, d=tm_, so=xa, m=mb:
                           e.tensor_tensor(d, so, m, ALU.subtract)),
                     reads=xregs + (mbr, tr_), writes=(tr_,) + aw)
                P.op(eng, (lambda e, d=la, t=tm_, r=rs:
                           e.tensor_tensor(d, t, r, ALU.mult)),
                     reads=(tr_, rsr, lnreg, lnreg + "#"), writes=(lnreg,))

    def x_load(buf_ap, dram_ap, s, T, slot_regs, sem):
        src = dram_ap[:, s * NCH * T:(s + 1) * NCH * T]
        return P.dma(sem, (lambda y, d=buf_ap, so=src: y.dma_start(d, so)),
                     reads=(), writes=tuple(slot_regs))

    def x_store(buf_ap, dram_ap, s, T, slot_regs, sem, hregs):
        dst = dram_ap[:, s * NCH * T:(s + 1) * NCH * T]
        return P.dma(sem, (lambda y, d=dst, so=buf_ap: y.dma_start(d, so)),
                     reads=tuple(slot_regs), writes=tuple(hregs))

    def phase_attn_a(src_dram, src_regs, dst_dram, dst_pref):
        T = 512
        QF, KF, KT, VT, ON, STB = 0, 4096, 8192, 12288, 16384, 20480
        P.op("v", lambda v: v.memset(AB[:, STB:STB + 2048], 0.0), writes=("stateB0", "stateB1"))
        P.op("v", lambda v: v.memset(SQ[:, 0:1024], 0.0), writes=("statef",))
        P.op("v", lambda v: v.memset(CAR[:, :], 0.0), writes=("car32",))
        P.op("v", lambda v: v.memset(CARB[:, :], 0.0), writes=("carb",))

        nsl = S // T

        def sctx(s):
            sl = s % 2
            xb = (XB0 if sl == 0 else XB1).ap()
            return sl, xb, (f"xs{2*sl}", f"xs{2*sl+1}")

        def load_ln(s):
            sl, xb, xregs = sctx(s)
            x_load(xb, src_dram, s, T, xregs, f"dx{sl}")
            ln_pass(T, xb, xregs, LNB.ap(), "lnb", 6, 7, 0)

        load_ln(0)
        load_weights([("aq", 8, 1024, "wAq", AA, 0),
                      ("ak", 8, 1024, "wAk", AA, 8192),
                      ("av", 8, 1024, "wAv", AA, 16384),
                      ("ao", 8, 1024, "wAo", AA, 24576)])
        wprep_jobs = []
        for k in range(32):
            rc, cp = k // 4, k % 4
            wprep_jobs.append(("w1pc", (rc, cp)))
        for rc in range(32):
            wprep_jobs.append(("w2", rc))
        for wi, nm in enumerate(("bq", "bk", "bv", "bo")):
            for rc in range(8):
                wprep_jobs.append((nm, (wi, rc)))
        wprep_k = [0]

        def wprep_emit(n):
            for _ in range(n):
                if wprep_k[0] >= len(wprep_jobs):
                    return
                job = wprep_jobs[wprep_k[0]]
                k = wprep_k[0]
                wprep_k[0] += 1
                if job[0] == "w1pc":
                    rc, cp = job[1]
                    slot = k % 2
                    stg = ABF_T[:, 12288 + slot * 1024: 13312 + slot * 1024]
                    bnc = AB[:, 22528 + slot * 1024: 23552 + slot * 1024]
                    sreg, breg = f"wps{slot}", f"wcb{slot}"
                    src = dr["w1"][rc * 128:(rc + 1) * 128,
                                   cp * 1024:(cp + 1) * 1024]
                    P.dma(f"dwp{slot}", (lambda y, d=stg, so=src:
                                         y.dma_start(d, so)),
                          reads=(), writes=(sreg,))
                    P.op("p", (lambda p, d=bnc, so=stg: p.tensor_copy(d, so)),
                         reads=(sreg,), writes=(breg,))
                    dst = wcA[:, rc * 4096 + cp * 1024: rc * 4096 + (cp + 1) * 1024]
                    P.dma(f"dwo{slot}", (lambda y, d=dst, so=bnc:
                                         y.dma_start(d, so)),
                          reads=(breg,), writes=("wc:w1",))
                elif job[0] == "w2":
                    wprep_piece("w2", job[1], wcB, 0, k)
                else:
                    wi, rc = job[1]
                    wprep_piece(job[0], rc, wcC, wi * 8192, k)
        P.op("v", (lambda v: v.tensor_scalar(AA[:, 24576:32768], AA[:, 24576:32768],
                                             NW[:, 0:1], None, ALU.mult)),
             reads=wp("wAo", 0) + ("c_nw",), writes=wp("wAo", 0))
        for s in range(nsl):
            sl, xb, xregs = sctx(s)
            def proj(dst_off, woff, dreg, kind):
                if kind == "fm":
                    for ec in range(NCH):
                        pb = ec % 2
                        pairs = []
                        for dc in range(NCH):
                            lh = AA[:, woff + dc * 1024 + ec * 128:
                                    woff + dc * 1024 + ec * 128 + 128]
                            rh = LNB[:, dc * T:(dc + 1) * T]
                            pairs.append((lh, rh, dc == 0, dc == 7, PS[pb][:, 0:T]))
                        MMg(None, pairs, reads=wp(dreg, 0) + ("lnb", f"PS{pb}"),
                            writes=(f"PS{pb}",))
                        dst = AB[:, dst_off + ec * T: dst_off + (ec + 1) * T]
                        P.op("v", (lambda v, d=dst, so=PS[pb][:, 0:T]:
                                   v.tensor_copy(d, so)),
                             reads=(f"PS{pb}",),
                             writes=(("qfm",) if dst_off == QF else ("kfm",)))
                else:
                    for t4 in range(4):
                        for eh in range(2):
                            pb = (t4 * 2 + eh) % 2
                            pairs = []
                            for dc in range(NCH):
                                lh = LNB[:, dc * T + t4 * 128: dc * T + t4 * 128 + 128]
                                rh = AA[:, woff + dc * 1024 + eh * 512:
                                        woff + dc * 1024 + (eh + 1) * 512]
                                pairs.append((lh, rh, dc == 0, dc == 7,
                                              PS[pb][:, 0:512]))
                            MMg(None, pairs, reads=wp(dreg, 0) + ("lnb", f"PS{pb}"),
                                writes=(f"PS{pb}",))
                            dst = AB[:, dst_off + t4 * 1024 + eh * 512:
                                     dst_off + t4 * 1024 + (eh + 1) * 512]
                            P.op("v", (lambda v, d=dst, so=PS[pb][:, 0:512]:
                                       v.tensor_copy(d, so)),
                                 reads=(f"PS{pb}",),
                                 writes=(("ktm",) if dst_off == KT else ("vtm",)))
            proj(QF, 0, "wAq", "fm")
            proj(KF, 8192, "wAk", "fm")
            proj(KT, 8192, "wAk", "tm")
            proj(VT, 16384, "wAv", "tm")

            for pp in range(T // 128):
                g = s * (T // 128) + pp
                par = g % 2
                stb_new = STB + 1024 * par
                stb_old = STB + 1024 * (1 - par)
                sb_new, sb_old = f"stateB{par}", f"stateB{1 - par}"
                for hg in range(2):
                    pairs = []
                    for hi in range(4):
                        h = hg * 4 + hi
                        lh = AB[:, KF + h * T + pp * 128: KF + h * T + pp * 128 + 128]
                        rh = AB[:, QF + h * T + pp * 128: QF + h * T + pp * 128 + 128]
                        pairs.append((lh, rh, hi == 0, hi == 3,
                                      PS[hg][:, hi * 128:(hi + 1) * 128]))
                    MMg(None, pairs, reads=("kfm", "qfm", f"PS{hg}"),
                        writes=(f"PS{hg}",))
                pairs = []
                for h in range(H):
                    kt = AB[:, KT + pp * 1024 + h * 128: KT + pp * 1024 + (h + 1) * 128]
                    vt = AB[:, VT + pp * 1024 + h * 128: VT + pp * 1024 + (h + 1) * 128]
                    bank = 2 + h // 4
                    oap = PS[bank][:, (h % 4) * 128:(h % 4 + 1) * 128]
                    pairs.append((kt, vt, h % 4 == 0, h % 4 == 3, oap))
                MMg(None, pairs, reads=("ktm", "vtm", "PS2", "PS3"),
                    writes=("PS2", "PS3"))
                sms = []
                for hg in range(2):
                    smx = BIGB[:, 6144 + hg * 512: 6656 + hg * 512]
                    smr = f"sm{hg}"
                    sms.append((smx, smr))
                    P.op("v", (lambda v, d=smx, pbs=hg:
                               v.tensor_tensor(d, PS[pbs][:, 0:512], MSK[:, :],
                                               ALU.mult)),
                         reads=(f"PS{hg}", "c_mask"), writes=(smr,))
                P.op("v", (lambda v: v.tensor_tensor(SQ[:, 0:512], SQ[:, 0:512],
                                                     PS[2][:, 0:512], ALU.add)),
                     reads=("PS2", "statef"), writes=("statef",))
                P.op("v", (lambda v: v.tensor_tensor(SQ[:, 512:1024], SQ[:, 512:1024],
                                                     PS[3][:, 0:512], ALU.add)),
                     reads=("PS3", "statef"), writes=("statef",))
                P.op("v", (lambda v, d=stb_new: v.tensor_copy(
                    AB[:, d:d + 1024], SQ[:, 0:1024])),
                     reads=("statef",), writes=(sb_new,))
                for hg in range(2):
                    smx, smr = sms[hg]
                    pb_o = 4 + hg
                    pairs = []
                    for hi in range(4):
                        h = hg * 4 + hi
                        vsl = AB[:, VT + pp * 1024 + h * 128:
                                 VT + pp * 1024 + (h + 1) * 128]
                        smh = smx[:, hi * 128:(hi + 1) * 128]
                        oap = PS[pb_o][:, hi * 128:(hi + 1) * 128]
                        pairs.append((vsl, smh, hi == 0, False, oap))
                        stt = AB[:, stb_old + h * 128: stb_old + (h + 1) * 128]
                        qsl = AB[:, QF + h * T + pp * 128: QF + h * T + pp * 128 + 128]
                        pairs.append((stt, qsl, False, hi == 3, oap))
                    MMg(None, pairs, reads=("vtm", smr, sb_old, "qfm",
                                            f"PS{pb_o}"),
                        writes=(f"PS{pb_o}",))
                    pairs = []
                    for hi in range(4):
                        h = hg * 4 + hi
                        cb = CARB[:, h:h + 1].broadcast_to([128, 128])
                        qsl = AB[:, QF + h * T + pp * 128: QF + h * T + pp * 128 + 128]
                        oap = PS[6][:, hi * 128:(hi + 1) * 128]
                        pairs.append((cb, qsl, hi == 0, False, oap))
                        smh = smx[:, hi * 128:(hi + 1) * 128]
                        pairs.append((CON[:, :], smh, False, hi == 3, oap))
                    MMg(None, pairs, reads=("carb", "qfm", smr, "c_ones", "PS6"),
                        writes=("PS6",))
                    o2 = BIGB[:, 5632:6144]
                    P.op("s", (lambda s_, d=o2, pbo=pb_o:
                               s_.activation(d, PS[pbo][:, 0:512], AF.Square)),
                         reads=(f"PS{pb_o}",), writes=("o2",))
                    sbc = BIG[:, 2304:2816]
                    P.op("s", (lambda s_, d=sbc: s_.activation(
                        d, PS[6][:, 0:512], AF.Square)),
                         reads=("PS6",), writes=("sbc",))
                    pairs = []
                    for hi in range(4):
                        pairs.append((CON[:, :], o2[:, hi * 128:(hi + 1) * 128],
                                      hi == 0, hi == 3,
                                      PS[7][:, hi * 128:(hi + 1) * 128]))
                    MMg(None, pairs, reads=("o2", "c_ones", "PS7"), writes=("PS7",))
                    P.op("v", (lambda v, d=sbc: v.tensor_scalar(
                        d, d, 1.0, RMS_EPS, ALU.max, ALU.mult)),
                         reads=("sbc",), writes=("sbc",))
                    P.op("v", (lambda v, d=sbc: v.scalar_tensor_tensor(
                        d, PS[7][:, 0:512], 1.0 / DH, d, ALU.mult, ALU.add)),
                         reads=("PS7", "sbc"), writes=("sbc",))
                    P.op("s", (lambda s_, d=sbc: s_.activation(d, d, AF.Ln)),
                         reads=("sbc",), writes=("sbc",))
                    P.op("s", (lambda s_, d=sbc: s_.activation(d, d, AF.Exp,
                                                               scale=-0.5)),
                         reads=("sbc",), writes=("sbc",))
                    onv = AB[:, ON:ON + 4096].rearrange(
                        "p (h k t) -> p h k t", h=H, k=T // 128)[
                        :, hg * 4:(hg + 1) * 4, pp, :]
                    piv = PS[pb_o][:, 0:512].rearrange("p (h t) -> p h t", h=4)
                    sbv = sbc.rearrange("p (h t) -> p h t", h=4)
                    P.op("v", (lambda v, d=onv, a=piv, b=sbv:
                               v.tensor_tensor(d, a, b, ALU.mult)),
                         reads=(f"PS{pb_o}", "sbc"), writes=("on",))
                kfv = AB[:, KF:KF + 4096].rearrange("p (h t) -> p h t", h=H)[
                    :, :, pp * 128:(pp + 1) * 128]
                P.op("v", (lambda v, d=K8[:, :], so=kfv:
                           v.tensor_reduce(d, so, AX.X, ALU.add)),
                     reads=("kfm",), writes=("k8",))
                P.op("v", (lambda v: v.tensor_tensor(CAR[:, :], CAR[:, :],
                                                     K8[:, :], ALU.add)),
                     reads=("car32", "k8"), writes=("car32",))
                P.op("v", (lambda v: v.tensor_copy(CARB[:, :], CAR[:, :])),
                     reads=("car32",), writes=("carb",))
            wprep_emit(ceil_div(len(wprep_jobs), nsl))
            if s + 1 < nsl:
                load_ln(s + 1)
            for doc in range(NCH):
                pb = doc % 2
                pairs = []
                for ec in range(NCH):
                    lh = AA[:, 24576 + ec * 1024 + doc * 128:
                            24576 + ec * 1024 + doc * 128 + 128]
                    rh = AB[:, ON + ec * T: ON + (ec + 1) * T]
                    pairs.append((lh, rh, ec == 0, ec == 7, PS[pb][:, 0:T]))
                MMg(None, pairs, reads=wp("wAo", 0) + ("on", f"PS{pb}"), writes=(f"PS{pb}",))
                xv = xb[:, doc * T:(doc + 1) * T]
                P.op("v", (lambda v, d=xv, pbb=pb: v.tensor_tensor(
                    d, d, PS[pbb][:, 0:T], ALU.add)),
                     reads=(f"PS{pb}",) + xregs, writes=(xregs[0], xregs[1]))
            x_store(xb, dst_dram, s, T, xregs, f"ds{sl}",
                    (f"{dst_pref}:{2*s}", f"{dst_pref}:{2*s+1}"))

    def phase_ffn(src_dram, src_pref, dst_dram, dst_pref, final=False, wmode='plain'):
        T = 256
        nsl = S // T
        def prepass(s):
            sl = s % 4
            xb = (XB0 if sl < 2 else XB1).ap()[
                :, (sl % 2) * 2048:(sl % 2) * 2048 + 2048]
            xreg = f"xs{sl}"
            blk = src_dram[:, (s // 2) * (NCH * 512):(s // 2 + 1) * (NCH * 512)]
            src = blk.rearrange("p (c t) -> p c t", c=NCH)[
                :, :, (s % 2) * T:(s % 2) * T + T]
            dst = xb.rearrange("p (c t) -> p c t", c=NCH)
            P.dma(f"dx{sl}", (lambda y, d=dst, so=src: y.dma_start(d, so)),
                  reads=(f"{src_pref}:{s}",) if src_pref else (),
                  writes=(xreg,))
            mb = SQB[:, s * T:(s + 1) * T]
            rs = SQB[:, 4096 + s * T: 4096 + (s + 1) * T]
            ln_pass(T, xb, (xreg,), LNB.ap()[:, 0:2048], "lnb", 4, 5, "ffnpre",
                    mr_store=(mb, rs, f"mrm{s}", f"mrr{s}"), rsqrt_dve=True)

        def fctx(s):
            sl = s % 4
            xb = (XB0 if sl < 2 else XB1).ap()[
                :, (sl % 2) * 2048:(sl % 2) * 2048 + 2048]
            return sl, xb, f"xs{sl}"

        def fpre(s):
            sl, xb, xreg = fctx(s)
            mb = SQB[:, s * T:(s + 1) * T]
            rs = SQB[:, 4096 + s * T: 4096 + (s + 1) * T]
            for c in range(NCH):
                la = LNB[:, c * T:(c + 1) * T]
                P.op("v", (lambda v, d=la, so=xb[:, c * T:(c + 1) * T]:
                           v.tensor_copy(d, so)),
                     reads=(xreg,), writes=("lnb",) if c == 0 else ("lnb#",))
            for c in range(NCH):
                la = LNB[:, c * T:(c + 1) * T]
                P.op("v", (lambda v, d=la, m=mb: v.tensor_tensor(d, d, m, ALU.subtract)),
                     reads=("lnb", "lnb#", f"mrm{s}"), writes=("lnb",))
                P.op("v", (lambda v, d=la, r=rs: v.tensor_tensor(d, d, r, ALU.mult)),
                     reads=("lnb", f"mrr{s}"), writes=("lnb",))

        for _k in range(min(3, nsl)):
            prepass(_k)
        fpre(0)
        ABF = AB.ap().bitcast(F32)
        if wmode == "load":
            P.op("v", (lambda v: v.memset(K8[0:1, 0:1], 0.0)),
                 reads=(), writes=tuple(ARENA_REGS))
            for k in range(8):
                regs = tuple(f"wAq@{r}:{c}" for (r, c) in
                             [((k * 4 + i) // 4, (k * 4 + i) % 4) for i in range(4)])
                P.dma(f"dw{k % 2}", (lambda y, kk=k: y.dma_start(
                    AA[:, kk * 4096:(kk + 1) * 4096],
                    wcA[:, kk * 4096:(kk + 1) * 4096])),
                      reads=("wc:w1",), writes=regs)
            for k in range(8):
                regs = tuple(f"wB@{r}:0" for r in range(k * 4, k * 4 + 4))
                P.dma(f"dw{k % 2}", (lambda y, kk=k: y.dma_start(
                    AB[:, kk * 4096:(kk + 1) * 4096],
                    wcB[:, kk * 4096:(kk + 1) * 4096])),
                      reads=("wc:w2",), writes=regs)
        else:
            load_weights([("w1", 8, 4096, "wAq", AA, 0),
                          ("w2", 32, 1024, "wB", AB, 0)],
                         slot1=(ABF[:, 15360:16384], ("wB@30:0", "wB@31:0"),
                                (lambda dn, rc, cp: dn == "w2" and rc >= 30)))
            if wmode == "store":
                for k in range(8):
                    regs = tuple(f"wAq@{r}:{c}" for (r, c) in
                                 [((k * 4 + i) // 4, (k * 4 + i) % 4)
                                  for i in range(4)])
                    P.dma(f"dwc{k % 2}", (lambda y, kk=k: y.dma_start(
                        wcA[:, kk * 4096:(kk + 1) * 4096],
                        AA[:, kk * 4096:(kk + 1) * 4096])),
                          reads=regs, writes=("wcA",))
                for k in range(8):
                    regs = tuple(f"wB@{r}:0" for r in range(k * 4, k * 4 + 4))
                    P.dma(f"dwc{k % 2}", (lambda y, kk=k: y.dma_start(
                        wcB[:, kk * 4096:(kk + 1) * 4096],
                        AB[:, kk * 4096:(kk + 1) * 4096])),
                          reads=regs, writes=("wcB",))
        for s in range(nsl):
            sl, xb, xreg = fctx(s)
            for j in range(NJ):
                pb = j % 2
                pairs = []
                for dc in range(NCH):
                    lh = AA[:, dc * 4096 + j * 128: dc * 4096 + j * 128 + 128]
                    rh = LNB[:, dc * T:(dc + 1) * T]
                    pairs.append((lh, rh, dc == 0, dc == 7, PS[pb][:, 0:T]))
                MMg(None, pairs, reads=wp("wAq", j * 128, 1024) + ("lnb", f"PS{pb}"), writes=(f"PS{pb}",))
                hj = BIGB[:, j * T:(j + 1) * T]
                P.op("s", (lambda s_, d=hj, pbb=pb, jj=j: s_.activation(
                    d, PS[pbb][:, 0:T], AF.Gelu, bias=B1C[:, jj:jj + 1])),
                     reads=(f"PS{pb}", "c_b1"), writes=("hsb",) if j == 0 else ("hsb#",))
            if s + 1 < nsl:
                fpre(s + 1)
            if s + 3 < nsl:
                prepass(s + 3)
            for doc in range(NCH):
                pb = (2, 3, 6, 7)[doc % 4]
                pairs = []
                for j in range(NJ):
                    lh = AB[:, j * 1024 + doc * 128: j * 1024 + doc * 128 + 128]
                    rh = BIGB[:, j * T:(j + 1) * T]
                    pairs.append((lh, rh, j == 0, j == NJ - 1, PS[pb][:, 0:T]))
                MMg(None, pairs,
                    reads=tuple(f"wB@{r2}:0" for r2 in range(32)) +
                          ("hsb", "hsb#", f"PS{pb}"),
                    writes=(f"PS{pb}",))
                xv = xb[:, doc * T:(doc + 1) * T]
                P.op("v", (lambda v, d=xv, pbb=pb, dd=doc: v.scalar_tensor_tensor(
                    d, PS[pbb][:, 0:T], B2C[:, dd:dd + 1], d, ALU.add, ALU.add)),
                     reads=(f"PS{pb}", "c_b2", xreg), writes=(xreg,))
            dblk = dst_dram[:, (s // 2) * (NCH * 512):(s // 2 + 1) * (NCH * 512)]
            dstd = dblk.rearrange("p (c t) -> p c t", c=NCH)[
                :, :, (s % 2) * T:(s % 2) * T + T]
            srcs = xb.rearrange("p (c t) -> p c t", c=NCH)
            P.dma(f"ds{s % 2}", (lambda y, d=dstd, so=srcs: y.dma_start(d, so)),
                  reads=(xreg,), writes=(f"{dst_pref}:{s}",))

    def phase_attn_b(src_dram, src_pref, dst_dram, dst_pref, wmode='plain'):
        T = 512
        for i, base in enumerate((0, 32, 64)):
            g = BVE[:, base:base + 8]; b = BVE[:, base + 8:base + 16]
            m = BVE[:, base + 16:base + 24]; vv = BVE[:, base + 24:base + 32]
            al = ALF[:, i * 16:i * 16 + 8]; be = ALF[:, i * 16 + 8:i * 16 + 16]
            tmp = ALF[:, 48:56]
            P.op("s", (lambda s_, d=tmp, so=vv: s_.activation(d, so, AF.Ln,
                                                              bias=CV[:, 2:3])),
                 reads=("c_bv", "c_cv"), writes=("alf_t",))
            P.op("s", (lambda s_, d=tmp: s_.activation(d, d, AF.Exp, scale=-0.5)),
                 reads=("alf_t",), writes=("alf_t",))
            P.op("v", (lambda v, d=al, gg=g, t=tmp: v.scalar_tensor_tensor(
                d, gg, 4.0, t, ALU.mult, ALU.mult)),
                 reads=("c_bv", "alf_t"), writes=(f"alf{i}",))
            P.op("v", (lambda v, d=tmp, a=al, mm=m: v.tensor_tensor(d, a, mm, ALU.mult)),
                 reads=(f"alf{i}", "c_bv"), writes=("alf_t",))
            P.op("v", (lambda v, d=be, bb=b, t=tmp: v.scalar_tensor_tensor(
                d, bb, 4.0, t, ALU.mult, ALU.subtract)),
                 reads=("c_bv", "alf_t"), writes=(f"alf{i}b",))
            P.op("v", (lambda v, d=be: v.tensor_scalar(d, d, 4.0, None, ALU.subtract)),
                 reads=(f"alf{i}b",), writes=(f"alf{i}b",))
        nsl = S // T

        def bctx(s):
            sl = s % 2
            xb = (XB0 if sl == 0 else XB1).ap()
            return sl, xb, (f"xs{2*sl}", f"xs{2*sl+1}")

        LNODD = AB[:, 28672:32768]

        def lnbuf(s):
            if s % 2 == 0 or s == nsl - 1:
                return LNB.ap(), "lnb"
            return LNODD, "lnbB"

        def qk_sums(s):
            lb, lreg = lnbuf(s)
            for (ii, woff, psb) in ((0, 0, 2), (1, 8192, 3)):
                wreg = "wAq" if ii == 0 else "wAk"
                for ec in range(NCH):
                    pb = ec % 2
                    pairs = []
                    for dc in range(NCH):
                        lh = AA[:, woff + dc * 1024 + ec * 128:
                                woff + dc * 1024 + ec * 128 + 128]
                        rh = lb[:, dc * T:(dc + 1) * T]
                        pairs.append((lh, rh, dc == 0, dc == 7, PS[pb][:, 0:T]))
                    MMg(None, pairs, reads=wp(wreg, 0) + (lreg, f"PS{pb}"),
                        writes=(f"PS{pb}",))
                    qs = BIG[:, (ec % 2) * 512:(ec % 2) * 512 + 512]
                    qreg = f"qsig{ec % 2}"
                    wl = (qreg,)
                    if ii == 0 and ec == 0:
                        wl = wl + ("vfm", "vfm#")
                    P.op("s", (lambda s_, d=qs, pbb=pb, i2=ii, e2=ec: s_.activation(
                        d, PS[pbb][:, 0:T], AF.Sigmoid,
                        bias=ALF[:, i2 * 16 + 8 + e2:i2 * 16 + 9 + e2],
                        scale=ALF[:, i2 * 16 + e2:i2 * 16 + 1 + e2])),
                         reads=(f"PS{pb}", f"alf{ii}", f"alf{ii}b"), writes=wl)
                    P.op("t", (lambda t, o=PS[psb][:, 0:T], q=qs, st=(ec == 0),
                               sp=(ec == NCH - 1):
                               t.matmul(o, CONF[:, :], q, start=st, stop=sp)),
                         reads=(qreg, "c_onesf", f"PS{psb}"), writes=(f"PS{psb}",))
            P.op("s", (lambda e, ss=s: e.copy(SQ[:, ss * T:(ss + 1) * T],
                                              PS[2][:, 0:T])),
                 reads=("PS2",), writes=("sq",) if s == 0 else ("sq#",))
            krow = BIG[0:1, 1024:1024 + T]
            P.op("s", (lambda s_, d=krow: s_.copy(d, PS[3][0:1, 0:T])),
                 reads=("PS3",), writes=("krow",))
            pairs = []
            for jseg in range(T // 128):
                pairs.append((krow[0:1, jseg * 128:(jseg + 1) * 128], ONEF[:, :],
                              jseg == 0, jseg == T // 128 - 1,
                              PS[3][:, jseg:jseg + 1]))
            MMg(None, pairs, reads=("krow", "c_onef", "PS3"), writes=("PS3",))
            P.op("s", (lambda e, ss=s: e.copy(
                SKC[:, ss * (T // 128):(ss + 1) * (T // 128)],
                PS[3][:, 0:T // 128])),
                 reads=("PS3",), writes=("skc",) if s == 0 else ("skc#",))

        def v_proj(s):
            lb, lreg = lnbuf(s)
            for ec in range(NCH):
                pb = ec % 2
                pairs = []
                for dc in range(NCH):
                    lh = AA[:, 16384 + dc * 1024 + ec * 128:
                            16384 + dc * 1024 + ec * 128 + 128]
                    rh = lb[:, dc * T:(dc + 1) * T]
                    pairs.append((lh, rh, dc == 0, dc == 7, PS[pb][:, 0:T]))
                MMg(None, pairs, reads=wp("wAv", 0) + (lreg, f"PS{pb}"),
                    writes=(f"PS{pb}",))
                vf = BIGB[:, ec * 512:(ec + 1) * 512]
                wl = ("vfm",) if ec == 0 else ("vfm#",)
                if ec == 0:
                    wl = wl + ("qsig0", "qsig1", "krow", "lnm2", "ln_mb",
                               "ln_rs", "ln_tmp", "ln_tmp2")
                P.op("s", (lambda s_, d=vf, pbb=pb, e2=ec: s_.activation(
                    d, PS[pbb][:, 0:T], AF.Sigmoid,
                    bias=ALF[:, 40 + e2:41 + e2], scale=ALF[:, 32 + e2:33 + e2])),
                     reads=(f"PS{pb}", "alf2", "alf2b"), writes=wl)

        def vw_mm(s):
            for t4 in range(4):
                for doh in range(2):
                    pb = 6 + (t4 * 2 + doh) % 2
                    pairs = []
                    for ec in range(NCH):
                        lh = BIGB[:, ec * 512 + t4 * 128: ec * 512 + t4 * 128 + 128]
                        rh = AA[:, 24576 + ec * 1024 + doh * 512:
                                24576 + ec * 1024 + (doh + 1) * 512]
                        pairs.append((lh, rh, ec == 0, ec == 7, PS[pb][:, 0:512]))
                    MMg(None, pairs, reads=("vfm", "vfm#") + wp("wAo", 0) +
                        (f"PS{pb}",), writes=(f"PS{pb}",))
                    tc = s * 4 + t4
                    dstv = AB[:, tc * 1024 + doh * 512: tc * 1024 + (doh + 1) * 512]
                    wl = ("VW",) if (s == 0 and t4 == 0 and doh == 0) else ("VW#",)
                    if (tc + 1) * 1024 > 28672:
                        wl = wl + ("lnbB",)
                    P.op("s", (lambda e, d=dstv, pbb=pb: e.copy(
                        d, PS[pbb][:, 0:512])),
                         reads=(f"PS{pb}",), writes=wl)

        def bload_ln(s, mid=None):
            sl, xb, xregs = bctx(s)
            src = src_dram[:, s * NCH * T:(s + 1) * NCH * T]
            P.dma(f"dx{sl}", (lambda y, d=xb, so=src: y.dma_start(d, so)),
                  reads=(f"{src_pref}:{2*s}", f"{src_pref}:{2*s+1}"),
                  writes=xregs)
            lb, lreg = lnbuf(s)
            ln_pass(T, xb, xregs, lb, lreg, 4, 5, 0,
                    mid=mid, x2_bb=7168, split_apply=True,
                    math_war=(), apply_war=(), rsqrt_dve=True,
                    alt_scratch=True)

        bload_ln(0)
        if wmode == "load":
            P.op("v", (lambda v: v.memset(K8[0:1, 0:1], 0.0)),
                 reads=(), writes=tuple(ARENA_REGS))
            for wi, wreg in enumerate(("wAq", "wAk", "wAv", "wAo")):
                for half in range(2):
                    k = wi * 2 + half
                    regs = tuple(f"{wreg}@{r}:0" for r in
                                 range(half * 4, half * 4 + 4))
                    off = wi * 8192 + half * 4096
                    P.dma(f"dw{k % 2}", (lambda y, o=off: y.dma_start(
                        AA[:, o:o + 4096], wcC[:, o:o + 4096])),
                          reads=(f"wc:{('bq','bk','bv','bo')[wi]}",),
                          writes=regs)
        else:
            load_weights([("bq", 8, 1024, "wAq", AA, 0),
                          ("bk", 8, 1024, "wAk", AA, 8192),
                          ("bv", 8, 1024, "wAv", AA, 16384),
                          ("bo", 8, 1024, "wAo", AA, 24576)])
        for s in range(nsl):
            sl, xb, xregs = bctx(s)
            qk_sums(s)
            v_proj(s)
            if s + 1 < nsl:
                bload_ln(s + 1, mid=lambda s2=s: vw_mm(s2))
            else:
                vw_mm(s)
        pairs = []
        for doc in range(NCH):
            for jc in range(NTC):
                lh = AB[:, jc * 1024 + doc * 128: jc * 1024 + doc * 128 + 128]
                pairs.append((lh, CON[:, 0:1], (doc == 0 and jc == 0),
                              (doc == NCH - 1 and jc == NTC - 1),
                              PS[0][:, doc:doc + 1]))
        MMg(None, pairs, reads=("VW", "VW#", "c_ones", "PS0"), writes=("PS0",))
        P.op("v", (lambda v: v.scalar_tensor_tensor(
            CSB[:, :], PS[0][:, 0:NCH], 0.9, BOC[:, :], ALU.mult, ALU.add)),
             reads=("PS0", "c_boc"), writes=("csb",))
        nib = S // T
        for ib in range(nib):
            sl = ib % 2
            xb = (XB0 if sl == 0 else XB1).ap()
            xregs = (f"xs{2*sl}", f"xs{2*sl+1}")
            src = src_dram[:, ib * NCH * T:(ib + 1) * NCH * T]
            P.dma(f"dx{sl}", (lambda y, d=xb, so=src: y.dma_start(d, so)),
                  reads=(f"{src_pref}:{2*ib}", f"{src_pref}:{2*ib+1}"),
                  writes=xregs)
            for jc in range(NTC):
                pu = jc % 2
                u = BIG[:, 512 + pu * 512: 1024 + pu * 512]
                ureg = f"u{pu}"
                f2 = BIG[:, 1536 + pu * 512: 2048 + pu * 512]
                f2reg = f"f2{pu}"
                mf = BIG[:, 2560:3072]
                at = BIGB[:, pu * 512:(pu + 1) * 512]
                atreg = f"at{pu}"
                sqv = SQ[:, ib * T:(ib + 1) * T]
                skcol = SKC[:, jc:jc + 1]
                du = BIG[:, 3072:3584] if pu == 0 else BIG[:, 3584:4096]
                dreg = f"du{pu}"
                P.op("v", (lambda v, d=du, sv=sqv, sk=skcol: v.tensor_scalar(
                    d, sv, sk, None, ALU.subtract)),
                     reads=("sq", "sq#", "skc", "skc#"), writes=(dreg,))
                P.op("s", (lambda s_, d=u, so=du: s_.activation(
                    d, so, AF.Abs, scale=2.0 / 1024.0)),
                     reads=(dreg,), writes=(ureg,))
                P.op("s", (lambda s_, d=f2, so=u: s_.activation(
                    d, so, AF.Exp, bias=CV[:, 0:1], scale=-1.0)),
                     reads=(ureg, "c_cv"), writes=(f2reg,))
                P.op("v", (lambda v, d=mf, sv=sqv, sk=skcol, f=f2:
                           v.scalar_tensor_tensor(d, sv, sk, f, ALU.is_gt, ALU.mult)),
                     reads=("sq", "sq#", "skc", "skc#", f2reg), writes=("mf",))
                P.op("v", (lambda v, d=at, m=mf, f=f2: v.scalar_tensor_tensor(
                    d, m, 2.0, f, ALU.mult, ALU.subtract)),
                     reads=("mf", f2reg), writes=(atreg,))
                pairs = []
                for doc in range(NCH):
                    lh = AB[:, jc * 1024 + doc * 128: jc * 1024 + doc * 128 + 128]
                    pairs.append((lh, at, jc == 0, jc == NTC - 1,
                                  PS[doc][:, 0:T]))
                MMg(None, pairs,
                    reads=("VW", "VW#", atreg) + tuple(f"PS{d2}" for d2 in range(8)),
                    writes=tuple(f"PS{d2}" for d2 in range(8)))
            for doc in range(NCH):
                xv = xb[:, doc * T:(doc + 1) * T]
                P.op("v", (lambda v, d=xv, dd=doc: v.scalar_tensor_tensor(
                    d, PS[dd][:, 0:T], CSB[:, dd:dd + 1], d, ALU.add, ALU.add)),
                     reads=(f"PS{doc}", "csb", xregs[0], xregs[1]),
                     writes=(xregs[0], xregs[1]))
            dstd = dst_dram[:, ib * NCH * T:(ib + 1) * NCH * T]
            P.dma(f"ds{sl}", (lambda y, d=dstd, so=xb: y.dma_start(d, so)),
                  reads=xregs, writes=(f"{dst_pref}:{2*ib}",
                                       f"{dst_pref}:{2*ib+1}"))

    chain = [p for p in ("attn_a", "ffn1", "attn_b", "ffn2") if p in phases]

    bufs = [xB, xC]
    cur_src, cur_pref = dr["xT"], None
    for i, name in enumerate(chain):
        last = i == len(chain) - 1
        dst = out if last else bufs[i % 2]
        dst_pref = "OUT" if last else ("hxB" if i % 2 == 0 else "hxC")
        has_a = "attn_a" in chain
        if name == "attn_a":
            phase_attn_a(cur_src, cur_pref, dst, dst_pref)
        elif name.startswith("ffn"):
            phase_ffn(cur_src, cur_pref, dst, dst_pref, final=last,
                      wmode="load" if has_a else "plain")
        else:
            phase_attn_b(cur_src, cur_pref, dst, dst_pref,
                         wmode="load" if has_a else "plain")
        cur_src, cur_pref = dst, dst_pref

    if DBG:
        P.dma("dbg", lambda y: y.dma_start(dbg1, SQ[:, 0:S]), reads=("sq", "sq#"))
        P.dma("dbg", lambda y: y.dma_start(dbg2, SKC[:, 0:NTC]), reads=("skc", "skc#"))
        P.dma("dbg", lambda y: y.dma_start(dbg3, AB[:, 0:NTC * 1024]), reads=("VW", "VW#"))
    P.final_waits("y")
    P.finalize()

    with nc.Block() as block:
        @block.tensor
        def _(t):
            P.run_engine("t", t)

        @block.vector
        def _(v):
            P.run_engine("v", v)

        @block.scalar
        def _(s):
            P.run_engine("s", s)

        @block.gpsimd
        def _(p):
            P.run_engine("p", p)

        @block.sync
        def _(y):
            P.run_engine("y", y)
    es.close()
    return nc


_NCACHE = {}


def _get_nc(S, phases):
    key = (S, tuple(phases))
    if key not in _NCACHE:
        _NCACHE[key] = build_nc(S, phases)
    return _NCACHE[key]


def _vec_cols(v, n):
    return np.ascontiguousarray(v.reshape(n, 128).T).astype(np.float32)


PHASES = ("attn_a", "ffn1", "attn_b", "ffn2")
RUNNER = None


def kernel(**inputs):
    import ml_dtypes
    phases = PHASES
    x = np.asarray(inputs["L0_attn_a_hidden_states"], dtype=np.float32)
    B = x.shape[0]
    S = x.shape[1]
    nc = _get_nc(S, phases)

    aq = np.ascontiguousarray(np.asarray(inputs["L0_attn_a_Wq"], np.float32).T)
    ak = np.ascontiguousarray(np.asarray(inputs["L0_attn_a_Wk"], np.float32).T)
    av = np.ascontiguousarray(np.asarray(inputs["L0_attn_a_Wv"], np.float32).T)
    ao = np.ascontiguousarray(np.asarray(inputs["L0_attn_a_Wo"], np.float32).T)
    anw = np.ascontiguousarray(
        np.asarray(inputs["L0_attn_a_norm_w"], np.float32).reshape(128, 1))
    w1 = np.ascontiguousarray(np.asarray(inputs["L0_ffn_w1"], np.float32))
    w2 = np.ascontiguousarray(np.asarray(inputs["L0_ffn_w2"], np.float32))
    b1c = _vec_cols(np.asarray(inputs["L0_ffn_b1"], np.float32), NJ)
    b2c = _vec_cols(np.asarray(inputs["L0_ffn_b2"], np.float32), NCH)
    bq = np.ascontiguousarray(np.asarray(inputs["L0_attn_b_Wq"], np.float32))
    bk = np.ascontiguousarray(np.asarray(inputs["L0_attn_b_Wk"], np.float32))
    bv = np.ascontiguousarray(np.asarray(inputs["L0_attn_b_Wv"], np.float32))
    bo = np.ascontiguousarray(np.asarray(inputs["L0_attn_b_Wo"], np.float32))
    bvecs = np.concatenate(
        [_vec_cols(np.asarray(inputs[f"L0_attn_b_{nm}"], np.float32), 8)
         for nm in ("gq", "bq", "mq", "vq", "gk", "bk", "mk", "vk",
                    "gv", "bv", "mv", "vv")], axis=1)
    boc = _vec_cols(np.asarray(inputs["L0_attn_b_bo"], np.float32), NCH)
    onesall = np.ones((128, 128), dtype=ml_dtypes.bfloat16)
    tk = np.arange(128)[:, None]
    tq = np.arange(128)[None, :]
    m128 = ((tk // 64 < tq // 64) | ((tk // 64 == tq // 64) & (tk <= tq))
            ).astype(np.float32)
    maskt = np.tile(m128, (1, 4)).astype(ml_dtypes.bfloat16)
    cvals = np.tile(np.array([[LN09, LN_EPS, A_EPS, RMS_EPS, 0, 0, 0, 0]],
                             np.float32), (128, 1))
    onef = np.ones((1, 1), np.float32)

    shared = dict(aq=aq, ak=ak, av=av, ao=ao, anw=anw, w1=w1, w2=w2, b1c=b1c,
                  b2c=b2c, bq=bq, bk=bk, bv=bv, bo=bo, bvecs=bvecs, boc=boc,
                  onesall=onesall, maskt=maskt, cvals=cvals, onef=onef,
                  onesf32=np.ones((128, 128), np.float32))
    NB = S // 512
    in_maps = []
    for b in range(B):
        m = dict(shared)
        m["xT"] = np.ascontiguousarray(
            x[b].T.reshape(NCH, 128, NB, 512).transpose(1, 2, 0, 3)
            .reshape(128, NCH * S))
        in_maps.append(m)
    runner = RUNNER if RUNNER is not None else run_bass_kernel_spmd
    res = runner(nc, in_maps, core_ids=list(range(B)))
    outs = [np.asarray(r["outT"]).reshape(128, NB, NCH, 512)
            .transpose(1, 3, 2, 0).reshape(S, D) for r in res.results]
    return np.stack(outs, axis=0).astype(np.float32)
